# version 02
# baseline (speedup 1.0000x reference)
import numpy as np
import ml_dtypes
from contextlib import ExitStack

import concourse.bass as bass
import concourse.mybir as mybir
from concourse.bass_utils import run_bass_kernel_spmd

F32 = mybir.dt.float32
BF16 = mybir.dt.bfloat16
AF = mybir.ActivationFunctionType
ALU = mybir.AluOpType
AX = mybir.AxisListType
RND = 12582912.0

ENGINES = ("tensor", "vector", "scalar", "gpsimd", "sync")

NSEQ = 2
T = 2048
DM = 1024
NA = DM // 128


class Prog:

    NDMASEM = 12

    def __init__(self, name="prog"):
        self.nc = bass.Bass("TRN2", target_bir_lowering=False)
        self.name = name
        self.ops = {e: [] for e in ENGINES}
        self.dmasems = [f"dma{i}" for i in range(self.NDMASEM)]
        self.allsems = list(ENGINES) + self.dmasems
        self.cnt = {s: 0 for s in self.allsems}
        self.watermark = {e: {s: 0 for s in self.allsems} for e in ENGINES}
        self.wr = {}
        self.rd = {}
        self._dma_rr = 0
        self.log = []

    def _edges(self, eng, reads, writes, mysem):
        need = {}
        for k in reads:
            w = self.wr.get(k)
            if w:
                need[w[0]] = max(need.get(w[0], 0), w[1])
        for k in writes:
            w = self.wr.get(k)
            if w and w[0] != mysem:
                need[w[0]] = max(need.get(w[0], 0), w[1])
            for sm, rv in self.rd.get(k, {}).items():
                if sm != mysem:
                    need[sm] = max(need.get(sm, 0), rv)
        waits = []
        for ts_, val in need.items():
            if ts_ == mysem and val >= self.cnt.get(mysem, 0) + 1:
                continue
            if val > self.watermark[eng][ts_]:
                waits.append((ts_, val))
                self.watermark[eng][ts_] = val
        return waits

    def _post(self, sem, reads, writes):
        v = self.cnt[sem]
        for k in reads:
            self.rd.setdefault(k, {})[sem] = v
        for k in writes:
            self.wr[k] = (sem, v)
            self.rd[k] = {}

    def emit(self, eng, fn, reads=(), writes=(), step=1):
        waits = self._edges(eng, reads, writes, eng)
        self.cnt[eng] += step
        self._post(eng, reads, writes)
        self.log.append((eng, self.cnt[eng], tuple(waits), tuple(reads), tuple(writes)))

        def run(e, sems, _w=waits, _f=fn, _s=step, _eng=eng):
            for ts_, val in _w:
                e.wait_ge(sems[ts_], val)
            _f(e).then_inc(sems[_eng], _s)
        self.ops[eng].append(run)

    def group(self, eng, fns, reads=(), writes=()):
        waits = self._edges(eng, reads, writes, eng)
        self.cnt[eng] += 1
        self._post(eng, reads, writes)
        self.log.append((eng, self.cnt[eng], tuple(waits), tuple(reads), tuple(writes), "grp"))

        def run(e, sems, _w=waits, _fns=fns, _eng=eng):
            for ts_, val in _w:
                e.wait_ge(sems[ts_], val)
            inst = None
            for fn in _fns:
                inst = fn(e)
            inst.then_inc(sems[_eng], 1)
        self.ops[eng].append(run)

    def barrier(self):
        snap = dict(self.cnt)
        for e in ENGINES:
            for ts_ in self.allsems:
                if ts_ != e and snap[ts_] > self.watermark[e][ts_]:
                    self.watermark[e][ts_] = snap[ts_]

            def run(eng, sems, _snap=snap, _e=e):
                for ts_ in list(ENGINES) + [s for s in _snap if s.startswith("dma")]:
                    if ts_ != _e and _snap.get(ts_, 0) > 0:
                        eng.wait_ge(sems[ts_], _snap[ts_])
            self.ops[e].append(run)

    def dma(self, queue, out, in_, reads=(), writes=(), **kw):
        sem = self.dmasems[self._dma_rr]
        self._dma_rr = (self._dma_rr + 1) % self.NDMASEM
        waits = self._edges(queue, reads, writes, sem)
        prior = self.cnt[sem]
        if prior > self.watermark[queue][sem]:
            waits.append((sem, prior))
            self.watermark[queue][sem] = prior
        self.cnt[sem] = prior + 16
        self._post(sem, reads, writes)
        self.log.append((queue, sem, self.cnt[sem], tuple(waits), tuple(reads), tuple(writes)))

        def run(e, sems, _w=waits, _sem=sem):
            for ts_, val in _w:
                e.wait_ge(sems[ts_], val)
            e.dma_start(out=out, in_=in_, **kw).then_inc(sems[_sem], 16)
        self.ops[queue].append(run)

    def tv(self, eng, name, *args, reads=(), writes=(), **kw):
        self.emit(eng, lambda e: getattr(e, name)(*args, **kw),
                  reads=reads, writes=writes)

    def finalize(self):
        nc = self.nc
        with ExitStack() as st:
            sems = {s: st.enter_context(nc.semaphore(f"S_{s}")) for s in self.allsems}
            block = st.enter_context(nc.Block())
            final = dict(self.cnt)
            oplists = self.ops

            def make(engname):
                def body(e):
                    for f in oplists[engname]:
                        f(e, sems)
                    if engname in ("sync", "gpsimd"):
                        for ts_ in self.allsems:
                            if final[ts_] > 0:
                                e.wait_ge(sems[ts_], final[ts_])
                return body
            block.tensor(make("tensor"))
            block.vector(make("vector"))
            block.scalar(make("scalar"))
            block.gpsimd(make("gpsimd"))
            block.sync(make("sync"))
        return nc



def stat_sums(p, src_tiles, W, ps_st, sq, sqkeys, onesc):
    nt = len(src_tiles)
    for i, (ap_, key) in enumerate(src_tiles):
        p.tv("scalar", "activation", sq[:, i * W:(i + 1) * W], ap_, AF.Square,
             reads=(key,), writes=(sqkeys[i],))
    fns = [(lambda e, i=i, ap_=ap_: e.matmul(ps_st[0:1, 0:W], onesc[:, 0:1], ap_,
                                             start=(i == 0), stop=(i == nt - 1)))
           for i, (ap_, key) in enumerate(src_tiles)]
    p.group("tensor", fns, reads=tuple(k for _, k in src_tiles) + ("onesc",),
            writes=("ps_st",))
    fns = [(lambda e, i=i: e.matmul(ps_st[32:33, 0:W], onesc[:, 0:1], sq[:, i * W:(i + 1) * W],
                                    start=(i == 0), stop=(i == nt - 1)))
           for i in range(nt)]
    p.group("tensor", fns, reads=tuple(sqkeys) + ("onesc",), writes=("ps_st",))


def ln_rows(p, rA, rB, W, ps_st, d_model, eps):
    mu, ve, rstd, mur = rA(0, W), rA(1, W), rA(2, W), rA(3, W)
    t0 = rB(0, W)
    p.tv("vector", "tensor_scalar", mu, ps_st[0:1, 0:W], 1.0 / d_model, None, ALU.mult,
         reads=("ps_st",), writes=("rA0",))
    p.tv("vector", "tensor_scalar", ve, ps_st[32:33, 0:W], 1.0 / d_model, eps, ALU.mult, ALU.add,
         reads=("ps_st",), writes=("rA1",))
    p.tv("vector", "tensor_mul", t0, mu, mu, reads=("rA0",), writes=("rB0",))
    p.tv("vector", "tensor_sub", ve, ve, t0, reads=("rA1", "rB0"), writes=("rA1",))
    p.tv("vector", "reciprocal", t0, ve, reads=("rA1",), writes=("rB0",))
    p.tv("scalar", "activation", rstd, t0, AF.Sqrt, reads=("rB0",), writes=("rA2",))
    t3 = rB(3, W)
    for _ in range(2):
        p.tv("vector", "tensor_mul", t3, rstd, rstd, reads=("rA2",), writes=("rB3",))
        p.tv("vector", "tensor_mul", t3, ve, t3, reads=("rA1", "rB3"), writes=("rB3",))
        p.tv("vector", "tensor_scalar", t3, t3, -0.5, 1.5, ALU.mult, ALU.add,
             reads=("rB3",), writes=("rB3",))
        p.tv("vector", "tensor_mul", rstd, rstd, t3, reads=("rA2", "rB3"), writes=("rA2",))
    p.tv("vector", "tensor_mul", mur, mu, rstd, reads=("rA0", "rA2"), writes=("rA3",))
    return rstd, mur


def bcast_row(p, row_ap, row_key, W, ps_bc, dst, dst_key, ones1):
    p.tv("tensor", "matmul", ps_bc[:, 0:W], ones1[0:1, :], row_ap, start=True, stop=True,
         reads=(row_key, "ones1"), writes=("ps_bc",))
    p.tv("scalar", "copy", dst[:, 0:W], ps_bc[:, 0:W], reads=("ps_bc",), writes=(dst_key,))


def amax_rows(p, cmb, cmb_key, W, ps_st, amax_tm, idf, scr_d, srow, bk):
    if W == 1:
        p.tv("tensor", "transpose", ps_st[0:1, 0:128], cmb[:, 0:1], idf[:],
             reads=(cmb_key, "idf"), writes=("ps_st",))
        p.tv("vector", "reduce_max", srow[0:1, 0:1], ps_st[0:1, 0:128], AX.X,
             reads=("ps_st",), writes=(bk + "_row",))
        return
    nblk = (W + 127) // 128
    for b in range(nblk):
        bw = min(128, W - b * 128)
        p.tv("tensor", "transpose", ps_st[0:bw, b * 128:b * 128 + 128],
             cmb[:, b * 128:b * 128 + bw], idf[:],
             reads=(cmb_key, "idf"), writes=("ps_st",))
        p.tv("vector", "reduce_max", amax_tm[0:bw, b:b + 1], ps_st[0:bw, b * 128:b * 128 + 128],
             AX.X, reads=("ps_st",), writes=(bk + f"_tm{b}",))
        p.dma("sync", scr_d.rearrange("a b -> b a")[b * 128:b * 128 + bw, :],
              amax_tm[0:bw, b:b + 1], reads=(bk + f"_tm{b}",), writes=(bk + f"_scr{b}",))
    p.dma("sync", srow[0:1, 0:W], scr_d[0:1, 0:W],
          reads=tuple(bk + f"_scr{b}" for b in range(nblk)), writes=(bk + "_row",))


def quant_rows(p, srow, srow_key, rA, rB, m_ap, m_key, W):
    p.tv("vector", "tensor_scalar", rB(1, W), srow[0:1, 0:W], 1e-5, None,
         ALU.max, reads=(srow_key,), writes=("rB1",))
    p.tv("vector", "reciprocal", rB(2, W), rB(1, W), reads=("rB1",), writes=("rB2",))
    p.tv("vector", "tensor_scalar", rA(0, W), rB(2, W), 127.0, None,
         ALU.mult, reads=("rB2",), writes=("rA0",))
    p.tv("vector", "tensor_scalar", rA(1, W), rB(1, W), 1.0 / 127.0, None,
         ALU.mult, reads=("rB1",), writes=("rA1",))
    p.tv("vector", "tensor_scalar", rA(1, W), rA(1, W), m_ap, None,
         ALU.mult, reads=("rA1", m_key), writes=("rA1",))



DEBUG_L0FFN = False


def build_l0ffn():
    p = Prog("l0ffn")
    build_l0ffn.last_prog = p
    nc = p.nc
    DH = 4096
    CH = 512
    NH = DH // 128
    AR_W = 516

    x_d = nc.dram_tensor("x_res", [NSEQ * DM, T], F32, kind="ExternalInput").ap()
    xo_d = nc.dram_tensor("x_out", [NSEQ * DM, T], F32, kind="ExternalOutput").ap()
    w1T_d = nc.dram_tensor("w1T", [DM, DH], F32, kind="ExternalInput").ap()
    w2T_d = nc.dram_tensor("w2T", [DH, DM], F32, kind="ExternalInput").ap()
    convw_d = nc.dram_tensor("convw", [128, 3 * DH // 128], F32, kind="ExternalInput").ap()
    convb_d = nc.dram_tensor("convb", [128, DH // 128], F32, kind="ExternalInput").ap()
    idf_d = nc.dram_tensor("identf32", [128, 128], F32, kind="ExternalInput").ap()
    if DEBUG_L0FFN:
        dbg_xn = nc.dram_tensor("dbg_xn", [128, 8 * 512], F32, kind="ExternalOutput").ap()
        dbg_xq = nc.dram_tensor("dbg_xq", [128, 8 * 512], F32, kind="ExternalOutput").ap()
        dbg_h = nc.dram_tensor("dbg_h", [128, 4 * 516], F32, kind="ExternalOutput").ap()
        dbg_hs = nc.dram_tensor("dbg_hs", [128, 4 * 516], F32, kind="ExternalOutput").ap()
        dbg_ra = nc.dram_tensor("dbg_ra", [1, 2048], F32, kind="ExternalOutput").ap()
        dbg_rb = nc.dram_tensor("dbg_rb", [1, 2048], F32, kind="ExternalOutput").ap()
        dbg_rows = nc.dram_tensor("dbg_rows", [1, 4 * 640], F32, kind="ExternalOutput").ap()
        dbg_sq = nc.dram_tensor("dbg_sq", [128, 4096], F32, kind="ExternalOutput").ap()
        dbg_x0 = nc.dram_tensor("dbg_x0", [128, 512], F32, kind="ExternalOutput").ap()
        dbg_sl1 = nc.dram_tensor("dbg_sl1", [128, 4 * 516], F32, kind="ExternalOutput").ap()
        dbg_wq = nc.dram_tensor("dbg_wq", [128, DH], BF16, kind="ExternalOutput").ap()
        dbg_m = nc.dram_tensor("dbg_m", [1, 4], F32, kind="ExternalOutput").ap()
        dbg_wraw = nc.dram_tensor("dbg_wraw", [128, DH], F32, kind="ExternalOutput").ap()
        dbg_wrnd = nc.dram_tensor("dbg_wrnd", [128, DH], F32, kind="ExternalOutput").ap()
        dbg_swc = nc.dram_tensor("dbg_swc", [128, 1], F32, kind="ExternalOutput").ap()
        dbg_m2 = nc.dram_tensor("dbg_m2", [1, 4], F32, kind="ExternalOutput").ap()
        dbg_pbc = nc.dram_tensor("dbg_pbc", [128, 1], F32, kind="ExternalOutput").ap()
        dbg_mini = nc.dram_tensor("dbg_mini", [1, 16], F32, kind="ExternalOutput").ap()
        dbg_mhs = nc.dram_tensor("dbg_mhs", [128, 32], F32, kind="ExternalOutput").ap()
        dbg_wq2 = nc.dram_tensor("dbg_wq2", [128, DM], BF16, kind="ExternalOutput").ap()
        dbg_q2 = nc.dram_tensor("dbg_q2", [128, 640], BF16, kind="ExternalOutput").ap()
        dbg_q5 = nc.dram_tensor("dbg_q5", [128, 640], BF16, kind="ExternalOutput").ap()
        dbg_ob = nc.dram_tensor("dbg_ob", [128, 640], F32, kind="ExternalOutput").ap()
        dbg_ps = nc.dram_tensor("dbg_ps", [128, 640], F32, kind="ExternalOutput").ap()
        dbg_ivb = nc.dram_tensor("dbg_ivb", [128, 640], F32, kind="ExternalOutput").ap()
    wq1_d = nc.dram_tensor("wq1", [DM, DH], BF16, kind="Internal").ap()
    wq2_d = nc.dram_tensor("wq2", [DH, DM], BF16, kind="Internal").ap()
    scr_d = nc.dram_tensor("scr", [1, 2048], F32, kind="Internal").ap()

    with ExitStack() as st:
        sb = lambda n_, s, d: st.enter_context(nc.sbuf_tensor(n_, s, d))
        psb = lambda n_, s: st.enter_context(nc.psum_tensor(n_, s, F32))

        xcur = sb("xcur", [128, NA * CH], F32)
        S1 = sb("S1", [128, NA * CH], F32)
        xq = sb("xq", [128, NA * CH], BF16)
        wbuf = sb("wbuf", [128, 16384], BF16)
        aren = sb("aren", [128, 33 * AR_W], F32)
        cmbA = sb("cmbA", [128, 640], F32)
        ctmpA = sb("ctmpA", [128, 640], F32)
        ctmpB = sb("ctmpB", [128, 640], F32)
        rows = sb("rows", [1, 4 * 640], F32)
        rows2 = sb("rows2", [1, 4 * 640], F32)
        rA = lambda i, W: rows[0:1, i * 640:i * 640 + W]
        rB = lambda i, W: rows2[0:1, i * 640:i * 640 + W]
        amax_tm = sb("amax_tm", [128, 8], F32)
        sc_b = sb("sc_b", [128, 640], F32)
        iv_b = sb("iv_b", [128, 640], F32)
        qtmp = [sb(f"qtmp{i}", [128, 640], BF16) for i in range(2)]
        qf32 = [sb("qf32_0", [128, 640], F32)] * 2
        obuf = [sb(f"obuf{i}", [128, 640], F32) for i in range(2)]
        ones1 = sb("ones1", [1, 128], F32)
        onesc = sb("onesc", [128, 1], F32)
        idf = sb("idf", [128, 128], F32)
        convw = sb("convw_sb", [128, 3 * NH], F32)
        convb = sb("convb_sb", [128, NH], F32)
        srow = sb("srow", [1, 640], F32)
        asum = sb("asum", [128, 32], F32)
        mrow = sb("mrow", [1, 4], F32)
        swcol = sb("swcol", [128, 1], F32)

        ps_mm = [psb(f"psA{i}", [128, 512]) for i in range(2)]
        ps_w2 = [psb(f"psW{i}", [128, 512]) for i in range(4)]
        ps_st = psb("ps_st", [128, 512])
        ps_bc = psb("ps_bc", [128, 512])
        ob_ps = ps_w2 + ps_mm + [ps_st, ps_bc]
        ob_keys = ["psW0", "psW1", "psW2", "psW3", "psA0", "psA1", "ps_st", "ps_bc"]

        p.tv("vector", "memset", ones1[:], 1.0, writes=("ones1",))
        p.tv("vector", "memset", onesc[:], 1.0, writes=("onesc",))
        p.dma("sync", idf[:], idf_d, writes=("idf",))
        p.dma("sync", convw[:], convw_d, writes=("convw",))
        p.dma("sync", convb[:], convb_d, writes=("convb",))
        p.tv("vector", "memset", cmbA[:], 0.0, writes=("cmbA",))
        p.dma("sync", xo_d, x_d, writes=("xo_init",))

        def wprep(wT_d, wq_dram, KD, OD, mcol_slice, mkey):
            nkt = KD // 128
            stag = [(S1, "wstA"), (xcur, "wstB")]
            for k in range(nkt):
                buf, bk = stag[k % 2]
                oth, ok_ = stag[(k + 1) % 2]
                p.dma("sync", buf[:, 0:OD], wT_d[k * 128:(k + 1) * 128, :], writes=(bk,))
                p.emit("scalar", lambda e, buf=buf, oth=oth, k=k, OD=OD: e.activation(
                    oth[:, 0:OD], buf[:, 0:OD], AF.Abs, accum_out=asum[:, k:k + 1]),
                    reads=(bk,), writes=(ok_, f"as{k}"))
            p.tv("vector", "reduce_sum", asum[:, 0:1], asum[:, 0:nkt], AX.X,
                 reads=tuple(f"as{k}" for k in range(nkt)), writes=("asum0",))
            p.tv("tensor", "matmul", ps_st[0:1, 0:1], onesc[:, 0:1], asum[:, 0:1],
                 start=True, stop=True, reads=("asum0", "onesc"), writes=("ps_st",))
            p.tv("vector", "tensor_scalar", mcol_slice, ps_st[0:1, 0:1],
                 1.0 / (KD * OD), 1e-5, ALU.mult, ALU.max, reads=("ps_st",), writes=(mkey,))
            p.tv("vector", "reciprocal", mrow[0:1, 2:3], mcol_slice, reads=(mkey,), writes=("minv",))
            if DEBUG_L0FFN and mkey == "m1":
                p.dma("sync", dbg_m2, mrow[0:1, :], reads=("minv", mkey), writes=("dbgm2",))
            p.tv("tensor", "matmul", ps_bc[:, 0:1], ones1[0:1, :], mrow[0:1, 2:3],
                 start=True, stop=True, reads=("minv", "ones1"), writes=("ps_bc",))
            p.tv("scalar", "copy", swcol[:], ps_bc[:, 0:1], reads=("ps_bc",), writes=("swcol",))
            if DEBUG_L0FFN and mkey == "m1":
                p.dma("sync", dbg_pbc, swcol[:], reads=("swcol",), writes=("dbgpbc",))
            for k in range(nkt):
                buf, bk = stag[k % 2]
                p.dma("sync", buf[:, 0:OD], wT_d[k * 128:(k + 1) * 128, :], writes=(bk,))
                if DEBUG_L0FFN and mkey == "m1" and k == 0:
                    p.dma("sync", dbg_wraw, buf[:, 0:OD], reads=(bk,), writes=("dbgwraw",))
                    p.dma("sync", dbg_swc, swcol[:], reads=("swcol",), writes=("dbgswc",))
                p.tv("vector", "tensor_scalar", buf[:, 0:OD], buf[:, 0:OD], swcol[:, 0:1], RND,
                     ALU.mult, ALU.add, reads=(bk, "swcol"), writes=(bk,))
                p.tv("vector", "tensor_scalar", buf[:, 0:OD], buf[:, 0:OD], RND, None,
                     ALU.subtract, reads=(bk,), writes=(bk,))
                if DEBUG_L0FFN and mkey == "m1" and k == 0:
                    p.dma("sync", dbg_wrnd, buf[:, 0:OD], reads=(bk,), writes=("dbgwrnd",))
                p.tv("vector", "tensor_scalar", wbuf[:, 0:OD], buf[:, 0:OD], 1.0, -1.0,
                     ALU.min, ALU.max, reads=(bk,), writes=("wbufM",))
                p.dma("sync", wq_dram[k * 128:(k + 1) * 128, :], wbuf[:, 0:OD],
                      reads=("wbufM",), writes=(mkey + f"_q{k}",))

        wprep(w1T_d, wq1_d, DM, DH, mrow[0:1, 0:1], "m1")
        wprep(w2T_d, wq2_d, DH, DM, mrow[0:1, 1:2], "m2")
        m1_ap, m2_ap = mrow[0:1, 0:1], mrow[0:1, 1:2]
        if DEBUG_L0FFN:
            p.dma("sync", dbg_wq, wq1_d[0:128, :], reads=("m1_q0",), writes=("dbgwq",))
            p.dma("sync", dbg_wq2, wq2_d[0:128, :], reads=("m2_q0",), writes=("dbgwq2",))
            p.dma("sync", dbg_m, mrow[0:1, :], reads=("m1", "m2", "m1_m", "m2_m"), writes=("dbgm",))
        p.barrier()

        windows = [(0, 511, 0), (511, 512, 1), (1023, 512, 2), (1535, 512, 3), (2047, 1, 3)]

        for s in range(NSEQ):
            for j in range(1, 33):
                p.tv("vector", "memset", aren[:, j * AR_W:j * AR_W + 2], 0.0,
                     writes=(f"slot{j}_lead",))
            for c in range(4):
                base = c * CH
                for a in range(NA):
                    p.dma("sync", xcur[:, a * CH:(a + 1) * CH],
                          x_d[s * DM + a * 128: s * DM + (a + 1) * 128, base:base + CH],
                          writes=(f"xcur{a}",))
                stat_sums(p, [(xcur[:, a * CH:(a + 1) * CH], f"xcur{a}") for a in range(NA)],
                          CH, ps_st, S1, [f"sq{a}" for a in range(NA)], onesc)
                if DEBUG_L0FFN and s == 0 and c == 0:
                    p.dma("sync", dbg_sq, S1[:],
                          reads=tuple(f"sq{a}" for a in range(NA)), writes=("dbgsq",))
                rstd, mur = ln_rows(p, rA, rB, CH, ps_st, DM, 1e-5)
                if DEBUG_L0FFN and s == 0 and c == 0:
                    p.dma("sync", dbg_rows, rows[0:1, :],
                          reads=("rA0", "rA1", "rA2", "rA3"), writes=("dbgrows",))
                    p.dma("sync", dbg_x0, xcur[:, 0:CH], reads=("xcur0",), writes=("dbgx0",))
                bcast_row(p, rstd, "rA2", CH, ps_bc, sc_b, "sc_b", ones1)
                bcast_row(p, mur, "rA3", CH, ps_bc, iv_b, "iv_b", ones1)
                for a in range(NA):
                    p.tv("vector", "tensor_mul", S1[:, a * CH:(a + 1) * CH],
                         xcur[:, a * CH:(a + 1) * CH], sc_b[:, 0:CH],
                         reads=(f"xcur{a}", "sc_b"), writes=(f"sq{a}",))
                    p.tv("vector", "tensor_sub", xcur[:, a * CH:(a + 1) * CH],
                         S1[:, a * CH:(a + 1) * CH], iv_b[:, 0:CH],
                         reads=(f"sq{a}", "iv_b"), writes=(f"xcur{a}",))
                for a in range(NA):
                    ct, ck = (ctmpA, "ctmpA") if a % 2 == 0 else (ctmpB, "ctmpB")
                    p.tv("scalar", "activation", ct[:, 0:CH], xcur[:, a * CH:(a + 1) * CH],
                         AF.Abs, reads=(f"xcur{a}",), writes=(ck,))
                    if a == 0:
                        p.tv("vector", "tensor_copy", cmbA[:, 0:CH], ct[:, 0:CH],
                             reads=(ck,), writes=("cmbA",))
                    else:
                        p.tv("vector", "tensor_max", cmbA[:, 0:CH], cmbA[:, 0:CH], ct[:, 0:CH],
                             reads=("cmbA", ck), writes=("cmbA",))
                amax_rows(p, cmbA, "cmbA", CH, ps_st, amax_tm, idf, scr_d, srow, "qa")
                quant_rows(p, srow, "qa_row", rA, rB, m1_ap, "m1", CH)
                bcast_row(p, rA(0, CH), "rA0", CH, ps_bc, sc_b, "sc_b", ones1)
                bcast_row(p, rA(1, CH), "rA1", CH, ps_bc, iv_b, "iv_b", ones1)
                for a in range(NA):
                    q = qf32[a % 2]
                    qk = f"qf{a % 2}"
                    p.tv("vector", "tensor_mul", q[:, 0:CH], xcur[:, a * CH:(a + 1) * CH],
                         sc_b[:, 0:CH], reads=(f"xcur{a}", "sc_b"), writes=(qk,))
                    p.tv("vector", "tensor_scalar", q[:, 0:CH], q[:, 0:CH], RND, None,
                         ALU.add, reads=(qk,), writes=(qk,))
                    p.tv("vector", "tensor_scalar", q[:, 0:CH], q[:, 0:CH], RND, None,
                         ALU.subtract, reads=(qk,), writes=(qk,))
                    p.tv("vector", "tensor_scalar", xq[:, a * CH:(a + 1) * CH], q[:, 0:CH],
                         127.0, -128.0, ALU.min, ALU.max, reads=(qk,), writes=(f"xq{a}",))
                for half in range(2):
                    for a in range(NA):
                        p.dma("sync", wbuf[:, a * 2048:(a + 1) * 2048],
                              wq1_d[a * 128:(a + 1) * 128, half * 2048:(half + 1) * 2048],
                              writes=("wbufM",))
                    for i in range(16 * half, 16 * half + 16):
                        ii = i - 16 * half
                        ps, pkey = ps_mm[i % 2], f"psA{i % 2}"
                        fns = [(lambda e, a=a, ii=ii, ps=ps: e.matmul(
                            ps[:, 0:CH], wbuf[:, a * 2048 + ii * 128:a * 2048 + (ii + 1) * 128],
                            xq[:, a * CH:(a + 1) * CH], start=(a == 0), stop=(a == NA - 1)))
                            for a in range(NA)]
                        p.group("tensor", fns,
                                reads=("wbufM",) + tuple(f"xq{a}" for a in range(NA)),
                                writes=(pkey,))
                        j = i + 1
                        p.tv("vector", "tensor_mul", aren[:, j * AR_W + 2:j * AR_W + 514],
                             ps[:, 0:CH], iv_b[:, 0:CH],
                             reads=(pkey, "iv_b"), writes=(f"slot{j}_body",))
                if c == 3:
                    for i in range(NH):
                        j = i + 1
                        p.tv("vector", "memset", aren[:, j * AR_W + 514:j * AR_W + 515], 0.0,
                             writes=(f"slot{j}_pad",))
                if DEBUG_L0FFN and s == 0 and c == 0:
                    p.dma("sync", dbg_xn, xcur[:],
                          reads=tuple(f"xcur{a}" for a in range(NA)), writes=("dbgxn",))
                    p.tv("vector", "tensor_copy", S1[:, 0:NA * CH], xq[:, 0:NA * CH],
                         reads=tuple(f"xq{a}" for a in range(NA)), writes=("dbgxq_s",))
                    p.dma("sync", dbg_xq, S1[:], reads=("dbgxq_s",), writes=("dbgxq",))
                    p.dma("sync", dbg_h, aren[:, 1 * AR_W:5 * AR_W],
                          reads=tuple(f"slot{j}_body" for j in range(1, 6)), writes=("dbgh",))
                    p.dma("sync", dbg_ra[0:1, 0:640], srow[0:1, 0:640],
                          reads=("qa_row",), writes=("dbgra",))
                for (g0, W, wc) in windows:
                    if wc != c:
                        continue
                    if DEBUG_L0FFN and s == 0 and c == 1:
                        p.dma("sync", dbg_sl1, aren[:, 1 * AR_W:5 * AR_W],
                              reads=tuple(f"slot{j}_body" for j in range(1, 6)) +
                                    tuple(f"slot{j}_lead" for j in range(1, 6)),
                              writes=("dbgsl1",))
                    off = g0 - base + 2
                    for i in range(NH):
                        j = i + 1
                        sl = aren[:, j * AR_W:(j + 1) * AR_W]
                        rd = (f"slot{j}_body", f"slot{j}_lead") + \
                             ((f"slot{j}_pad",) if W == 1 else ())
                        if W == 1:
                            vl, vc, vr = sl[:, 0:1], sl[:, 1:2], sl[:, 514:515]
                        else:
                            vl = sl[:, off - 1:off - 1 + W]
                            vc = sl[:, off:off + W]
                            vr = sl[:, off + 1:off + 1 + W]
                        p.tv("vector", "tensor_scalar", ctmpA[:, 0:W], vl,
                             convw[:, 3 * i:3 * i + 1], None, ALU.mult,
                             reads=rd + ("convw",), writes=("ctmpA",))
                        p.tv("vector", "scalar_tensor_tensor", ctmpB[:, 0:W], vc,
                             convw[:, 3 * i + 1:3 * i + 2], ctmpA[:, 0:W], ALU.mult, ALU.add,
                             reads=rd + ("convw", "ctmpA"), writes=("ctmpB",))
                        p.tv("vector", "scalar_tensor_tensor", ctmpA[:, 0:W], vr,
                             convw[:, 3 * i + 2:3 * i + 3], ctmpB[:, 0:W], ALU.mult, ALU.add,
                             reads=rd + ("convw", "ctmpB"), writes=("ctmpA",))
                        if W > 1:
                            p.tv("gpsimd", "tensor_copy", sl[:, 0:2], sl[:, 512:514],
                                 reads=(f"slot{j}_body",), writes=(f"slot{j}_lead",))
                            dst = aren[:, i * AR_W + 2:i * AR_W + 2 + W]
                            dk = f"slot{i}_body"
                        else:
                            dst = aren[:, j * AR_W + 515:j * AR_W + 516]
                            dk = f"slot{j}_mini"
                        p.emit("scalar", lambda e, dst=dst, W=W, i=i: e.activation(
                            dst, ctmpA[:, 0:W], AF.Silu, bias=convb[:, i:i + 1]),
                            reads=("ctmpA", "convb"), writes=(dk,))

                    def hs_ap(i):
                        if W > 1:
                            return aren[:, i * AR_W + 2:i * AR_W + 2 + W], f"slot{i}_body"
                        return (aren[:, (i + 1) * AR_W + 515:(i + 1) * AR_W + 516],
                                f"slot{i + 1}_mini")
                    for i in range(NH):
                        ai, ki = hs_ap(i)
                        ct, ck = (ctmpA, "ctmpA") if i % 2 == 0 else (ctmpB, "ctmpB")
                        p.tv("scalar", "activation", ct[:, 0:W], ai, AF.Abs,
                             reads=(ki,), writes=(ck,))
                        if i == 0:
                            p.tv("vector", "tensor_copy", cmbA[:, 0:W], ct[:, 0:W],
                                 reads=(ck,), writes=("cmbA",))
                        else:
                            p.tv("vector", "tensor_max", cmbA[:, 0:W], cmbA[:, 0:W], ct[:, 0:W],
                                 reads=("cmbA", ck), writes=("cmbA",))
                    amax_rows(p, cmbA, "cmbA", W, ps_st, amax_tm, idf, scr_d, srow, "qb")
                    if DEBUG_L0FFN and s == 0 and W == 1:
                        p.dma("sync", dbg_mini[0:1, 0:1], srow[0:1, 0:1],
                              reads=("qb_row",), writes=("dbgmini0",))
                        for _i in range(32):
                            _a, _k = hs_ap(_i)
                            p.dma("sync", dbg_mhs[:, _i:_i + 1], _a,
                                  reads=(_k,), writes=(f"dbgmhs{_i}",),
                                  allow_slow_non_contiguous=True)
                    if DEBUG_L0FFN and s == 0 and c == 0 and g0 == 0:
                        p.dma("sync", dbg_hs, aren[:, 0 * AR_W:4 * AR_W],
                              reads=tuple(f"slot{i}_body" for i in range(4)), writes=("dbghs",))
                        p.dma("sync", dbg_rb[0:1, 0:640], srow[0:1, 0:640],
                              reads=("qb_row",), writes=("dbgrb",))
                    quant_rows(p, srow, "qb_row", rA, rB, m2_ap, "m2", W)
                    bcast_row(p, rA(0, W), "rA0", W, ps_bc, sc_b, "sc_b", ones1)
                    bcast_row(p, rA(1, W), "rA1", W, ps_bc, iv_b, "iv_b", ones1)
                    for half in range(2):
                        for k in range(16 * half, 16 * half + 16):
                            p.dma("sync", wbuf[:, (k - 16 * half) * DM:(k - 16 * half + 1) * DM],
                                  wq2_d[k * 128:(k + 1) * 128, :], writes=("wbufM",))
                        for k in range(16 * half, 16 * half + 16):
                            kk = k - 16 * half
                            ai, ki = hs_ap(k)
                            q, qk = qf32[k % 2], f"qf{k % 2}"
                            qt, qtk = qtmp[k % 2], f"qt{k % 2}"
                            p.tv("vector", "tensor_mul", q[:, 0:W], ai, sc_b[:, 0:W],
                                 reads=(ki, "sc_b"), writes=(qk,))
                            p.tv("vector", "tensor_scalar", q[:, 0:W], q[:, 0:W], RND, None,
                                 ALU.add, reads=(qk,), writes=(qk,))
                            p.tv("vector", "tensor_scalar", q[:, 0:W], q[:, 0:W], RND, None,
                                 ALU.subtract, reads=(qk,), writes=(qk,))
                            p.tv("vector", "tensor_scalar", qt[:, 0:W], q[:, 0:W],
                                 127.0, -128.0, ALU.min, ALU.max, reads=(qk,), writes=(qtk,))
                            if DEBUG_L0FFN and s == 0 and c == 0 and g0 == 0 and k == 0:
                                p.dma("sync", dbg_q2[:, 0:W], qt[:, 0:W],
                                      reads=(qtk,), writes=("dbgq2",))
                            if DEBUG_L0FFN and s == 0 and c == 0 and g0 == 0 and k == 5:
                                p.dma("sync", dbg_q5[:, 0:W], qt[:, 0:W],
                                      reads=(qtk,), writes=("dbgq5",))
                            fns = [(lambda e, ob=ob, kk=kk, qt=qt, W=W,
                                    st_=(k == 0), sp_=(k == NH - 1): e.matmul(
                                ob_ps[ob][:, 0:W], wbuf[:, kk * DM + ob * 128:kk * DM + (ob + 1) * 128],
                                qt[:, 0:W], start=st_, stop=sp_))
                                for ob in range(8)]
                            p.group("tensor", fns, reads=("wbufM", qtk), writes=tuple(ob_keys))
                    for ob in range(8):
                        o, ok = obuf[ob % 2], f"ob{ob % 2}"
                        p.tv("vector", "tensor_mul", o[:, 0:W], ob_ps[ob][:, 0:W],
                             iv_b[:, 0:W], reads=(ob_keys[ob], "iv_b"), writes=(ok,))
                        if DEBUG_L0FFN and s == 0 and c == 0 and g0 == 0 and ob == 0:
                            p.dma("sync", dbg_ob[:, 0:W], o[:, 0:W],
                                  reads=(ok,), writes=("dbgob",))
                            p.tv("scalar", "copy", S1[:, 0:W], ob_ps[0][:, 0:W],
                                 reads=(ob_keys[0],), writes=("sq0", "dbgps_s"))
                            p.dma("sync", dbg_ps[:, 0:W], S1[:, 0:W],
                                  reads=("dbgps_s",), writes=("dbgps",))
                            p.dma("sync", dbg_ivb[:, 0:W], iv_b[:, 0:W],
                                  reads=("iv_b",), writes=("dbgivb",))

                        p.dma("gpsimd", xo_d[s * DM + ob * 128:s * DM + (ob + 1) * 128, g0:g0 + W],
                              o[:, 0:W], reads=(ok, "xo_init"),
                              writes=(f"xo_s{s}c{c}w{g0}o{ob}",), accum_op=ALU.add,
                              allow_slow_non_contiguous=(W == 1))
    return p.finalize()



_cache = {}


def _shard_x(x, core):
    xs = x[2 * core:2 * core + 2]
    return np.ascontiguousarray(xs.transpose(0, 2, 1)).reshape(NSEQ * DM, T)


def _unshard(res_list, key="x_out"):
    outs = []
    for r in res_list:
        xo = r[key].reshape(NSEQ, DM, T)
        outs.append(xo.transpose(0, 2, 1))
    return np.ascontiguousarray(np.concatenate(outs, axis=0))


def run_l0ffn(inputs, x_current, cores=8):
    if "l0ffn" not in _cache:
        _cache["l0ffn"] = build_l0ffn()
    nc = _cache["l0ffn"]
    w = {
        "w1T": np.ascontiguousarray(np.asarray(inputs["L0_ffn_w1"], dtype=np.float32).T),
        "w2T": np.ascontiguousarray(np.asarray(inputs["L0_ffn_w2"], dtype=np.float32).T),
        "convw": np.ascontiguousarray(np.asarray(inputs["L0_ffn_conv_w"], dtype=np.float32).reshape(32, 128, 3).transpose(1, 0, 2).reshape(128, 96)),
        "convb": np.ascontiguousarray(np.asarray(inputs["L0_ffn_conv_b"], dtype=np.float32).reshape(32, 128).T),
        "identf32": np.eye(128, dtype=np.float32),
    }
    in_maps = [dict(w, x_res=_shard_x(x_current, c)) for c in range(cores)]
    res = run_bass_kernel_spmd(nc, in_maps, core_ids=list(range(cores)))
    return _unshard([r for r in res.results])


def kernel(**inputs):
    x = np.ascontiguousarray(np.asarray(inputs["L0_attn_x"], dtype=np.float32))
    x = run_l0attn(inputs, x)
    x = run_l0ffn(inputs, x)
    x = run_l1attn(inputs, x)
    x = run_l1ffn(inputs, x)
    return x



DEBUG_L1FFN = False


def build_l1ffn():
    p = Prog("l1ffn")
    build_l1ffn.last_prog = p
    nc = p.nc
    CH = 512
    NE = 8
    DI = 4096
    NI = DI // 128

    x_d = nc.dram_tensor("x_res", [NSEQ * DM, T], F32, kind="ExternalInput").ap()
    xo_d = nc.dram_tensor("x_out", [NSEQ * DM, T], F32, kind="ExternalOutput").ap()
    attnT_d = nc.dram_tensor("attnT", [DM, DM], F32, kind="ExternalInput").ap()
    gateT_d = nc.dram_tensor("gateT", [128, NA * NE], F32, kind="ExternalInput").ap()
    gpT_d = nc.dram_tensor("gpT", [NE * DM, DI], F32, kind="ExternalInput").ap()
    upT_d = nc.dram_tensor("upT", [NE * DM, DI], F32, kind="ExternalInput").ap()
    dnT_d = nc.dram_tensor("dnT", [NE * DI, DM], F32, kind="ExternalInput").ap()
    lnp_d = nc.dram_tensor("lnp", [128, 4 * NA], F32, kind="ExternalInput").ap()
    idf_d = nc.dram_tensor("identf32", [128, 128], F32, kind="ExternalInput").ap()
    scr2_d = nc.dram_tensor("scr2", [NE, CH], F32, kind="Internal").ap()
    if DEBUG_L1FFN:
        dbg_xin = nc.dram_tensor("dbg_xin", [128, 8 * 512], F32, kind="ExternalOutput").ap()
        dbg_h = nc.dram_tensor("dbg_h", [128, 8 * 512], F32, kind="ExternalOutput").ap()
        dbg_t = nc.dram_tensor("dbg_t", [128, 8 * 512], F32, kind="ExternalOutput").ap()
        dbg_lgt = nc.dram_tensor("dbg_lgt", [8, 512], F32, kind="ExternalOutput").ap()
        dbg_cmb = nc.dram_tensor("dbg_cmb", [8, 512], F32, kind="ExternalOutput").ap()
        dbg_gu = nc.dram_tensor("dbg_gu", [128, 512], F32, kind="ExternalOutput").ap()
        dbg_hf = nc.dram_tensor("dbg_hf", [128, 8 * 512], F32, kind="ExternalOutput").ap()

    with ExitStack() as st:
        sb = lambda n_, s, d: st.enter_context(nc.sbuf_tensor(n_, s, d))
        psb = lambda n_, s: st.enter_context(nc.psum_tensor(n_, s, F32))

        xcur = sb("xcur", [128, NA * CH], F32)
        xin = sb("xin", [128, NA * CH], F32)
        hbuf = sb("hbuf", [128, NA * CH], F32)
        S1 = sb("S1", [128, NA * CH], F32)
        tq = sb("tq", [128, NA * CH], BF16)
        gu = sb("gu", [128, NI * CH], BF16)
        wbufW = sb("wbufW", [128, 32768], BF16)
        gateT = sb("gateT_sb", [128, NA * NE], F32)
        gwg = sb("gwg_sb", [128, NA * NE], F32)
        b2gw = sb("b2gw", [1, NE], F32)
        lnp = sb("lnp_sb", [128, 4 * NA], F32)
        gs = [sb(f"gs{i}", [128, CH], BF16) for i in range(2)]
        otmp = sb("otmp", [128, CH], F32)
        obuf = [sb("obuf0", [128, CH], F32)] * 2
        lgt = sb("lgt", [8, CH], F32)
        lgt_tm = sb("lgt_tm", [128, 4 * 8], F32)
        comb_tm = sb("comb_tm", [128, 4 * 8], F32)
        comb_fm = sb("comb_fm", [8, CH], F32)
        combB = sb("combB", [128, CH], F32)
        crow = sb("crow", [1, CH], F32)
        rows = sb("rows", [1, 4 * 512], F32)
        rows2 = sb("rows2", [1, 4 * 512], F32)
        rA = lambda i, W: rows[0:1, i * 512:i * 512 + W]
        rB = lambda i, W: rows2[0:1, i * 512:i * 512 + W]
        sc_b = sb("sc_b", [128, 512], F32)
        iv_b = sb("iv_b", [128, 512], F32)
        ones1 = sb("ones1", [1, 128], F32)
        onesc = sb("onesc", [128, 1], F32)
        onesrow = sb("onesrow", [1, CH], F32)
        idf = sb("idf", [128, 128], F32)
        e1c = sb("e1c", [128, 1], F32)
        m1c = sb("m1c", [128, 1], F32)
        m2c = sb("m2c", [128, 1], F32)
        w1c = sb("w1c", [128, 1], F32)
        w2c = sb("w2c", [128, 1], F32)
        eqA = sb("eqA", [128, 8], F32)
        eqB = sb("eqB", [128, 8], F32)
        mskd = sb("mskd", [128, 8], F32)

        ps_g = psb("ps_g", [128, 512])
        ps_u = psb("ps_u", [128, 512])
        ps_d = [psb(f"ps_d{i}", [128, 512]) for i in range(4)]
        ps_st = psb("ps_st", [128, 512])
        ps_bc = psb("ps_bc", [128, 512])

        p.tv("vector", "memset", ones1[:], 1.0, writes=("ones1",))
        p.tv("vector", "memset", onesc[:], 1.0, writes=("onesc",))
        p.tv("vector", "memset", onesrow[:], 1.0, writes=("onesrow",))
        p.dma("sync", idf[:], idf_d, writes=("idf",))
        p.dma("sync", lnp[:], lnp_d, writes=("lnp",))
        p.dma("sync", gateT[:], gateT_d, writes=("gateT",))

        g1 = lambda a: lnp[:, 0 * NA + a:0 * NA + a + 1]
        b1 = lambda a: lnp[:, 1 * NA + a:1 * NA + a + 1]
        g2 = lambda a: lnp[:, 2 * NA + a:2 * NA + a + 1]
        b2 = lambda a: lnp[:, 3 * NA + a:3 * NA + a + 1]

        for a in range(NA):
            p.tv("vector", "tensor_scalar", gwg[:, a * NE:(a + 1) * NE],
                 gateT[:, a * NE:(a + 1) * NE], g2(a), None, ALU.mult,
                 reads=("gateT", "lnp"), writes=("gwg",))
        fns = [(lambda e, a=a: e.matmul(ps_st[0:1, 0:NE], b2(a), gateT[:, a * NE:(a + 1) * NE],
                                        start=(a == 0), stop=(a == NA - 1)))
               for a in range(NA)]
        p.group("tensor", fns, reads=("lnp", "gateT"), writes=("ps_st",))
        p.tv("vector", "tensor_copy", b2gw[:], ps_st[0:1, 0:NE], reads=("ps_st",), writes=("b2gw",))

        def ln_to(src, skeys, dst, dkeys, W, eps, gain=None, bias=None):
            stat_sums(p, [(src[:, a * W:(a + 1) * W], skeys[a]) for a in range(NA)],
                      W, ps_st, S1, [f"sq{a}" for a in range(NA)], onesc)
            rstd, mur = ln_rows(p, rA, rB, W, ps_st, DM, eps)
            bcast_row(p, rstd, "rA2", W, ps_bc, sc_b, "sc_b", ones1)
            bcast_row(p, mur, "rA3", W, ps_bc, iv_b, "iv_b", ones1)
            for a in range(NA):
                p.tv("vector", "tensor_mul", S1[:, a * W:(a + 1) * W],
                     src[:, a * W:(a + 1) * W], sc_b[:, 0:W],
                     reads=(skeys[a], "sc_b"), writes=(f"sq{a}",))
                if gain is None:
                    p.tv("vector", "tensor_sub", dst[:, a * W:(a + 1) * W],
                         S1[:, a * W:(a + 1) * W], iv_b[:, 0:W],
                         reads=(f"sq{a}", "iv_b"), writes=(dkeys[a],))
                else:
                    p.tv("vector", "tensor_sub", S1[:, a * W:(a + 1) * W],
                         S1[:, a * W:(a + 1) * W], iv_b[:, 0:W],
                         reads=(f"sq{a}", "iv_b"), writes=(f"sq{a}",))
                    p.tv("vector", "tensor_scalar", dst[:, a * W:(a + 1) * W],
                         S1[:, a * W:(a + 1) * W], gain(a), bias(a), ALU.mult, ALU.add,
                         reads=(f"sq{a}", "lnp"), writes=(dkeys[a],))

        for s in range(NSEQ):
            for c in range(4):
                base = c * CH
                for a in range(NA):
                    p.dma("sync", xcur[:, a * CH:(a + 1) * CH],
                          x_d[s * DM + a * 128:s * DM + (a + 1) * 128, base:base + CH],
                          writes=(f"xcur{a}",))
                for a in range(NA):
                    p.dma("gpsimd", wbufW[:, a * DM:(a + 1) * DM],
                          attnT_d[a * 128:(a + 1) * 128, :], writes=("attnT", "wbufW"))
                ln_to(xcur, [f"xcur{a}" for a in range(NA)], xin,
                      [f"xin{a}" for a in range(NA)], CH, 1e-5)
                ln_to(xin, [f"xin{a}" for a in range(NA)], tq,
                      [f"tq{a}" for a in range(NA)], CH, 1e-5, gain=g1, bias=b1)
                for ob in range(NA):
                    ps, pk = ps_d[ob % 4], f"ps_d{ob % 4}"
                    fns = [(lambda e, a=a, ob=ob, ps=ps: e.matmul(
                        ps[:, 0:CH], wbufW[:, a * DM + ob * 128:a * DM + (ob + 1) * 128],
                        tq[:, a * CH:(a + 1) * CH], start=(a == 0), stop=(a == NA - 1)))
                        for a in range(NA)]
                    p.group("tensor", fns,
                            reads=("attnT", "wbufW") + tuple(f"tq{a}" for a in range(NA)), writes=(pk,))
                    p.tv("vector", "tensor_add", hbuf[:, ob * CH:(ob + 1) * CH],
                         ps[:, 0:CH], xin[:, ob * CH:(ob + 1) * CH],
                         reads=(pk, f"xin{ob}"), writes=(f"h{ob}",))
                if DEBUG_L1FFN and s == 0 and c == 0:
                    p.dma("sync", dbg_xin, xin[:], reads=tuple(f"xin{a}" for a in range(NA)),
                          writes=("dbgxin",))
                    p.dma("sync", dbg_h, hbuf[:], reads=tuple(f"h{a}" for a in range(NA)),
                          writes=("dbgh",))
                ln_to(hbuf, [f"h{a}" for a in range(NA)], tq,
                      [f"tq{a}" for a in range(NA)], CH, 1e-5, gain=g2, bias=b2)
                fns = [(lambda e, a=a: e.matmul(
                    ps_st[0:8, 0:CH], gwg[:, a * NE:(a + 1) * NE],
                    S1[:, a * CH:(a + 1) * CH], start=(a == 0), stop=False))
                    for a in range(NA)]
                fns.append(lambda e: e.matmul(ps_st[0:8, 0:CH], b2gw[:], onesrow[:, 0:CH],
                                              start=False, stop=True))
                p.group("tensor", fns,
                        reads=("gwg", "b2gw", "onesrow") + tuple(f"sq{a}" for a in range(NA)),
                        writes=("ps_st",))
                p.tv("scalar", "copy", lgt[:], ps_st[0:8, 0:CH], reads=("ps_st",), writes=("lgt",))
                if DEBUG_L1FFN and s == 0 and c == 0:
                    p.tv("vector", "tensor_copy", S1[:, 0:NA * CH], tq[:, 0:NA * CH],
                         reads=tuple(f"tq{a}" for a in range(NA)),
                         writes=("dbgt_s",) + tuple(f"sq{a}" for a in range(NA)))
                    p.dma("sync", dbg_t, S1[:], reads=("dbgt_s",), writes=("dbgt",))
                    p.dma("sync", dbg_lgt, lgt[0:8, :], reads=("lgt",), writes=("dbglgt",))
                for blk in range(4):
                    p.tv("tensor", "transpose", ps_st[0:128, 0:8],
                         lgt[0:8, blk * 128:(blk + 1) * 128], idf[0:8, 0:8],
                         reads=("lgt", "idf"), writes=("ps_st",))
                    p.tv("vector", "tensor_copy", lgt_tm[:, blk * 8:(blk + 1) * 8],
                         ps_st[0:128, 0:8], reads=("ps_st",), writes=(f"ltm{blk}",))
                for blk in range(4):
                    lt = lgt_tm[:, blk * 8:(blk + 1) * 8]
                    lk = f"ltm{blk}"
                    p.tv("vector", "reduce_max", m1c[:], lt, AX.X, reads=(lk,), writes=("m1c",))
                    p.tv("vector", "tensor_scalar", eqA[:], lt, m1c[:, 0:1], None, ALU.is_equal,
                         reads=(lk, "m1c"), writes=("eqA",))
                    p.tv("vector", "scalar_tensor_tensor", mskd[:], eqA[:], -1e30, lt,
                         ALU.mult, ALU.add, reads=("eqA", lk), writes=("mskd",))
                    p.tv("vector", "reduce_max", m2c[:], mskd[:], AX.X,
                         reads=("mskd",), writes=("m2c",))
                    p.tv("vector", "tensor_scalar", eqB[:], mskd[:], m2c[:, 0:1], None,
                         ALU.is_equal, reads=("mskd", "m2c"), writes=("eqB",))
                    p.tv("vector", "tensor_sub", e1c[:], m2c[:], m1c[:],
                         reads=("m1c", "m2c"), writes=("e1c",))
                    p.tv("scalar", "activation", e1c[:], e1c[:], AF.Exp,
                         reads=("e1c",), writes=("e1c",))
                    p.tv("vector", "tensor_scalar", w1c[:], e1c[:], 1.0, None, ALU.add,
                         reads=("e1c",), writes=("w1c",))
                    p.tv("vector", "reciprocal", w1c[:], w1c[:], reads=("w1c",), writes=("w1c",))
                    p.tv("vector", "tensor_mul", w2c[:], e1c[:], w1c[:],
                         reads=("e1c", "w1c"), writes=("w2c",))
                    p.tv("vector", "tensor_scalar", eqA[:], eqA[:], w1c[:, 0:1], None, ALU.mult,
                         reads=("eqA", "w1c"), writes=("eqA",))
                    p.tv("vector", "scalar_tensor_tensor", comb_tm[:, blk * 8:(blk + 1) * 8],
                         eqB[:], w2c[:, 0:1], eqA[:], ALU.mult, ALU.add,
                         reads=("eqB", "w2c", "eqA"), writes=(f"ctm{blk}",))
                    p.tv("tensor", "transpose", ps_st[0:8, 0:128],
                         comb_tm[:, blk * 8:(blk + 1) * 8], idf[:],
                         reads=(f"ctm{blk}", "idf"), writes=("ps_st",))
                    p.tv("vector", "tensor_copy", comb_fm[0:8, blk * 128:(blk + 1) * 128],
                         ps_st[0:8, 0:128], reads=("ps_st",), writes=("comb_fm",))
                p.dma("sync", scr2_d, comb_fm[0:8, :], reads=("comb_fm",), writes=("scr2",))
                if DEBUG_L1FFN and s == 0 and c == 0:
                    p.dma("sync", dbg_cmb, comb_fm[0:8, :], reads=("comb_fm",), writes=("dbgcmb",))
                for ex in range(NE):
                    p.dma("sync", crow[0:1, :], scr2_d[ex:ex + 1, :],
                          reads=("scr2",), writes=("crow",))
                    bcast_row(p, crow[0:1, 0:CH], "crow", CH, ps_bc,
                              combB, "combB", ones1)
                    for half in range(2):
                        for a in range(NA):
                            p.dma("gpsimd", wbufW[:, a * 2048:(a + 1) * 2048],
                                  gpT_d[ex * DM + a * 128:ex * DM + (a + 1) * 128,
                                        half * 2048:(half + 1) * 2048],
                                  writes=("wbufG", "wbufW"))
                            p.dma("gpsimd", wbufW[:, 16384 + a * 2048:16384 + (a + 1) * 2048],
                                  upT_d[ex * DM + a * 128:ex * DM + (a + 1) * 128,
                                        half * 2048:(half + 1) * 2048],
                                  writes=("wbufU", "wbufW"))
                        for i in range(16 * half, 16 * half + 16):
                            ii = i - 16 * half
                            fns = [(lambda e, a=a, ii=ii: e.matmul(
                                ps_g[:, 0:CH], wbufW[:, a * 2048 + ii * 128:a * 2048 + (ii + 1) * 128],
                                tq[:, a * CH:(a + 1) * CH], start=(a == 0), stop=(a == NA - 1)))
                                for a in range(NA)]
                            p.group("tensor", fns,
                                    reads=("wbufG", "wbufW") + tuple(f"tq{a}" for a in range(NA)),
                                    writes=("ps_g",))
                            fns = [(lambda e, a=a, ii=ii: e.matmul(
                                ps_u[:, 0:CH],
                                wbufW[:, 16384 + a * 2048 + ii * 128:16384 + a * 2048 + (ii + 1) * 128],
                                tq[:, a * CH:(a + 1) * CH], start=(a == 0), stop=(a == NA - 1)))
                                for a in range(NA)]
                            p.group("tensor", fns,
                                    reads=("wbufU", "wbufW") + tuple(f"tq{a}" for a in range(NA)),
                                    writes=("ps_u",))
                            gb, gk = gs[i % 2], f"gs{i % 2}"
                            p.tv("scalar", "activation", gb[:, 0:CH], ps_g[:, 0:CH], AF.Silu,
                                 reads=("ps_g",), writes=(gk,))
                            p.tv("vector", "tensor_mul", gu[:, i * CH:(i + 1) * CH],
                                 gb[:, 0:CH], ps_u[:, 0:CH],
                                 reads=(gk, "ps_u"), writes=(f"gu{i}",))
                            if DEBUG_L1FFN and s == 0 and c == 0 and ex == 0 and i == 0:
                                p.tv("vector", "tensor_copy", otmp[:, 0:CH],
                                     gu[:, 0:CH], reads=("gu0",), writes=("dbggu_s",))
                                p.dma("sync", dbg_gu, otmp[:, 0:CH],
                                      reads=("dbggu_s",), writes=("dbggu",))
                    for k in range(NI):
                        p.dma("gpsimd", wbufW[:, k * DM:(k + 1) * DM],
                              dnT_d[ex * DI + k * 128:ex * DI + (k + 1) * 128, :],
                              writes=("wbufD", "wbufW"))
                    for sweep in range(2):
                        for ob in range(sweep * 4, sweep * 4 + 4):
                            ps, pk = ps_d[ob % 4], f"ps_d{ob % 4}"
                            fns = [(lambda e, k=k, ob=ob, ps=ps: e.matmul(
                                ps[:, 0:CH], wbufW[:, k * DM + ob * 128:k * DM + (ob + 1) * 128],
                                gu[:, k * CH:(k + 1) * CH], start=(k == 0), stop=(k == NI - 1)))
                                for k in range(NI)]
                            p.group("tensor", fns,
                                    reads=("wbufD", "wbufW") + tuple(f"gu{k}" for k in range(NI)),
                                    writes=(pk,))
                            p.tv("vector", "tensor_mul", otmp[:, 0:CH], ps[:, 0:CH],
                                 combB[:, 0:CH], reads=(pk, "combB"), writes=("otmp",))
                            p.tv("gpsimd", "tensor_tensor", hbuf[:, ob * CH:(ob + 1) * CH],
                                 hbuf[:, ob * CH:(ob + 1) * CH], otmp[:, 0:CH], ALU.add,
                                 reads=(f"h{ob}", "otmp"), writes=(f"h{ob}",))
                if DEBUG_L1FFN and s == 0 and c == 0:
                    p.dma("sync", dbg_hf, hbuf[:], reads=tuple(f"h{a}" for a in range(NA)),
                          writes=("dbghf",))
                for ob in range(NA):
                    o, ok = obuf[0], "ob0"
                    p.tv("vector", "tensor_add", o[:, 0:CH], xcur[:, ob * CH:(ob + 1) * CH],
                         hbuf[:, ob * CH:(ob + 1) * CH],
                         reads=(f"xcur{ob}", f"h{ob}"), writes=(ok,))
                    p.dma("sync", xo_d[s * DM + ob * 128:s * DM + (ob + 1) * 128, base:base + CH],
                          o[:, 0:CH], reads=(ok,), writes=(f"xo_s{s}c{c}o{ob}",))
    return p.finalize()


def run_l1ffn(inputs, x_current, cores=8):
    if "l1ffn" not in _cache:
        _cache["l1ffn"] = build_l1ffn()
    nc = _cache["l1ffn"]
    f32 = lambda v: np.asarray(v, dtype=np.float32)
    w = {
        "attnT": np.ascontiguousarray(f32(inputs["L1_ffn_attn_w"]).T),
        "gateT": np.ascontiguousarray(f32(inputs["L1_ffn_gate_w"]).T.reshape(8, 128, 8)
                                      .transpose(1, 0, 2).reshape(128, 64)),
        "gpT": np.ascontiguousarray(f32(inputs["L1_ffn_gate_proj"]).transpose(0, 2, 1)).reshape(8 * 1024, 4096),
        "upT": np.ascontiguousarray(f32(inputs["L1_ffn_up_proj"]).transpose(0, 2, 1)).reshape(8 * 1024, 4096),
        "dnT": np.ascontiguousarray(f32(inputs["L1_ffn_down_proj"]).transpose(0, 2, 1)).reshape(8 * 4096, 1024),
        "lnp": np.ascontiguousarray(np.stack([
            f32(inputs["L1_ffn_ln1_g"]).reshape(8, 128),
            f32(inputs["L1_ffn_ln1_b"]).reshape(8, 128),
            f32(inputs["L1_ffn_ln2_g"]).reshape(8, 128),
            f32(inputs["L1_ffn_ln2_b"]).reshape(8, 128),
        ], axis=0).reshape(32, 128).T.reshape(128, 32)),
        "identf32": np.eye(128, dtype=np.float32),
    }
    in_maps = [dict(w, x_res=_shard_x(x_current, c)) for c in range(cores)]
    res = run_bass_kernel_spmd(nc, in_maps, core_ids=list(range(cores)))
    return _unshard([r for r in res.results])



DEBUG_L1ATTN = False


def build_l1attn():
    p = Prog("l1attn")
    build_l1attn.last_prog = p
    nc = p.nc
    CH = 512

    x_d = nc.dram_tensor("x_res", [NSEQ * DM, T], F32, kind="ExternalInput").ap()
    xo_d = nc.dram_tensor("x_out", [NSEQ * DM, T], F32, kind="ExternalOutput").ap()
    wk_d = nc.dram_tensor("wk", [DM, DM], F32, kind="ExternalInput").ap()
    wv_d = nc.dram_tensor("wv", [DM, DM], F32, kind="ExternalInput").ap()
    wr_d = nc.dram_tensor("wr", [DM, DM], F32, kind="ExternalInput").ap()
    wo_d = nc.dram_tensor("wo", [DM, DM], F32, kind="ExternalInput").ap()
    cmk_d = nc.dram_tensor("cmk", [DM, 4 * DM], F32, kind="ExternalInput").ap()
    cmr_d = nc.dram_tensor("cmr", [DM, DM], F32, kind="ExternalInput").ap()
    cmv_d = nc.dram_tensor("cmv", [4 * DM, DM], F32, kind="ExternalInput").ap()
    par_d = nc.dram_tensor("par", [128, 11 * NA], F32, kind="ExternalInput").ap()

    with ExitStack() as st:
        sb = lambda n_, s, d: st.enter_context(nc.sbuf_tensor(n_, s, d))
        psb = lambda n_, s: st.enter_context(nc.psum_tensor(n_, s, F32))

        xcur = sb("xcur", [128, NA * CH], F32)
        xin = sb("xin", [128, NA * CH], F32)
        hm = sb("hm", [128, NA * CH], BF16)
        tmlead = sb("tmlead", [128, NA], F32)
        cmlead = sb("cmlead", [128, NA], F32)
        mxk = sb("mxk", [128, NA * CH], BF16)
        mxv = sb("mxv", [128, NA * CH], BF16)
        mxr = sb("mxr", [128, NA * CH], BF16)
        rw = sb("rw", [128, NA * CH], BF16)
        kcq = sb("kcq", [128, 32 * CH], BF16)
        wbufW = sb("wbufW", [128, 32768], BF16)
        par = sb("par_sb", [128, 11 * NA], F32)
        sqt = [sb(f"sqt{i}", [128, CH], F32) for i in range(2)]
        ek = sb("ek", [128, CH], F32)
        vv = sb("vv", [128, CH], F32)
        ekv = sb("ekv", [128, CH], F32)
        dtile = sb("dtile", [128, CH], F32)
        scanA = sb("scanA", [128, CH + 1], F32)
        scanB = sb("scanB", [128, CH + 1], F32)
        wkvb = None
        carA = sb("carA", [128, NA], F32)
        carB = sb("carB", [128, NA], F32)
        dcol = sb("dcol", [128, NA], F32)
        eucol = sb("eucol", [128, NA], F32)
        obuf = sb("obuf", [128, CH], F32)
        rows = sb("rows", [1, 4 * 512], F32)
        rows2 = sb("rows2", [1, 2 * 512], F32)
        rA = lambda i, W: rows[0:1, i * 512:i * 512 + W]
        rB = lambda i, W: rows2[0:1, (0 if i == 0 else 1) * 512:(0 if i == 0 else 1) * 512 + W]
        sc_b = sb("sc_b", [128, 512], F32)
        iv_b = sb("iv_b", [128, 512], F32)
        ones1 = sb("ones1", [1, 128], F32)
        onesc = sb("onesc", [128, 1], F32)

        ps_k = psb("ps_k", [128, 512])
        ps_v = psb("ps_v", [128, 512])
        ps_r = psb("ps_r", [128, 512])
        ps_o = psb("ps_o", [128, 512])
        ps_o2 = psb("ps_o2", [128, 512])
        ps_st = psb("ps_st", [128, 512])
        ps_bc = psb("ps_bc", [128, 512])

        p.tv("vector", "memset", ones1[:], 1.0, writes=("ones1",))
        p.tv("vector", "memset", onesc[:], 1.0, writes=("onesc",))
        p.dma("sync", par[:], par_d, writes=("par",))
        p.dma("sync", xo_d, x_d, writes=("xo_init",))

        PG = lambda g, a: par[:, g * NA + a:g * NA + a + 1]
        ln1g, ln1b = (lambda a: PG(0, a)), (lambda a: PG(1, a))
        ln2g, ln2b = (lambda a: PG(2, a)), (lambda a: PG(3, a))
        tmk, tmv, tmr = (lambda a: PG(4, a)), (lambda a: PG(5, a)), (lambda a: PG(6, a))
        cmkx, cmrx = (lambda a: PG(9, a)), (lambda a: PG(10, a))

        p.tv("scalar", "activation", dcol[:], par[:, 7 * NA:8 * NA], AF.Exp,
             reads=("par",), writes=("dcol",))
        p.tv("vector", "tensor_scalar", dcol[:], dcol[:], -1.0, None, ALU.mult,
             reads=("dcol",), writes=("dcol",))
        p.tv("scalar", "activation", dcol[:], dcol[:], AF.Exp, reads=("dcol",), writes=("dcol",))
        p.tv("scalar", "activation", eucol[:], par[:, 8 * NA:9 * NA], AF.Exp,
             reads=("par",), writes=("eucol",))

        def stats_seq(src, skeys, W):
            n = len(skeys)
            for a in range(n):
                sq, sk = sqt[a % 2], f"sqt{a % 2}"
                p.tv("scalar", "activation", sq[:, 0:W], src[:, a * W:(a + 1) * W],
                     AF.Square, reads=(skeys[a],), writes=(sk,))
                p.emit("tensor", lambda e, a=a, W=W, src=src, n=n: e.matmul(
                    ps_st[0:1, 0:W], onesc[:, 0:1], src[:, a * W:(a + 1) * W],
                    start=(a == 0), stop=(a == n - 1)),
                    reads=(skeys[a], "onesc"), writes=("ps_st",))
                p.emit("tensor", lambda e, a=a, W=W, sq=sq, n=n: e.matmul(
                    ps_st[32:33, 0:W], onesc[:, 0:1], sq[:, 0:W],
                    start=(a == 0), stop=(a == n - 1)),
                    reads=(sk, "onesc"), writes=("ps_st",))

        def ln_full(src, skeys, dst, dkeys, W, eps, gain=None, bias=None):
            stats_seq(src, skeys, W)
            rstd, mur = ln_rows(p, rA, rB, W, ps_st, DM, eps)
            bcast_row(p, rstd, "rA2", W, ps_bc, sc_b, "sc_b", ones1)
            bcast_row(p, mur, "rA3", W, ps_bc, iv_b, "iv_b", ones1)
            for a in range(NA):
                off = a * W
                sq, sk = sqt[a % 2], f"sqt{a % 2}"
                p.tv("vector", "tensor_mul", sq[:, 0:W], src[:, a * W:(a + 1) * W],
                     sc_b[:, 0:W], reads=(skeys[a], "sc_b"), writes=(sk,))
                if gain is None:
                    p.tv("vector", "tensor_sub", dst[:, off:off + W], sq[:, 0:W],
                         iv_b[:, 0:W], reads=(sk, "iv_b"), writes=(dkeys[a],))
                else:
                    p.tv("vector", "tensor_sub", sq[:, 0:W], sq[:, 0:W], iv_b[:, 0:W],
                         reads=(sk, "iv_b"), writes=(sk,))
                    p.tv("vector", "tensor_scalar", dst[:, off:off + W], sq[:, 0:W],
                         gain(a), bias(a), ALU.mult, ALU.add,
                         reads=(sk, "par"), writes=(dkeys[a],))

        def mixes(dsts, leadbuf, leadkey):
            for a in range(NA):
                off = a * CH
                cur1 = hm[:, off + 1:off + CH]
                shf1 = hm[:, off:off + CH - 1]
                dif = sqt[0]
                p.tv("vector", "tensor_sub", dif[:, 1:CH], cur1, shf1,
                     reads=(f"hm{a}",), writes=("sqt0",))
                p.tv("vector", "tensor_sub", dif[:, 0:1], hm[:, off:off + 1],
                     leadbuf[:, a:a + 1], reads=(f"hm{a}", leadkey), writes=("sqt0",))
                for (dst, kp, mc) in dsts:
                    p.tv("vector", "scalar_tensor_tensor", dst[:, a * CH + 1:(a + 1) * CH],
                         dif[:, 1:CH], mc(a), shf1, ALU.mult, ALU.add,
                         reads=("sqt0", "par", f"hm{a}"), writes=(f"{kp}{a}",))
                    p.tv("vector", "scalar_tensor_tensor", dst[:, a * CH:a * CH + 1],
                         dif[:, 0:1], mc(a), leadbuf[:, a:a + 1], ALU.mult, ALU.add,
                         reads=("sqt0", "par", leadkey), writes=(f"{kp}{a}",))

        def mm_group(psum, pk, lhsf, rhs, rhs_keys, n, extra_reads=()):
            fns = [(lambda e, a=a: e.matmul(psum[:, 0:CH], lhsf(a),
                                            rhs[:, a * CH:(a + 1) * CH],
                                            start=(a == 0), stop=(a == n - 1)))
                   for a in range(n)]
            p.group("tensor", fns, reads=tuple(rhs_keys) + tuple(extra_reads), writes=(pk,))

        for s in range(NSEQ):
            p.tv("vector", "memset", carA[:], 0.0, writes=("carA",))
            p.tv("vector", "memset", carB[:], 0.0, writes=("carB",))
            p.tv("vector", "memset", tmlead[:], 0.0, writes=("tmlead",))
            p.tv("vector", "memset", cmlead[:], 0.0, writes=("cmlead",))
            for c in range(4):
                base = c * CH
                for a in range(NA):
                    p.dma("sync", xcur[:, a * CH:(a + 1) * CH],
                          x_d[s * DM + a * 128:s * DM + (a + 1) * 128, base:base + CH],
                          writes=(f"xcur{a}",))
                for wi, wd in enumerate((wk_d, wv_d, wr_d, wo_d)):
                    for a in range(NA):
                        p.dma("gpsimd", wbufW[:, wi * 8192 + a * DM:wi * 8192 + (a + 1) * DM],
                              wd[a * 128:(a + 1) * 128, :], writes=("wbufW",))
                xink = [f"xin{a}" for a in range(NA)]
                ln_full(xcur, [f"xcur{a}" for a in range(NA)], xin, xink, CH, 1e-5)
                ln_full(xin, xink, hm, [f"hm{a}" for a in range(NA)], CH, 1e-3,
                        gain=ln1g, bias=ln1b)
                mixes([(mxk, "mxk", tmk), (mxv, "mxv", tmv), (mxr, "mxr", tmr)],
                      tmlead, "tmlead")
                for a in range(NA):
                    p.tv("gpsimd", "tensor_copy", tmlead[:, a:a + 1],
                         hm[:, a * CH + CH - 1:a * CH + CH],
                         reads=(f"hm{a}",), writes=("tmlead",))
                for a in range(NA):
                    mk = tuple(f"mxk{i}" for i in range(NA))
                    mv = tuple(f"mxv{i}" for i in range(NA))
                    mr = tuple(f"mxr{i}" for i in range(NA))
                    mm_group(ps_k, "ps_k",
                             lambda e_, a=a: wbufW[:, 0 * 8192 + e_ * DM + a * 128:
                                                   0 * 8192 + e_ * DM + (a + 1) * 128],
                             mxk, mk, NA, extra_reads=("wbufW",))
                    mm_group(ps_v, "ps_v",
                             lambda e_, a=a: wbufW[:, 1 * 8192 + e_ * DM + a * 128:
                                                   1 * 8192 + e_ * DM + (a + 1) * 128],
                             mxv, mv, NA, extra_reads=("wbufW",))
                    mm_group(ps_r, "ps_r",
                             lambda e_, a=a: wbufW[:, 2 * 8192 + e_ * DM + a * 128:
                                                   2 * 8192 + e_ * DM + (a + 1) * 128],
                             mxr, mr, NA, extra_reads=("wbufW",))
                    p.tv("scalar", "activation", ek[:, 0:CH], ps_k[:, 0:CH], AF.Exp,
                         reads=("ps_k",), writes=("ek",))
                    p.tv("scalar", "copy", vv[:, 0:CH], ps_v[:, 0:CH],
                         reads=("ps_v",), writes=("vv",))
                    p.tv("vector", "tensor_mul", ekv[:, 0:CH], ek[:, 0:CH], vv[:, 0:CH],
                         reads=("ek", "vv"), writes=("ekv",))
                    p.tv("vector", "memset", dtile[:, 0:CH], 1.0, writes=("dtile",))
                    p.tv("vector", "tensor_scalar", dtile[:, 0:CH], dtile[:, 0:CH],
                         dcol[:, a:a + 1], None, ALU.mult,
                         reads=("dtile", "dcol"), writes=("dtile",))
                    p.tv("vector", "tensor_copy", scanA[:, 0:1], carA[:, a:a + 1],
                         reads=("carA",), writes=("scanA",))
                    p.tv("vector", "tensor_tensor_scan", scanA[:, 1:CH + 1], dtile[:, 0:CH],
                         ekv[:, 0:CH], carA[:, a:a + 1], ALU.mult, ALU.add,
                         reads=("dtile", "ekv", "carA"), writes=("scanA",))
                    p.tv("vector", "tensor_copy", scanB[:, 0:1], carB[:, a:a + 1],
                         reads=("carB",), writes=("scanB",))
                    p.tv("vector", "tensor_tensor_scan", scanB[:, 1:CH + 1], dtile[:, 0:CH],
                         ek[:, 0:CH], carB[:, a:a + 1], ALU.mult, ALU.add,
                         reads=("dtile", "ek", "carB"), writes=("scanB",))
                    p.tv("vector", "tensor_copy", carA[:, a:a + 1], scanA[:, CH:CH + 1],
                         reads=("scanA",), writes=("carA",))
                    p.tv("vector", "tensor_copy", carB[:, a:a + 1], scanB[:, CH:CH + 1],
                         reads=("scanB",), writes=("carB",))
                    p.tv("vector", "tensor_scalar", ek[:, 0:CH], ek[:, 0:CH],
                         eucol[:, a:a + 1], None, ALU.mult,
                         reads=("ek", "eucol"), writes=("ek",))
                    p.tv("vector", "tensor_mul", ekv[:, 0:CH], ek[:, 0:CH], vv[:, 0:CH],
                         reads=("ek", "vv"), writes=("ekv",))
                    p.tv("vector", "tensor_add", dtile[:, 0:CH], scanB[:, 0:CH], ek[:, 0:CH],
                         reads=("scanB", "ek", "dtile"), writes=("dtile",))
                    p.tv("vector", "reciprocal", dtile[:, 0:CH], dtile[:, 0:CH],
                         reads=("dtile",), writes=("dtile",))
                    p.tv("vector", "tensor_add", ekv[:, 0:CH], scanA[:, 0:CH], ekv[:, 0:CH],
                         reads=("scanA", "ekv"), writes=("ekv",))
                    p.tv("vector", "tensor_mul", ekv[:, 0:CH], ekv[:, 0:CH], dtile[:, 0:CH],
                         reads=("ekv", "dtile"), writes=("ekv",))
                    p.tv("scalar", "activation", ek[:, 0:CH], ps_r[:, 0:CH], AF.Sigmoid,
                         reads=("ps_r",), writes=("ek",))
                    p.tv("vector", "tensor_mul", rw[:, a * CH:(a + 1) * CH], ek[:, 0:CH],
                         ekv[:, 0:CH], reads=("ek", "ekv"), writes=(f"rw{a}",))
                for ob in range(NA):
                    ps, pk = (ps_o, "ps_o") if ob % 2 == 0 else (ps_o2, "ps_o2")
                    mm_group(ps, pk,
                             lambda e_, ob=ob: wbufW[:, 3 * 8192 + e_ * DM + ob * 128:
                                                     3 * 8192 + e_ * DM + (ob + 1) * 128],
                             rw, tuple(f"rw{i}" for i in range(NA)), NA,
                             extra_reads=("wbufW",))
                    p.tv("vector", "tensor_add", xin[:, ob * CH:(ob + 1) * CH],
                         ps[:, 0:CH], xin[:, ob * CH:(ob + 1) * CH],
                         reads=(pk, f"xin{ob}"), writes=(f"xin{ob}",))
                ln_full(xin, xink, hm, [f"hm{a}" for a in range(NA)], CH, 1e-3,
                        gain=ln2g, bias=ln2b)
                mixes([(mxk, "mxk", cmkx), (mxr, "mxr", cmrx)], cmlead, "cmlead")
                for a in range(NA):
                    p.tv("gpsimd", "tensor_copy", cmlead[:, a:a + 1],
                         hm[:, a * CH + CH - 1:a * CH + CH],
                         reads=(f"hm{a}",), writes=("cmlead",))
                for a in range(NA):
                    p.dma("gpsimd", wbufW[:, a * 4096:(a + 1) * 4096],
                          cmk_d[a * 128:(a + 1) * 128, :], writes=("wbufW",))
                for ob in range(32):
                    ps, pk = (ps_o, "ps_o") if ob % 2 == 0 else (ps_o2, "ps_o2")
                    mm_group(ps, pk,
                             lambda e_, ob=ob: wbufW[:, e_ * 4096 + ob * 128:
                                                     e_ * 4096 + (ob + 1) * 128],
                             mxk, tuple(f"mxk{i}" for i in range(NA)), NA,
                             extra_reads=("wbufW",))
                    sq, sk = sqt[ob % 2], f"sqt{ob % 2}"
                    p.tv("scalar", "activation", sq[:, 0:CH], ps[:, 0:CH], AF.Relu,
                         reads=(pk,), writes=(sk,))
                    p.tv("scalar", "activation", kcq[:, ob * CH:(ob + 1) * CH], sq[:, 0:CH],
                         AF.Square, reads=(sk,), writes=(f"kcq{ob}",))
                for a in range(NA):
                    p.dma("gpsimd", wbufW[:, a * DM:(a + 1) * DM],
                          cmr_d[a * 128:(a + 1) * 128, :], writes=("wbufW",))
                for ob in range(NA):
                    ps, pk = (ps_o, "ps_o") if ob % 2 == 0 else (ps_o2, "ps_o2")
                    mm_group(ps, pk,
                             lambda e_, ob=ob: wbufW[:, e_ * DM + ob * 128:
                                                     e_ * DM + (ob + 1) * 128],
                             mxr, tuple(f"mxr{i}" for i in range(NA)), NA,
                             extra_reads=("wbufW",))
                    p.tv("scalar", "activation", xcur[:, ob * CH:(ob + 1) * CH], ps[:, 0:CH],
                         AF.Sigmoid, reads=(pk,), writes=(f"xcur{ob}",))
                for k in range(32):
                    p.dma("gpsimd", wbufW[:, k * DM:(k + 1) * DM],
                          cmv_d[k * 128:(k + 1) * 128, :], writes=("wbufW",))
                for ob in range(NA):
                    ps, pk = (ps_o, "ps_o") if ob % 2 == 0 else (ps_o2, "ps_o2")
                    fns = [(lambda e, k=k, ob=ob, ps=ps: e.matmul(
                        ps[:, 0:CH], wbufW[:, k * DM + ob * 128:k * DM + (ob + 1) * 128],
                        kcq[:, k * CH:(k + 1) * CH], start=(k == 0), stop=(k == 31)))
                        for k in range(32)]
                    p.group("tensor", fns,
                            reads=("wbufW",) + tuple(f"kcq{k}" for k in range(32)),
                            writes=(pk,))
                    p.tv("vector", "tensor_mul", obuf[:, 0:CH], ps[:, 0:CH],
                         xcur[:, ob * CH:(ob + 1) * CH], reads=(pk, f"xcur{ob}"),
                         writes=("obuf",))
                    p.tv("vector", "tensor_add", obuf[:, 0:CH], obuf[:, 0:CH],
                         xin[:, ob * CH:(ob + 1) * CH], reads=("obuf", f"xin{ob}"),
                         writes=("obuf",))
                    p.dma("gpsimd", xo_d[s * DM + ob * 128:s * DM + (ob + 1) * 128, base:base + CH],
                          obuf[:, 0:CH], reads=("obuf", "xo_init"),
                          writes=(f"xo{s}_{c}_{ob}",), accum_op=ALU.add)
    return p.finalize()


def run_l1attn(inputs, x_current, cores=8):
    if "l1attn" not in _cache:
        _cache["l1attn"] = build_l1attn()
    nc = _cache["l1attn"]
    f32 = lambda v: np.asarray(v, dtype=np.float32)
    ctile = lambda v: np.ascontiguousarray(f32(v).reshape(8, 128).T)
    par = np.concatenate([
        ctile(inputs["L1_attn_ln1_g"]), ctile(inputs["L1_attn_ln1_b"]),
        ctile(inputs["L1_attn_ln2_g"]), ctile(inputs["L1_attn_ln2_b"]),
        ctile(inputs["L1_attn_tm_mix_k"]), ctile(inputs["L1_attn_tm_mix_v"]),
        ctile(inputs["L1_attn_tm_mix_r"]), ctile(inputs["L1_attn_time_decay"]),
        ctile(inputs["L1_attn_time_first"]), ctile(inputs["L1_attn_cm_mix_k"]),
        ctile(inputs["L1_attn_cm_mix_r"])], axis=1)
    w = {
        "wk": np.ascontiguousarray(f32(inputs["L1_attn_tm_Wk"])),
        "wv": np.ascontiguousarray(f32(inputs["L1_attn_tm_Wv"])),
        "wr": np.ascontiguousarray(f32(inputs["L1_attn_tm_Wr"])),
        "wo": np.ascontiguousarray(f32(inputs["L1_attn_Wo"])),
        "cmk": np.ascontiguousarray(f32(inputs["L1_attn_cm_Wk"])),
        "cmr": np.ascontiguousarray(f32(inputs["L1_attn_cm_Wr"])),
        "cmv": np.ascontiguousarray(f32(inputs["L1_attn_cm_Wv"])),
        "par": np.ascontiguousarray(par),
    }
    in_maps = [dict(w, x_res=_shard_x(x_current, c)) for c in range(cores)]
    res = run_bass_kernel_spmd(nc, in_maps, core_ids=list(range(cores)))
    return _unshard([r for r in res.results])



DEBUG_L0ATTN = False


def build_l0attn():
    p = Prog("l0attn")
    build_l0attn.last_prog = p
    nc = p.nc
    CW = 512
    DI = 2048
    NXT = 16
    NBC = 17
    SC = 64

    x_d = nc.dram_tensor("x_res", [NSEQ * DM, T], F32, kind="ExternalInput").ap()
    xo_d = nc.dram_tensor("x_out", [NSEQ * DM, T], F32, kind="ExternalOutput").ap()
    ipT_d = nc.dram_tensor("ipT", [DM, 4256], F32, kind="ExternalInput").ap()
    opT_d = nc.dram_tensor("opT", [DI, DM], F32, kind="ExternalInput").ap()
    convw_d = nc.dram_tensor("convw", [128, NBC * 4], F32, kind="ExternalInput").ap()
    convb_d = nc.dram_tensor("convb", [128, NBC], F32, kind="ExternalInput").ap()
    lnwb_d = nc.dram_tensor("lnwb", [128, 2 * NA], F32, kind="ExternalInput").ap()
    normw_d = nc.dram_tensor("normw", [128, NXT], F32, kind="ExternalInput").ap()
    hpar_d = nc.dram_tensor("hpar", [32, 3], F32, kind="ExternalInput").ap()
    ecst_d = nc.dram_tensor("ecst", [32, DI], F32, kind="ExternalInput").ap()
    triu_d = nc.dram_tensor("triu", [64, 64], BF16, kind="ExternalInput").ap()
    ecst2_d = nc.dram_tensor("ecst2", [64, 2048], BF16, kind="ExternalInput").ap()
    idb_d = nc.dram_tensor("identb16", [128, 128], BF16, kind="ExternalInput").ap()
    scr3_d = nc.dram_tensor("scr3", [1, DI], F32, kind="Internal").ap()
    if DEBUG_L0ATTN:
        dbg_cum = nc.dram_tensor("dbg_cum", [32, 512], F32, kind="ExternalOutput").ap()
        dbg_dt = nc.dram_tensor("dbg_dt", [32, 512], F32, kind="ExternalOutput").ap()
        dbg_xc = nc.dram_tensor("dbg_xc", [128, 3 * 512], F32, kind="ExternalOutput").ap()
        dbg_w = nc.dram_tensor("dbg_w", [64, 2048], F32, kind="ExternalOutput").ap()
        dbg_y = nc.dram_tensor("dbg_y", [128, 2048], F32, kind="ExternalOutput").ap()
        dbg_h = nc.dram_tensor("dbg_h", [64, 2048], F32, kind="ExternalOutput").ap()

    with ExitStack() as st:
        sb = lambda n_, s, d: st.enter_context(nc.sbuf_tensor(n_, s, d))
        xcur = sb("xcur", [128, NA * CW], F32)
        xb = sb("xb", [128, NA * CW], F32)
        xn = sb("xn", [128, NA * CW], BF16)
        zbuf = sb("zbuf", [128, 4 * DI], BF16)
        xbcb = sb("xbcb", [128, NBC * (CW + 3)], BF16)
        xcnv = sb("xcnv", [128, NBC * CW], BF16)
        wbuf = sb("wbuf", [128, 8192], BF16)
        ygf = sb("ygf", [128, NXT * CW], BF16)
        dtb = sb("dtb", [32, CW], F32)
        dt2 = sb("dt2", [32, CW], F32)
        cum = sb("cum", [32, CW], F32)
        e2s = sb("e2s", [32, CW], F32)
        msk = sb("msk", [32, CW], BF16)
        hpar = sb("hpar_sb", [32, 3], F32)
        acol = sb("acol", [32, 1], F32)
        ecst = sb("ecst_sb", [32, DI], F32)
        triu = sb("triu_sb", [64, 64], BF16)
        ecst2 = sb("ecst2_sb", [64, 2048], BF16)
        dt3 = sb("dt3", [32, 64], F32)
        idb = sb("idb_sb", [128, 128], BF16)
        gms = sb("gms", [64, 64], BF16)
        cvb = sb("cvb", [64, 64], BF16)
        xtr = sb("xtr", [64, DI], BF16)
        xtp = sb("xtp", [64, DI], BF16)
        hb16 = sb("hb16", [64, DI], BF16)
        h32 = sb("h32", [64, DI], F32)
        cdx = sb("cdx", [64, DI], F32)
        ypair = sb("ypair", [128, DI], F32)
        ztmp = sb("ztmp", [128, DI], BF16)
        dskb = sb("dskb", [64, DI], BF16)
        rsq = sb("rsq", [128, 1], F32)
        rsq2 = sb("rsq2", [128, 1], F32)
        rsq3 = None
        convw = sb("convw_sb", [128, NBC * 4], F32)
        convb = sb("convb_sb", [128, NBC], F32)
        lnwb = sb("lnwb_sb", [128, 2 * NA], F32)
        normw = sb("normw_sb", [128, NXT], F32)
        sqt = [sb("sqt0", [128, CW], F32)] * 2
        SQK = ["sqt0", "sqt0"]
        rows = sb("rows", [1, 4 * 512], F32)
        rows2 = sb("rows2", [1, 2 * 512], F32)
        rA = lambda i, W: rows[0:1, i * 512:i * 512 + W]
        rB = lambda i, W: rows2[0:1, (0 if i == 0 else 1) * 512:(0 if i == 0 else 1) * 512 + W]
        sc_b = sb("sc_b", [128, 512], F32)
        iv_b = sb("iv_b", [128, 512], F32)
        ones1 = sb("ones1", [1, 128], F32)
        onesc = sb("onesc", [128, 1], F32)

        psb = lambda n_, s, d=F32: st.enter_context(nc.psum_tensor(n_, s, d))
        ps_y = psb("ps_y", [64, DI])
        ps_st = psb("ps_st", [128, 512])
        ps_bc = psb("ps_bc", [128, 512])
        ps_a = psb("ps_a", [128, 512])
        ps_tr = psb("ps_tr", [128, 128], BF16)

        p.tv("vector", "memset", ones1[:], 1.0, writes=("ones1",))
        p.tv("vector", "memset", onesc[:], 1.0, writes=("onesc",))
        p.dma("sync", hpar[:], hpar_d, writes=("hpar",))
        p.dma("sync", ecst[:], ecst_d, writes=("ecst",))
        p.dma("sync", triu[:], triu_d, writes=("triu",))
        p.dma("sync", ecst2[:], ecst2_d, writes=("ecst2",))
        p.dma("sync", idb[:], idb_d, writes=("idb",))
        p.dma("sync", convw[:], convw_d, writes=("convw",))
        p.dma("sync", convb[:], convb_d, writes=("convb",))
        p.dma("sync", lnwb[:], lnwb_d, writes=("lnwb",))
        p.dma("sync", normw[:], normw_d, writes=("normw",))
        p.dma("sync", xo_d, x_d, writes=("xo_init",))
        p.tv("scalar", "activation", acol[:], hpar[:, 1:2], AF.Exp,
             reads=("hpar",), writes=("acol",))
        p.tv("vector", "tensor_scalar", acol[:], acol[:], -1.0, None, ALU.mult,
             reads=("acol",), writes=("acol",))
        p.tv("vector", "memset", msk[:], 1.0, writes=("msk",))
        for j in range(8):
            p.tv("vector", "memset", msk[:, j * SC:j * SC + 1], 0.0, writes=("msk",))
        for j in range(4):
            p.emit("tensor", lambda e, j=j: e.matmul(
                ps_a[0:1, 0:512], hpar[:, 2:3], ecst[:, j * 512:(j + 1) * 512],
                start=True, stop=True), reads=("hpar", "ecst"), writes=("ps_a",))
            p.tv("vector", "tensor_copy", rA(j, 512), ps_a[0:1, 0:512],
                 reads=("ps_a",), writes=(f"rA{j}",))
        p.dma("sync", scr3_d, rows[0:1, 0:DI], reads=("rA0", "rA1", "rA2", "rA3"),
              writes=("scr3",))
        p.dma("gpsimd", dskb[:], scr3_d.partition_broadcast(64), reads=("scr3",),
              writes=("dskb",))

        lnw = lambda a: lnwb[:, a:a + 1]
        lnb = lambda a: lnwb[:, NA + a:NA + a + 1]

        def stats_seq(src, skeys, W):
            n = len(skeys)
            for a in range(n):
                sq, sk = sqt[a % 2], "sqt0"
                p.tv("scalar", "activation", sq[:, 0:W], src[:, a * W:(a + 1) * W],
                     AF.Square, reads=(skeys[a],), writes=(sk,))
                p.emit("tensor", lambda e, a=a, W=W, src=src, n=n: e.matmul(
                    ps_st[0:1, 0:W], onesc[:, 0:1], src[:, a * W:(a + 1) * W],
                    start=(a == 0), stop=(a == n - 1)),
                    reads=(skeys[a], "onesc"), writes=("ps_st",))
                p.emit("tensor", lambda e, a=a, W=W, sq=sq, n=n: e.matmul(
                    ps_st[32:33, 0:W], onesc[:, 0:1], sq[:, 0:W],
                    start=(a == 0), stop=(a == n - 1)),
                    reads=(sk, "onesc"), writes=("ps_st",))

        def ln_full(src, skeys, dst, dkeys, W, eps, gain=None, bias=None):
            stats_seq(src, skeys, W)
            rstd, mur = ln_rows(p, rA, rB, W, ps_st, DM, eps)
            bcast_row(p, rstd, "rA2", W, ps_bc, sc_b, "sc_b", ones1)
            bcast_row(p, mur, "rA3", W, ps_bc, iv_b, "iv_b", ones1)
            for a in range(NA):
                sq, sk = sqt[a % 2], "sqt0"
                p.tv("vector", "tensor_mul", sq[:, 0:W], src[:, a * W:(a + 1) * W],
                     sc_b[:, 0:W], reads=(skeys[a], "sc_b"), writes=(sk,))
                if gain is None:
                    p.tv("vector", "tensor_sub", dst[:, a * W:(a + 1) * W], sq[:, 0:W],
                         iv_b[:, 0:W], reads=(sk, "iv_b"), writes=(dkeys[a],))
                else:
                    p.tv("vector", "tensor_sub", sq[:, 0:W], sq[:, 0:W], iv_b[:, 0:W],
                         reads=(sk, "iv_b"), writes=(sk,))
                    p.tv("vector", "tensor_scalar", dst[:, a * W:(a + 1) * W], sq[:, 0:W],
                         gain(a), bias(a), ALU.mult, ALU.add,
                         reads=(sk, "lnwb"), writes=(dkeys[a],))

        BV = lambda off: xcnv[:, 16 * CW + off:16 * CW + off + SC][0:64, :]
        CV = lambda off: xcnv[:, 16 * CW + off:16 * CW + off + SC][64:128, :]

        for s in range(NSEQ):
            p.tv("vector", "memset", h32[:], 0.0, writes=("h32",))
            p.tv("vector", "memset", hb16[:], 0.0, writes=("hb16",))
            for a in range(NBC):
                p.tv("vector", "memset", xbcb[:, a * (CW + 3):a * (CW + 3) + 3], 0.0,
                     writes=(f"xl{a}",))
            for c in range(4):
                base = c * CW
                for a in range(NA):
                    p.dma("sync", xcur[:, a * CW:(a + 1) * CW],
                          x_d[s * DM + a * 128:s * DM + (a + 1) * 128, base:base + CW],
                          writes=(f"xcur{a}",))
                ln_full(xcur, [f"xcur{a}" for a in range(NA)], xb,
                        [f"xb{a}" for a in range(NA)], CW, 1e-5)
                ln_full(xb, [f"xb{a}" for a in range(NA)], xn,
                        [f"xn{a}" for a in range(NA)], CW, 1e-5, gain=lnw, bias=lnb)
                xnk = tuple(f"xn{a}" for a in range(NA))
                for half, (o0, nob, wdth) in enumerate(((0, 7, 896), (7, 7, 896), (14, 4, 416))):
                    for a in range(NA):
                        p.dma("gpsimd", wbuf[:, a * wdth:a * wdth + wdth],
                              ipT_d[a * 128:(a + 1) * 128,
                                    2048 + o0 * 128:2048 + o0 * 128 + wdth],
                              writes=("wbuf",))
                    for oi in range(nob):
                        ob = o0 + oi
                        if ob >= NBC:
                            wid = 32
                        else:
                            wid = 128
                        if ob > NBC:
                            continue
                        ps, pk = (ps_a, "ps_a") if oi % 2 == 0 else (ps_bc, "ps_bc")
                        fns = [(lambda e, a=a, oi=oi, ps=ps, wid=wid, wdth=wdth: e.matmul(
                            ps[0:wid, 0:CW], wbuf[:, a * wdth + oi * 128:a * wdth + oi * 128 + wid],
                            xn[:, a * CW:(a + 1) * CW], start=(a == 0), stop=(a == NA - 1)))
                            for a in range(NA)]
                        p.group("tensor", fns, reads=("wbuf",) + xnk, writes=(pk,))
                        if ob < NBC:
                            off = ob * (CW + 3)
                            p.tv("vector", "tensor_copy", xbcb[:, off + 3:off + 3 + CW],
                                 ps[:, 0:CW], reads=(pk,), writes=(f"xc{ob}",))
                        else:
                            p.tv("vector", "tensor_scalar", dtb[:, 0:CW], ps[0:32, 0:CW],
                                 1.0, hpar[0:32, 0:1], ALU.mult, ALU.add,
                                 reads=(pk, "hpar"), writes=("dtb",))
                for half in range(2):
                    for a in range(NA):
                        p.dma("gpsimd", wbuf[:, a * 1024:(a + 1) * 1024],
                              ipT_d[a * 128:(a + 1) * 128, half * 1024:(half + 1) * 1024],
                              writes=("wbuf",))
                    for tb in range(4):
                        for zc in range(2):
                            ps, pk = (ps_a, "ps_a") if zc % 2 == 0 else (ps_bc, "ps_bc")
                            fns = [(lambda e, a=a, tb=tb, zc=zc, ps=ps: e.matmul(
                                ps[:, 0:512],
                                xn[:, a * CW + tb * 128:a * CW + (tb + 1) * 128],
                                wbuf[:, a * 1024 + zc * 512:a * 1024 + (zc + 1) * 512],
                                start=(a == 0), stop=(a == NA - 1)))
                                for a in range(NA)]
                            p.group("tensor", fns, reads=("wbuf",) + xnk, writes=(pk,))
                            p.tv("vector", "tensor_copy",
                                 zbuf[:, tb * DI + half * 1024 + zc * 512:
                                      tb * DI + half * 1024 + (zc + 1) * 512],
                                 ps[:, 0:512], reads=(pk,), writes=("ZB",))
                for a in range(NBC):
                    off = a * (CW + 3)
                    sq, sk = sqt[a % 2], "sqt0"
                    p.tv("vector", "tensor_scalar", sq[:, 0:CW], xbcb[:, off:off + CW],
                         convw[:, 4 * a:4 * a + 1], None, ALU.mult,
                         reads=(f"xc{a}", f"xl{a}", "convw"), writes=(sk,))
                    for j in range(1, 4):
                        p.tv("vector", "scalar_tensor_tensor", sq[:, 0:CW],
                             xbcb[:, off + j:off + j + CW], convw[:, 4 * a + j:4 * a + j + 1],
                             sq[:, 0:CW], ALU.mult, ALU.add,
                             reads=(f"xc{a}", f"xl{a}", "convw", sk), writes=(sk,))
                    p.tv("gpsimd", "tensor_copy", xbcb[:, off:off + 3],
                         xbcb[:, off + CW:off + CW + 3], reads=(f"xc{a}",),
                         writes=(f"xl{a}",))
                    p.emit("scalar", lambda e, a=a, sq=sq: e.activation(
                        xcnv[:, a * CW:(a + 1) * CW], sq[:, 0:CW], AF.Silu,
                        bias=convb[:, a:a + 1]), reads=(sk, "convb"), writes=(f"cv{a}",))
                p.tv("scalar", "activation", dt2[:], dtb[:], AF.Abs, reads=("dtb",), writes=("dt2",))
                p.tv("scalar", "activation", dt2[:], dt2[:], AF.Exp, scale=-1.0,
                     reads=("dt2",), writes=("dt2",))
                p.tv("scalar", "activation", dt2[:], dt2[:], AF.Ln, bias=1.0,
                     reads=("dt2",), writes=("dt2",))
                p.tv("vector", "tensor_scalar", dtb[:], dtb[:], 0.0, None, ALU.max,
                     reads=("dtb",), writes=("dtb",))
                p.tv("vector", "tensor_add", dtb[:], dtb[:], dt2[:],
                     reads=("dtb", "dt2"), writes=("dtb",))
                p.tv("vector", "tensor_scalar", dt2[:], dtb[:], acol[0:32, 0:1], None,
                     ALU.mult, reads=("dtb", "acol"), writes=("dt2",))
                p.tv("vector", "tensor_tensor_scan", cum[:], msk[:], dt2[:], 0.0,
                     ALU.mult, ALU.add, reads=("msk", "dt2"), writes=("cum",))
                if DEBUG_L0ATTN and s == 0 and c == 0:
                    p.dma("sync", dbg_cum, cum[:], reads=("cum",), writes=("dbgcum",))
                    p.dma("sync", dbg_dt, dtb[:], reads=("dtb",), writes=("dbgdt",))
                    p.tv("vector", "tensor_copy", sqt[0][:, 0:CW], xcnv[:, 0:CW],
                         reads=("cv0",), writes=("sqt0",))
                    p.dma("sync", dbg_xc[:, 0:512], sqt[0][:, 0:CW], reads=("sqt0",), writes=("dbgxc",))
                    p.tv("vector", "tensor_copy", sqt[0][:, 0:CW], xcnv[:, CW:2 * CW],
                         reads=("cv1",), writes=("sqt0",))
                    p.dma("sync", dbg_xc[:, 512:1024], sqt[0][:, 0:CW], reads=("sqt0",), writes=("dbgxc",))
                    p.tv("vector", "tensor_copy", sqt[0][:, 0:CW], xcnv[:, 16 * CW:17 * CW],
                         reads=("cv16",), writes=("sqt0",))
                    p.dma("sync", dbg_xc[:, 1024:1536], sqt[0][:, 0:CW], reads=("sqt0",), writes=("dbgxc",))
                p.tv("scalar", "activation", e2s[:], cum[:], AF.Exp,
                     reads=("cum",), writes=("e2s",))
                p.tv("scalar", "activation", dt2[:], dtb[:], AF.Ln,
                     reads=("dtb",), writes=("dt2",))
                p.tv("vector", "tensor_sub", dtb[:], dt2[:], cum[:],
                     reads=("dt2", "cum"), writes=("dtb",))
                for cc in range(8):
                    off = cc * SC
                    for tk in range(NXT):
                        p.tv("tensor", "transpose", ps_tr[0:64, 0:128],
                             xcnv[:, tk * CW + off:tk * CW + off + SC], idb[:],
                             reads=(f"cv{tk}", "idb"), writes=("ps_tr",))
                        p.tv("vector", "tensor_copy", xtr[:, tk * 128:(tk + 1) * 128],
                             ps_tr[0:64, 0:128], reads=("ps_tr",), writes=("xtr",))
                    p.tv("scalar", "activation", dt3[:], dtb[0:32, off:off + SC], AF.Exp,
                         bias=cum[0:32, off + SC - 1:off + SC], reads=("dtb", "cum"),
                         writes=("dt3",))
                    for j in range(4):
                        p.emit("tensor", lambda e, j=j: e.matmul(
                            ps_y[:, j * 512:(j + 1) * 512], dt3[:],
                            ecst[:, j * 512:(j + 1) * 512], start=True, stop=True),
                            reads=("dt3", "ecst"), writes=("ps_y",))
                    p.tv("vector", "tensor_mul", xtp[:], xtr[:], ps_y[0:64, :],
                         reads=("xtr", "ps_y"), writes=("xtp",))
                    p.dma("sync", cvb[:], CV(off), reads=("cv16",), writes=("cvb",))
                    p.emit("tensor", lambda e, off=off: e.matmul(
                        ps_a[0:64, 0:64], cvb[:], BV(off), start=True, stop=True),
                        reads=("cv16", "cvb"), writes=("ps_a",))
                    p.tv("vector", "tensor_tensor", gms[:], ps_a[0:64, 0:64], triu[:],
                         ALU.mult, reads=("ps_a", "triu"), writes=("gms",))
                    for j in range(4):
                        p.emit("tensor", lambda e, j=j, off=off: e.matmul(
                            ps_a[0:64, 0:512], e2s[0:32, off:off + SC],
                            ecst[:, j * 512:(j + 1) * 512], start=True, stop=True),
                            reads=("e2s", "ecst"), writes=("ps_a",))
                        p.tv("vector", "tensor_copy", ztmp[0:64, j * 512:(j + 1) * 512],
                             ps_a[0:64, 0:512], reads=("ps_a",), writes=("ztmp",))
                    for j in range(4):
                        p.emit("tensor", lambda e, j=j: e.matmul(
                            ps_y[:, j * 512:(j + 1) * 512], cvb[:],
                            hb16[:, j * 512:(j + 1) * 512], start=True, stop=True),
                            reads=("cvb", "hb16"), writes=("ps_y",))
                    half = 64 * (cc % 2)
                    p.tv("vector", "tensor_mul", ypair[half:half + 64, :], ps_y[0:64, :],
                         ztmp[0:64, :], reads=("ps_y", "ztmp"), writes=("ypair",))
                    p.dma("sync", scr3_d.rearrange("a (h l) -> (a h) l", l=SC),
                          cum[0:32, off:off + SC], reads=("cum",), writes=("scr3",))
                    p.dma("sync", rows[0:1, 0:DI], scr3_d[0:1, 0:DI],
                          reads=("scr3",), writes=("rA0", "rA1", "rA2", "rA3"))
                    for j in range(4):
                        p.emit("tensor", lambda e, j=j: e.matmul(
                            ps_y[0:64, j * 512:(j + 1) * 512], ones1[0:1, 0:64],
                            rows[0:1, j * 512:(j + 1) * 512], start=True, stop=False),
                            reads=("rA0", "rA1", "rA2", "rA3", "ones1"), writes=("ps_y",))
                        p.emit("tensor", lambda e, j=j, off=off: e.matmul(
                            ps_y[0:64, j * 512:(j + 1) * 512], dtb[0:32, off:off + SC],
                            ecst[:, j * 512:(j + 1) * 512], start=False, stop=True),
                            reads=("dtb", "ecst"), writes=("ps_y",))
                    p.tv("vector", "tensor_scalar", cdx[:], ps_y[0:64, :], 80.0, None,
                         ALU.min, reads=("ps_y",), writes=("cdx",))
                    p.tv("scalar", "activation", ztmp[0:64, :], cdx[:], AF.Exp,
                         reads=("cdx",), writes=("ztmp",))
                    for j in range(4):
                        p.emit("tensor", lambda e, j=j: e.matmul(
                            ps_y[0:64, j * 512:(j + 1) * 512], gms[:],
                            ecst2[:, j * 512:(j + 1) * 512], start=True, stop=True),
                            reads=("gms", "ecst2"), writes=("ps_y",))
                    p.tv("vector", "tensor_mul", ztmp[0:64, :], ztmp[0:64, :],
                         ps_y[0:64, :], reads=("ztmp", "ps_y"), writes=("ztmp",))
                    for h in range(32):
                        p.emit("tensor", lambda e, h=h: e.matmul(
                            ps_y[:, h * SC:(h + 1) * SC], ztmp[0:64, h * SC:(h + 1) * SC],
                            xtr[:, h * SC:(h + 1) * SC], start=True, stop=True),
                            reads=("ztmp", "xtr"), writes=("ps_y",))
                    p.tv("vector", "tensor_add", ypair[half:half + 64, :],
                         ypair[half:half + 64, :], ps_y[0:64, :],
                         reads=("ypair", "ps_y"), writes=("ypair",))
                    p.tv("vector", "tensor_mul", ztmp[half:half + 64, :], xtr[:], dskb[:],
                         reads=("xtr", "dskb"), writes=("ztmp",))
                    p.tv("vector", "tensor_add", ypair[half:half + 64, :],
                         ypair[half:half + 64, :], ztmp[half:half + 64, :],
                         reads=("ypair", "ztmp"), writes=("ypair",))
                    p.tv("tensor", "transpose", ps_tr[0:64, 0:64], BV(off),
                         idb[0:64, 0:64], reads=("cv16", "idb"), writes=("ps_tr",))
                    p.tv("vector", "tensor_copy", gms[:], ps_tr[0:64, 0:64],
                         reads=("ps_tr",), writes=("gms",))
                    for j in range(4):
                        p.emit("tensor", lambda e, j=j: e.matmul(
                            ps_y[:, j * 512:(j + 1) * 512], gms[:],
                            xtp[:, j * 512:(j + 1) * 512], start=True, stop=True),
                            reads=("gms", "xtp"), writes=("ps_y",))
                    for j in range(4):
                        p.emit("tensor", lambda e, j=j, off=off: e.matmul(
                            ps_a[0:1, 0:512], e2s[0:32, off + SC - 1:off + SC],
                            ecst[:, j * 512:(j + 1) * 512], start=True, stop=True),
                            reads=("e2s", "ecst"), writes=("ps_a",))
                        p.tv("vector", "tensor_copy", rA(j, 512), ps_a[0:1, 0:512],
                             reads=("ps_a",), writes=(f"rA{j}",))
                    p.dma("sync", scr3_d, rows[0:1, 0:DI],
                          reads=("rA0", "rA1", "rA2", "rA3"), writes=("scr3",))
                    p.dma("gpsimd", cdx[:], scr3_d.partition_broadcast(64),
                          reads=("scr3",), writes=("cdx",))
                    p.tv("vector", "tensor_mul", h32[:], h32[:], cdx[:],
                         reads=("h32", "cdx"), writes=("h32",))
                    p.tv("vector", "tensor_add", h32[:], h32[:], ps_y[0:64, :],
                         reads=("h32", "ps_y"), writes=("h32",))
                    p.tv("vector", "tensor_copy", hb16[:], h32[:], reads=("h32",),
                         writes=("hb16",))
                    if DEBUG_L0ATTN and s == 0 and c == 0 and cc == 1:
                        p.dma("sync", dbg_y, ypair[:], reads=("ypair",), writes=("dbgy",))
                        p.tv("vector", "tensor_copy", cdx[:], h32[:],
                             reads=("h32",), writes=("cdx",))
                        p.dma("sync", dbg_h, cdx[:], reads=("cdx",), writes=("dbgh",))
                    if cc % 2 == 1:
                        pair = cc // 2
                        zp = zbuf[:, pair * DI:(pair + 1) * DI]
                        p.tv("scalar", "activation", ztmp[:], zp, AF.Silu,
                             reads=("ZB",), writes=("ztmp",))
                        p.tv("vector", "tensor_mul", ypair[:], ypair[:], ztmp[:],
                             reads=("ypair", "ztmp"), writes=("ypair",))
                        p.tv("vector", "scalar_tensor_tensor", ztmp[:], ypair[:], 1.0,
                             ypair[:], ALU.mult, ALU.mult, accum_out=rsq[:],
                             reads=("ypair",), writes=("ztmp", "rsq"))
                        p.tv("vector", "tensor_scalar", rsq2[:], rsq[:], 1.0 / DI, 1e-5,
                             ALU.mult, ALU.add, reads=("rsq",), writes=("rsq2",))
                        p.tv("vector", "reciprocal", rsq[:], rsq2[:], reads=("rsq2",),
                             writes=("rsq",))
                        p.tv("scalar", "activation", rsq[:], rsq[:], AF.Sqrt,
                             reads=("rsq",), writes=("rsq",))
                        nt_ = sc_b[:, 0:1]
                        for _ in range(2):
                            p.tv("vector", "tensor_mul", nt_, rsq[:], rsq[:],
                                 reads=("rsq",), writes=("sc_b",))
                            p.tv("vector", "tensor_mul", nt_, rsq2[:], nt_,
                                 reads=("rsq2", "sc_b"), writes=("sc_b",))
                            p.tv("vector", "tensor_scalar", nt_, nt_, -0.5, 1.5,
                                 ALU.mult, ALU.add, reads=("sc_b",), writes=("sc_b",))
                            p.tv("vector", "tensor_mul", rsq[:], rsq[:], nt_,
                                 reads=("rsq", "sc_b"), writes=("rsq",))
                        p.tv("vector", "tensor_scalar", ztmp[:], ypair[:], rsq[:, 0:1],
                             None, ALU.mult, reads=("ypair", "rsq"), writes=("ztmp",))
                        for tk in range(NXT):
                            p.tv("tensor", "transpose", ps_tr[:, 0:128],
                                 ztmp[:, tk * 128:(tk + 1) * 128], idb[:],
                                 reads=("ztmp", "idb"), writes=("ps_tr",))
                            p.tv("vector", "tensor_copy",
                                 ygf[:, tk * CW + pair * 128:tk * CW + (pair + 1) * 128],
                                 ps_tr[:, 0:128], reads=("ps_tr",), writes=(f"yg{tk}",))
                for tk in range(NXT):
                    p.tv("vector", "tensor_scalar", ygf[:, tk * CW:(tk + 1) * CW],
                         ygf[:, tk * CW:(tk + 1) * CW], normw[:, tk:tk + 1], None,
                         ALU.mult, reads=(f"yg{tk}", "normw"), writes=(f"yg{tk}",))
                for a in range(8):
                    p.dma("gpsimd", wbuf[:, a * 1024:(a + 1) * 1024],
                          opT_d[a * 128:(a + 1) * 128, :], writes=("wbuf",))
                for a in range(8):
                    p.dma("gpsimd", zbuf[:, a * 1024:(a + 1) * 1024],
                          opT_d[(8 + a) * 128:(9 + a) * 128, :], writes=("ZB",))
                for ob in range(NA):
                    ps, pk = (ps_a, "ps_a") if ob % 2 == 0 else (ps_bc, "ps_bc")
                    fns = [(lambda e, k=k, ob=ob, ps=ps: e.matmul(
                        ps[:, 0:CW],
                        (wbuf if k < 8 else zbuf)[:, (k % 8) * 1024 + ob * 128:
                                                  (k % 8) * 1024 + (ob + 1) * 128],
                        ygf[:, k * CW:(k + 1) * CW], start=(k == 0), stop=(k == 15)))
                        for k in range(16)]
                    p.group("tensor", fns,
                            reads=("wbuf", "ZB") + tuple(f"yg{k}" for k in range(16)),
                            writes=(pk,))
                    sq, sk = sqt[ob % 2], "sqt0"
                    p.tv("vector", "tensor_add", sq[:, 0:CW], ps[:, 0:CW],
                         xb[:, ob * CW:(ob + 1) * CW], reads=(pk, f"xb{ob}"), writes=(sk,))
                    p.dma("gpsimd", xo_d[s * DM + ob * 128:s * DM + (ob + 1) * 128,
                                         base:base + CW],
                          sq[:, 0:CW], reads=(sk, "xo_init"),
                          writes=(f"xoF{s}_{c}_{ob}",), accum_op=ALU.add)
    return p.finalize()


def run_l0attn(inputs, x_current, cores=8):
    if "l0attn" not in _cache:
        _cache["l0attn"] = build_l0attn()
    nc = _cache["l0attn"]
    f32 = lambda v: np.asarray(v, dtype=np.float32)
    NBC = 17
    convw = f32(inputs["L0_attn_conv_w"]).reshape(2176, 4).reshape(NBC, 128, 4) \
        .transpose(1, 0, 2).reshape(128, NBC * 4)
    convb = f32(inputs["L0_attn_conv_b"]).reshape(NBC, 128).T
    lnwb = np.concatenate([f32(inputs["L0_attn_ln_w"]).reshape(8, 128).T,
                           f32(inputs["L0_attn_ln_b"]).reshape(8, 128).T], axis=1)
    normw = f32(inputs["L0_attn_norm_w"]).reshape(16, 128).T
    hpar = np.stack([f32(inputs["L0_attn_dt_bias"]), f32(inputs["L0_attn_A_log"]),
                     f32(inputs["L0_attn_D_skip"])], axis=1)
    ecst = np.zeros((32, 2048), dtype=np.float32)
    for h in range(32):
        ecst[h, h * 64:(h + 1) * 64] = 1.0
    triu = np.tril(np.ones((64, 64))).astype(ml_dtypes.bfloat16)
    ecst2 = np.tile(np.eye(64), (1, 32)).astype(ml_dtypes.bfloat16)
    w = {
        "ipT": np.ascontiguousarray(f32(inputs["L0_attn_in_proj_w"]).T),
        "opT": np.ascontiguousarray(f32(inputs["L0_attn_out_proj_w"]).T),
        "convw": np.ascontiguousarray(convw),
        "convb": np.ascontiguousarray(convb),
        "lnwb": np.ascontiguousarray(lnwb),
        "normw": np.ascontiguousarray(normw),
        "hpar": np.ascontiguousarray(hpar),
        "ecst": ecst,
        "triu": triu,
        "ecst2": np.ascontiguousarray(ecst2),
        "identb16": np.eye(128, dtype=ml_dtypes.bfloat16),
    }
    in_maps = [dict(w, x_res=_shard_x(x_current, c)) for c in range(cores)]
    res = run_bass_kernel_spmd(nc, in_maps, core_ids=list(range(cores)))
    return _unshard([r for r in res.results])


# version 03
# speedup vs baseline: 1.0013x; 1.0013x over previous
import numpy as np
import ml_dtypes
from contextlib import ExitStack

import concourse.bass as bass
import concourse.mybir as mybir
from concourse.bass_utils import run_bass_kernel_spmd

F32 = mybir.dt.float32
BF16 = mybir.dt.bfloat16
AF = mybir.ActivationFunctionType
ALU = mybir.AluOpType
AX = mybir.AxisListType
RND = 12582912.0

ENGINES = ("tensor", "vector", "scalar", "gpsimd", "sync")

NSEQ = 2
T = 2048
DM = 1024
NA = DM // 128


class Prog:

    NDMASEM = 12

    def __init__(self, name="prog"):
        self.nc = bass.Bass("TRN2", target_bir_lowering=False)
        self.name = name
        self.ops = {e: [] for e in ENGINES}
        self.dmasems = [f"dma{i}" for i in range(self.NDMASEM)]
        self.allsems = list(ENGINES) + self.dmasems
        self.cnt = {s: 0 for s in self.allsems}
        self.watermark = {e: {s: 0 for s in self.allsems} for e in ENGINES}
        self.wr = {}
        self.rd = {}
        self._dma_rr = 0
        self.log = []

    def _edges(self, eng, reads, writes, mysem):
        need = {}
        for k in reads:
            w = self.wr.get(k)
            if w:
                need[w[0]] = max(need.get(w[0], 0), w[1])
        for k in writes:
            w = self.wr.get(k)
            if w and w[0] != mysem:
                need[w[0]] = max(need.get(w[0], 0), w[1])
            for sm, rv in self.rd.get(k, {}).items():
                if sm != mysem:
                    need[sm] = max(need.get(sm, 0), rv)
        waits = []
        for ts_, val in need.items():
            if ts_ == mysem and val >= self.cnt.get(mysem, 0) + 1:
                continue
            if val > self.watermark[eng][ts_]:
                waits.append((ts_, val))
                self.watermark[eng][ts_] = val
        return waits

    def _post(self, sem, reads, writes):
        v = self.cnt[sem]
        for k in reads:
            self.rd.setdefault(k, {})[sem] = v
        for k in writes:
            self.wr[k] = (sem, v)
            self.rd[k] = {}

    def emit(self, eng, fn, reads=(), writes=(), step=1):
        waits = self._edges(eng, reads, writes, eng)
        self.cnt[eng] += step
        self._post(eng, reads, writes)
        self.log.append((eng, self.cnt[eng], tuple(waits), tuple(reads), tuple(writes)))

        def run(e, sems, _w=waits, _f=fn, _s=step, _eng=eng):
            for ts_, val in _w:
                e.wait_ge(sems[ts_], val)
            _f(e).then_inc(sems[_eng], _s)
        self.ops[eng].append(run)

    def group(self, eng, fns, reads=(), writes=()):
        waits = self._edges(eng, reads, writes, eng)
        self.cnt[eng] += 1
        self._post(eng, reads, writes)
        self.log.append((eng, self.cnt[eng], tuple(waits), tuple(reads), tuple(writes), "grp"))

        def run(e, sems, _w=waits, _fns=fns, _eng=eng):
            for ts_, val in _w:
                e.wait_ge(sems[ts_], val)
            inst = None
            for fn in _fns:
                inst = fn(e)
            inst.then_inc(sems[_eng], 1)
        self.ops[eng].append(run)

    def barrier(self):
        snap = dict(self.cnt)
        for e in ENGINES:
            for ts_ in self.allsems:
                if ts_ != e and snap[ts_] > self.watermark[e][ts_]:
                    self.watermark[e][ts_] = snap[ts_]

            def run(eng, sems, _snap=snap, _e=e):
                for ts_ in list(ENGINES) + [s for s in _snap if s.startswith("dma")]:
                    if ts_ != _e and _snap.get(ts_, 0) > 0:
                        eng.wait_ge(sems[ts_], _snap[ts_])
            self.ops[e].append(run)

    def dma(self, queue, out, in_, reads=(), writes=(), **kw):
        sem = self.dmasems[self._dma_rr]
        self._dma_rr = (self._dma_rr + 1) % self.NDMASEM
        waits = self._edges(queue, reads, writes, sem)
        prior = self.cnt[sem]
        if prior > self.watermark[queue][sem]:
            waits.append((sem, prior))
            self.watermark[queue][sem] = prior
        self.cnt[sem] = prior + 16
        self._post(sem, reads, writes)
        self.log.append((queue, sem, self.cnt[sem], tuple(waits), tuple(reads), tuple(writes)))

        def run(e, sems, _w=waits, _sem=sem):
            for ts_, val in _w:
                e.wait_ge(sems[ts_], val)
            e.dma_start(out=out, in_=in_, **kw).then_inc(sems[_sem], 16)
        self.ops[queue].append(run)

    def tv(self, eng, name, *args, reads=(), writes=(), **kw):
        self.emit(eng, lambda e: getattr(e, name)(*args, **kw),
                  reads=reads, writes=writes)

    def finalize(self):
        nc = self.nc
        with ExitStack() as st:
            sems = {s: st.enter_context(nc.semaphore(f"S_{s}")) for s in self.allsems}
            block = st.enter_context(nc.Block())
            final = dict(self.cnt)
            oplists = self.ops

            def make(engname):
                def body(e):
                    for f in oplists[engname]:
                        f(e, sems)
                    if engname in ("sync", "gpsimd"):
                        for ts_ in self.allsems:
                            if final[ts_] > 0:
                                e.wait_ge(sems[ts_], final[ts_])
                return body
            block.tensor(make("tensor"))
            block.vector(make("vector"))
            block.scalar(make("scalar"))
            block.gpsimd(make("gpsimd"))
            block.sync(make("sync"))
        return nc



def stat_sums(p, src_tiles, W, ps_st, sq, sqkeys, onesc):
    nt = len(src_tiles)
    for i, (ap_, key) in enumerate(src_tiles):
        p.tv("scalar", "activation", sq[:, i * W:(i + 1) * W], ap_, AF.Square,
             reads=(key,), writes=(sqkeys[i],))
    fns = [(lambda e, i=i, ap_=ap_: e.matmul(ps_st[0:1, 0:W], onesc[:, 0:1], ap_,
                                             start=(i == 0), stop=(i == nt - 1)))
           for i, (ap_, key) in enumerate(src_tiles)]
    p.group("tensor", fns, reads=tuple(k for _, k in src_tiles) + ("onesc",),
            writes=("ps_st",))
    fns = [(lambda e, i=i: e.matmul(ps_st[32:33, 0:W], onesc[:, 0:1], sq[:, i * W:(i + 1) * W],
                                    start=(i == 0), stop=(i == nt - 1)))
           for i in range(nt)]
    p.group("tensor", fns, reads=tuple(sqkeys) + ("onesc",), writes=("ps_st",))


def ln_rows(p, rA, rB, W, ps_st, d_model, eps):
    mu, ve, rstd, mur = rA(0, W), rA(1, W), rA(2, W), rA(3, W)
    t0 = rB(0, W)
    p.tv("vector", "tensor_scalar", mu, ps_st[0:1, 0:W], 1.0 / d_model, None, ALU.mult,
         reads=("ps_st",), writes=("rA0",))
    p.tv("vector", "tensor_scalar", ve, ps_st[32:33, 0:W], 1.0 / d_model, eps, ALU.mult, ALU.add,
         reads=("ps_st",), writes=("rA1",))
    p.tv("vector", "tensor_mul", t0, mu, mu, reads=("rA0",), writes=("rB0",))
    p.tv("vector", "tensor_sub", ve, ve, t0, reads=("rA1", "rB0"), writes=("rA1",))
    p.tv("vector", "reciprocal", t0, ve, reads=("rA1",), writes=("rB0",))
    p.tv("scalar", "activation", rstd, t0, AF.Sqrt, reads=("rB0",), writes=("rA2",))
    t3 = rB(3, W)
    for _ in range(2):
        p.tv("vector", "tensor_mul", t3, rstd, rstd, reads=("rA2",), writes=("rB3",))
        p.tv("vector", "tensor_mul", t3, ve, t3, reads=("rA1", "rB3"), writes=("rB3",))
        p.tv("vector", "tensor_scalar", t3, t3, -0.5, 1.5, ALU.mult, ALU.add,
             reads=("rB3",), writes=("rB3",))
        p.tv("vector", "tensor_mul", rstd, rstd, t3, reads=("rA2", "rB3"), writes=("rA2",))
    p.tv("vector", "tensor_mul", mur, mu, rstd, reads=("rA0", "rA2"), writes=("rA3",))
    return rstd, mur


def bcast_row(p, row_ap, row_key, W, ps_bc, dst, dst_key, ones1):
    p.tv("tensor", "matmul", ps_bc[:, 0:W], ones1[0:1, :], row_ap, start=True, stop=True,
         reads=(row_key, "ones1"), writes=("ps_bc",))
    p.tv("scalar", "copy", dst[:, 0:W], ps_bc[:, 0:W], reads=("ps_bc",), writes=(dst_key,))


def amax_rows(p, cmb, cmb_key, W, ps_st, amax_tm, idf, scr_d, srow, bk):
    if W == 1:
        p.tv("tensor", "transpose", ps_st[0:1, 0:128], cmb[:, 0:1], idf[:],
             reads=(cmb_key, "idf"), writes=("ps_st",))
        p.tv("vector", "reduce_max", srow[0:1, 0:1], ps_st[0:1, 0:128], AX.X,
             reads=("ps_st",), writes=(bk + "_row",))
        return
    nblk = (W + 127) // 128
    for b in range(nblk):
        bw = min(128, W - b * 128)
        p.tv("tensor", "transpose", ps_st[0:bw, b * 128:b * 128 + 128],
             cmb[:, b * 128:b * 128 + bw], idf[:],
             reads=(cmb_key, "idf"), writes=("ps_st",))
        p.tv("vector", "reduce_max", amax_tm[0:bw, b:b + 1], ps_st[0:bw, b * 128:b * 128 + 128],
             AX.X, reads=("ps_st",), writes=(bk + f"_tm{b}",))
        p.dma("sync", scr_d.rearrange("a b -> b a")[b * 128:b * 128 + bw, :],
              amax_tm[0:bw, b:b + 1], reads=(bk + f"_tm{b}",), writes=(bk + f"_scr{b}",))
    p.dma("sync", srow[0:1, 0:W], scr_d[0:1, 0:W],
          reads=tuple(bk + f"_scr{b}" for b in range(nblk)), writes=(bk + "_row",))


def quant_rows(p, srow, srow_key, rA, rB, m_ap, m_key, W):
    p.tv("vector", "tensor_scalar", rB(1, W), srow[0:1, 0:W], 1e-5, None,
         ALU.max, reads=(srow_key,), writes=("rB1",))
    p.tv("vector", "reciprocal", rB(2, W), rB(1, W), reads=("rB1",), writes=("rB2",))
    p.tv("vector", "tensor_scalar", rA(0, W), rB(2, W), 127.0, None,
         ALU.mult, reads=("rB2",), writes=("rA0",))
    p.tv("vector", "tensor_scalar", rA(1, W), rB(1, W), 1.0 / 127.0, None,
         ALU.mult, reads=("rB1",), writes=("rA1",))
    p.tv("vector", "tensor_scalar", rA(1, W), rA(1, W), m_ap, None,
         ALU.mult, reads=("rA1", m_key), writes=("rA1",))



DEBUG_L0FFN = False


def emit_l0ffn(p, x_d, xo_d, pfx="f_"):
    nc = p.nc
    DH = 4096
    CH = 512
    NH = DH // 128
    AR_W = 516

    w1T_d = nc.dram_tensor(pfx + "w1T", [DM, DH], F32, kind="ExternalInput").ap()
    w2T_d = nc.dram_tensor(pfx + "w2T", [DH, DM], F32, kind="ExternalInput").ap()
    convw_d = nc.dram_tensor(pfx + "convw", [128, 3 * DH // 128], F32, kind="ExternalInput").ap()
    convb_d = nc.dram_tensor(pfx + "convb", [128, DH // 128], F32, kind="ExternalInput").ap()
    idf_d = nc.dram_tensor(pfx + "identf32", [128, 128], F32, kind="ExternalInput").ap()
    if DEBUG_L0FFN:
        dbg_xn = nc.dram_tensor(pfx + "dbg_xn", [128, 8 * 512], F32, kind="ExternalOutput").ap()
        dbg_xq = nc.dram_tensor(pfx + "dbg_xq", [128, 8 * 512], F32, kind="ExternalOutput").ap()
        dbg_h = nc.dram_tensor(pfx + "dbg_h", [128, 4 * 516], F32, kind="ExternalOutput").ap()
        dbg_hs = nc.dram_tensor(pfx + "dbg_hs", [128, 4 * 516], F32, kind="ExternalOutput").ap()
        dbg_ra = nc.dram_tensor(pfx + "dbg_ra", [1, 2048], F32, kind="ExternalOutput").ap()
        dbg_rb = nc.dram_tensor(pfx + "dbg_rb", [1, 2048], F32, kind="ExternalOutput").ap()
        dbg_rows = nc.dram_tensor(pfx + "dbg_rows", [1, 4 * 640], F32, kind="ExternalOutput").ap()
        dbg_sq = nc.dram_tensor(pfx + "dbg_sq", [128, 4096], F32, kind="ExternalOutput").ap()
        dbg_x0 = nc.dram_tensor(pfx + "dbg_x0", [128, 512], F32, kind="ExternalOutput").ap()
        dbg_sl1 = nc.dram_tensor(pfx + "dbg_sl1", [128, 4 * 516], F32, kind="ExternalOutput").ap()
        dbg_wq = nc.dram_tensor(pfx + "dbg_wq", [128, DH], BF16, kind="ExternalOutput").ap()
        dbg_m = nc.dram_tensor(pfx + "dbg_m", [1, 4], F32, kind="ExternalOutput").ap()
        dbg_wraw = nc.dram_tensor(pfx + "dbg_wraw", [128, DH], F32, kind="ExternalOutput").ap()
        dbg_wrnd = nc.dram_tensor(pfx + "dbg_wrnd", [128, DH], F32, kind="ExternalOutput").ap()
        dbg_swc = nc.dram_tensor(pfx + "dbg_swc", [128, 1], F32, kind="ExternalOutput").ap()
        dbg_m2 = nc.dram_tensor(pfx + "dbg_m2", [1, 4], F32, kind="ExternalOutput").ap()
        dbg_pbc = nc.dram_tensor(pfx + "dbg_pbc", [128, 1], F32, kind="ExternalOutput").ap()
        dbg_mini = nc.dram_tensor(pfx + "dbg_mini", [1, 16], F32, kind="ExternalOutput").ap()
        dbg_mhs = nc.dram_tensor(pfx + "dbg_mhs", [128, 32], F32, kind="ExternalOutput").ap()
        dbg_wq2 = nc.dram_tensor(pfx + "dbg_wq2", [128, DM], BF16, kind="ExternalOutput").ap()
        dbg_q2 = nc.dram_tensor(pfx + "dbg_q2", [128, 640], BF16, kind="ExternalOutput").ap()
        dbg_q5 = nc.dram_tensor(pfx + "dbg_q5", [128, 640], BF16, kind="ExternalOutput").ap()
        dbg_ob = nc.dram_tensor(pfx + "dbg_ob", [128, 640], F32, kind="ExternalOutput").ap()
        dbg_ps = nc.dram_tensor(pfx + "dbg_ps", [128, 640], F32, kind="ExternalOutput").ap()
        dbg_ivb = nc.dram_tensor(pfx + "dbg_ivb", [128, 640], F32, kind="ExternalOutput").ap()
    wq1_d = nc.dram_tensor(pfx + "wq1", [DM, DH], BF16, kind="Internal").ap()
    wq2_d = nc.dram_tensor(pfx + "wq2", [DH, DM], BF16, kind="Internal").ap()
    scr_d = nc.dram_tensor(pfx + "scr", [1, 2048], F32, kind="Internal").ap()

    with ExitStack() as st:
        sb = lambda n_, s, d: st.enter_context(nc.sbuf_tensor(pfx + n_, s, d))
        psb = lambda n_, s: st.enter_context(nc.psum_tensor(pfx + n_, s, F32))

        xcur = sb("xcur", [128, NA * CH], F32)
        S1 = sb("S1", [128, NA * CH], F32)
        xq = sb("xq", [128, NA * CH], BF16)
        wbuf = sb("wbuf", [128, 16384], BF16)
        aren = sb("aren", [128, 33 * AR_W], F32)
        cmbA = sb("cmbA", [128, 640], F32)
        ctmpA = sb("ctmpA", [128, 640], F32)
        ctmpB = sb("ctmpB", [128, 640], F32)
        rows = sb("rows", [1, 4 * 640], F32)
        rows2 = sb("rows2", [1, 4 * 640], F32)
        rA = lambda i, W: rows[0:1, i * 640:i * 640 + W]
        rB = lambda i, W: rows2[0:1, i * 640:i * 640 + W]
        amax_tm = sb("amax_tm", [128, 8], F32)
        sc_b = sb("sc_b", [128, 640], F32)
        iv_b = sb("iv_b", [128, 640], F32)
        qtmp = [sb(f"qtmp{i}", [128, 640], BF16) for i in range(2)]
        qf32 = [sb("qf32_0", [128, 640], F32)] * 2
        obuf = [sb(f"obuf{i}", [128, 640], F32) for i in range(2)]
        ones1 = sb("ones1", [1, 128], F32)
        onesc = sb("onesc", [128, 1], F32)
        idf = sb("idf", [128, 128], F32)
        convw = sb("convw_sb", [128, 3 * NH], F32)
        convb = sb("convb_sb", [128, NH], F32)
        srow = sb("srow", [1, 640], F32)
        asum = sb("asum", [128, 32], F32)
        mrow = sb("mrow", [1, 4], F32)
        swcol = sb("swcol", [128, 1], F32)

        ps_mm = [psb(f"psA{i}", [128, 512]) for i in range(2)]
        ps_w2 = [psb(f"psW{i}", [128, 512]) for i in range(4)]
        ps_st = psb("ps_st", [128, 512])
        ps_bc = psb("ps_bc", [128, 512])
        ob_ps = ps_w2 + ps_mm + [ps_st, ps_bc]
        ob_keys = ["psW0", "psW1", "psW2", "psW3", "psA0", "psA1", "ps_st", "ps_bc"]

        p.tv("vector", "memset", ones1[:], 1.0, writes=("ones1",))
        p.tv("vector", "memset", onesc[:], 1.0, writes=("onesc",))
        p.dma("sync", idf[:], idf_d, writes=("idf",))
        p.dma("sync", convw[:], convw_d, writes=("convw",))
        p.dma("sync", convb[:], convb_d, writes=("convb",))
        p.tv("vector", "memset", cmbA[:], 0.0, writes=("cmbA",))
        p.dma("sync", xo_d, x_d, writes=("xo_init",))

        def wprep(wT_d, wq_dram, KD, OD, mcol_slice, mkey):
            nkt = KD // 128
            stag = [(S1, "wstA"), (xcur, "wstB")]
            for k in range(nkt):
                buf, bk = stag[k % 2]
                oth, ok_ = stag[(k + 1) % 2]
                p.dma("sync", buf[:, 0:OD], wT_d[k * 128:(k + 1) * 128, :], writes=(bk,))
                p.emit("scalar", lambda e, buf=buf, oth=oth, k=k, OD=OD: e.activation(
                    oth[:, 0:OD], buf[:, 0:OD], AF.Abs, accum_out=asum[:, k:k + 1]),
                    reads=(bk,), writes=(ok_, f"as{k}"))
            p.tv("vector", "reduce_sum", asum[:, 0:1], asum[:, 0:nkt], AX.X,
                 reads=tuple(f"as{k}" for k in range(nkt)), writes=("asum0",))
            p.tv("tensor", "matmul", ps_st[0:1, 0:1], onesc[:, 0:1], asum[:, 0:1],
                 start=True, stop=True, reads=("asum0", "onesc"), writes=("ps_st",))
            p.tv("vector", "tensor_scalar", mcol_slice, ps_st[0:1, 0:1],
                 1.0 / (KD * OD), 1e-5, ALU.mult, ALU.max, reads=("ps_st",), writes=(mkey,))
            p.tv("vector", "reciprocal", mrow[0:1, 2:3], mcol_slice, reads=(mkey,), writes=("minv",))
            if DEBUG_L0FFN and mkey == "m1":
                p.dma("sync", dbg_m2, mrow[0:1, :], reads=("minv", mkey), writes=("dbgm2",))
            p.tv("tensor", "matmul", ps_bc[:, 0:1], ones1[0:1, :], mrow[0:1, 2:3],
                 start=True, stop=True, reads=("minv", "ones1"), writes=("ps_bc",))
            p.tv("scalar", "copy", swcol[:], ps_bc[:, 0:1], reads=("ps_bc",), writes=("swcol",))
            if DEBUG_L0FFN and mkey == "m1":
                p.dma("sync", dbg_pbc, swcol[:], reads=("swcol",), writes=("dbgpbc",))
            for k in range(nkt):
                buf, bk = stag[k % 2]
                p.dma("sync", buf[:, 0:OD], wT_d[k * 128:(k + 1) * 128, :], writes=(bk,))
                if DEBUG_L0FFN and mkey == "m1" and k == 0:
                    p.dma("sync", dbg_wraw, buf[:, 0:OD], reads=(bk,), writes=("dbgwraw",))
                    p.dma("sync", dbg_swc, swcol[:], reads=("swcol",), writes=("dbgswc",))
                p.tv("vector", "tensor_scalar", buf[:, 0:OD], buf[:, 0:OD], swcol[:, 0:1], RND,
                     ALU.mult, ALU.add, reads=(bk, "swcol"), writes=(bk,))
                p.tv("vector", "tensor_scalar", buf[:, 0:OD], buf[:, 0:OD], RND, None,
                     ALU.subtract, reads=(bk,), writes=(bk,))
                if DEBUG_L0FFN and mkey == "m1" and k == 0:
                    p.dma("sync", dbg_wrnd, buf[:, 0:OD], reads=(bk,), writes=("dbgwrnd",))
                p.tv("vector", "tensor_scalar", wbuf[:, 0:OD], buf[:, 0:OD], 1.0, -1.0,
                     ALU.min, ALU.max, reads=(bk,), writes=("wbufM",))
                p.dma("sync", wq_dram[k * 128:(k + 1) * 128, :], wbuf[:, 0:OD],
                      reads=("wbufM",), writes=(mkey + f"_q{k}",))

        wprep(w1T_d, wq1_d, DM, DH, mrow[0:1, 0:1], "m1")
        wprep(w2T_d, wq2_d, DH, DM, mrow[0:1, 1:2], "m2")
        m1_ap, m2_ap = mrow[0:1, 0:1], mrow[0:1, 1:2]
        if DEBUG_L0FFN:
            p.dma("sync", dbg_wq, wq1_d[0:128, :], reads=("m1_q0",), writes=("dbgwq",))
            p.dma("sync", dbg_wq2, wq2_d[0:128, :], reads=("m2_q0",), writes=("dbgwq2",))
            p.dma("sync", dbg_m, mrow[0:1, :], reads=("m1", "m2", "m1_m", "m2_m"), writes=("dbgm",))
        p.barrier()

        windows = [(0, 511, 0), (511, 512, 1), (1023, 512, 2), (1535, 512, 3), (2047, 1, 3)]

        for s in range(NSEQ):
            for j in range(1, 33):
                p.tv("vector", "memset", aren[:, j * AR_W:j * AR_W + 2], 0.0,
                     writes=(f"slot{j}_lead",))
            for c in range(4):
                base = c * CH
                for a in range(NA):
                    p.dma("sync", xcur[:, a * CH:(a + 1) * CH],
                          x_d[s * DM + a * 128: s * DM + (a + 1) * 128, base:base + CH],
                          writes=(f"xcur{a}",))
                stat_sums(p, [(xcur[:, a * CH:(a + 1) * CH], f"xcur{a}") for a in range(NA)],
                          CH, ps_st, S1, [f"sq{a}" for a in range(NA)], onesc)
                if DEBUG_L0FFN and s == 0 and c == 0:
                    p.dma("sync", dbg_sq, S1[:],
                          reads=tuple(f"sq{a}" for a in range(NA)), writes=("dbgsq",))
                rstd, mur = ln_rows(p, rA, rB, CH, ps_st, DM, 1e-5)
                if DEBUG_L0FFN and s == 0 and c == 0:
                    p.dma("sync", dbg_rows, rows[0:1, :],
                          reads=("rA0", "rA1", "rA2", "rA3"), writes=("dbgrows",))
                    p.dma("sync", dbg_x0, xcur[:, 0:CH], reads=("xcur0",), writes=("dbgx0",))
                bcast_row(p, rstd, "rA2", CH, ps_bc, sc_b, "sc_b", ones1)
                bcast_row(p, mur, "rA3", CH, ps_bc, iv_b, "iv_b", ones1)
                for a in range(NA):
                    p.tv("vector", "tensor_mul", S1[:, a * CH:(a + 1) * CH],
                         xcur[:, a * CH:(a + 1) * CH], sc_b[:, 0:CH],
                         reads=(f"xcur{a}", "sc_b"), writes=(f"sq{a}",))
                    p.tv("vector", "tensor_sub", xcur[:, a * CH:(a + 1) * CH],
                         S1[:, a * CH:(a + 1) * CH], iv_b[:, 0:CH],
                         reads=(f"sq{a}", "iv_b"), writes=(f"xcur{a}",))
                for a in range(NA):
                    ct, ck = (ctmpA, "ctmpA") if a % 2 == 0 else (ctmpB, "ctmpB")
                    p.tv("scalar", "activation", ct[:, 0:CH], xcur[:, a * CH:(a + 1) * CH],
                         AF.Abs, reads=(f"xcur{a}",), writes=(ck,))
                    if a == 0:
                        p.tv("vector", "tensor_copy", cmbA[:, 0:CH], ct[:, 0:CH],
                             reads=(ck,), writes=("cmbA",))
                    else:
                        p.tv("vector", "tensor_max", cmbA[:, 0:CH], cmbA[:, 0:CH], ct[:, 0:CH],
                             reads=("cmbA", ck), writes=("cmbA",))
                amax_rows(p, cmbA, "cmbA", CH, ps_st, amax_tm, idf, scr_d, srow, "qa")
                quant_rows(p, srow, "qa_row", rA, rB, m1_ap, "m1", CH)
                bcast_row(p, rA(0, CH), "rA0", CH, ps_bc, sc_b, "sc_b", ones1)
                bcast_row(p, rA(1, CH), "rA1", CH, ps_bc, iv_b, "iv_b", ones1)
                for a in range(NA):
                    q = qf32[a % 2]
                    qk = f"qf{a % 2}"
                    p.tv("vector", "tensor_mul", q[:, 0:CH], xcur[:, a * CH:(a + 1) * CH],
                         sc_b[:, 0:CH], reads=(f"xcur{a}", "sc_b"), writes=(qk,))
                    p.tv("vector", "tensor_scalar", q[:, 0:CH], q[:, 0:CH], RND, None,
                         ALU.add, reads=(qk,), writes=(qk,))
                    p.tv("vector", "tensor_scalar", q[:, 0:CH], q[:, 0:CH], RND, None,
                         ALU.subtract, reads=(qk,), writes=(qk,))
                    p.tv("vector", "tensor_scalar", xq[:, a * CH:(a + 1) * CH], q[:, 0:CH],
                         127.0, -128.0, ALU.min, ALU.max, reads=(qk,), writes=(f"xq{a}",))
                for half in range(2):
                    for a in range(NA):
                        p.dma("sync", wbuf[:, a * 2048:(a + 1) * 2048],
                              wq1_d[a * 128:(a + 1) * 128, half * 2048:(half + 1) * 2048],
                              writes=("wbufM",))
                    for i in range(16 * half, 16 * half + 16):
                        ii = i - 16 * half
                        ps, pkey = ps_mm[i % 2], f"psA{i % 2}"
                        fns = [(lambda e, a=a, ii=ii, ps=ps: e.matmul(
                            ps[:, 0:CH], wbuf[:, a * 2048 + ii * 128:a * 2048 + (ii + 1) * 128],
                            xq[:, a * CH:(a + 1) * CH], start=(a == 0), stop=(a == NA - 1)))
                            for a in range(NA)]
                        p.group("tensor", fns,
                                reads=("wbufM",) + tuple(f"xq{a}" for a in range(NA)),
                                writes=(pkey,))
                        j = i + 1
                        p.tv("vector", "tensor_mul", aren[:, j * AR_W + 2:j * AR_W + 514],
                             ps[:, 0:CH], iv_b[:, 0:CH],
                             reads=(pkey, "iv_b"), writes=(f"slot{j}_body",))
                if c == 3:
                    for i in range(NH):
                        j = i + 1
                        p.tv("vector", "memset", aren[:, j * AR_W + 514:j * AR_W + 515], 0.0,
                             writes=(f"slot{j}_pad",))
                if DEBUG_L0FFN and s == 0 and c == 0:
                    p.dma("sync", dbg_xn, xcur[:],
                          reads=tuple(f"xcur{a}" for a in range(NA)), writes=("dbgxn",))
                    p.tv("vector", "tensor_copy", S1[:, 0:NA * CH], xq[:, 0:NA * CH],
                         reads=tuple(f"xq{a}" for a in range(NA)), writes=("dbgxq_s",))
                    p.dma("sync", dbg_xq, S1[:], reads=("dbgxq_s",), writes=("dbgxq",))
                    p.dma("sync", dbg_h, aren[:, 1 * AR_W:5 * AR_W],
                          reads=tuple(f"slot{j}_body" for j in range(1, 6)), writes=("dbgh",))
                    p.dma("sync", dbg_ra[0:1, 0:640], srow[0:1, 0:640],
                          reads=("qa_row",), writes=("dbgra",))
                for (g0, W, wc) in windows:
                    if wc != c:
                        continue
                    if DEBUG_L0FFN and s == 0 and c == 1:
                        p.dma("sync", dbg_sl1, aren[:, 1 * AR_W:5 * AR_W],
                              reads=tuple(f"slot{j}_body" for j in range(1, 6)) +
                                    tuple(f"slot{j}_lead" for j in range(1, 6)),
                              writes=("dbgsl1",))
                    off = g0 - base + 2
                    for i in range(NH):
                        j = i + 1
                        sl = aren[:, j * AR_W:(j + 1) * AR_W]
                        rd = (f"slot{j}_body", f"slot{j}_lead") + \
                             ((f"slot{j}_pad",) if W == 1 else ())
                        if W == 1:
                            vl, vc, vr = sl[:, 0:1], sl[:, 1:2], sl[:, 514:515]
                        else:
                            vl = sl[:, off - 1:off - 1 + W]
                            vc = sl[:, off:off + W]
                            vr = sl[:, off + 1:off + 1 + W]
                        p.tv("vector", "tensor_scalar", ctmpA[:, 0:W], vl,
                             convw[:, 3 * i:3 * i + 1], None, ALU.mult,
                             reads=rd + ("convw",), writes=("ctmpA",))
                        p.tv("vector", "scalar_tensor_tensor", ctmpB[:, 0:W], vc,
                             convw[:, 3 * i + 1:3 * i + 2], ctmpA[:, 0:W], ALU.mult, ALU.add,
                             reads=rd + ("convw", "ctmpA"), writes=("ctmpB",))
                        p.tv("vector", "scalar_tensor_tensor", ctmpA[:, 0:W], vr,
                             convw[:, 3 * i + 2:3 * i + 3], ctmpB[:, 0:W], ALU.mult, ALU.add,
                             reads=rd + ("convw", "ctmpB"), writes=("ctmpA",))
                        if W > 1:
                            p.tv("gpsimd", "tensor_copy", sl[:, 0:2], sl[:, 512:514],
                                 reads=(f"slot{j}_body",), writes=(f"slot{j}_lead",))
                            dst = aren[:, i * AR_W + 2:i * AR_W + 2 + W]
                            dk = f"slot{i}_body"
                        else:
                            dst = aren[:, j * AR_W + 515:j * AR_W + 516]
                            dk = f"slot{j}_mini"
                        p.emit("scalar", lambda e, dst=dst, W=W, i=i: e.activation(
                            dst, ctmpA[:, 0:W], AF.Silu, bias=convb[:, i:i + 1]),
                            reads=("ctmpA", "convb"), writes=(dk,))

                    def hs_ap(i):
                        if W > 1:
                            return aren[:, i * AR_W + 2:i * AR_W + 2 + W], f"slot{i}_body"
                        return (aren[:, (i + 1) * AR_W + 515:(i + 1) * AR_W + 516],
                                f"slot{i + 1}_mini")
                    for i in range(NH):
                        ai, ki = hs_ap(i)
                        ct, ck = (ctmpA, "ctmpA") if i % 2 == 0 else (ctmpB, "ctmpB")
                        p.tv("scalar", "activation", ct[:, 0:W], ai, AF.Abs,
                             reads=(ki,), writes=(ck,))
                        if i == 0:
                            p.tv("vector", "tensor_copy", cmbA[:, 0:W], ct[:, 0:W],
                                 reads=(ck,), writes=("cmbA",))
                        else:
                            p.tv("vector", "tensor_max", cmbA[:, 0:W], cmbA[:, 0:W], ct[:, 0:W],
                                 reads=("cmbA", ck), writes=("cmbA",))
                    amax_rows(p, cmbA, "cmbA", W, ps_st, amax_tm, idf, scr_d, srow, "qb")
                    if DEBUG_L0FFN and s == 0 and W == 1:
                        p.dma("sync", dbg_mini[0:1, 0:1], srow[0:1, 0:1],
                              reads=("qb_row",), writes=("dbgmini0",))
                        for _i in range(32):
                            _a, _k = hs_ap(_i)
                            p.dma("sync", dbg_mhs[:, _i:_i + 1], _a,
                                  reads=(_k,), writes=(f"dbgmhs{_i}",),
                                  allow_slow_non_contiguous=True)
                    if DEBUG_L0FFN and s == 0 and c == 0 and g0 == 0:
                        p.dma("sync", dbg_hs, aren[:, 0 * AR_W:4 * AR_W],
                              reads=tuple(f"slot{i}_body" for i in range(4)), writes=("dbghs",))
                        p.dma("sync", dbg_rb[0:1, 0:640], srow[0:1, 0:640],
                              reads=("qb_row",), writes=("dbgrb",))
                    quant_rows(p, srow, "qb_row", rA, rB, m2_ap, "m2", W)
                    bcast_row(p, rA(0, W), "rA0", W, ps_bc, sc_b, "sc_b", ones1)
                    bcast_row(p, rA(1, W), "rA1", W, ps_bc, iv_b, "iv_b", ones1)
                    for half in range(2):
                        for k in range(16 * half, 16 * half + 16):
                            p.dma("sync", wbuf[:, (k - 16 * half) * DM:(k - 16 * half + 1) * DM],
                                  wq2_d[k * 128:(k + 1) * 128, :], writes=("wbufM",))
                        for k in range(16 * half, 16 * half + 16):
                            kk = k - 16 * half
                            ai, ki = hs_ap(k)
                            q, qk = qf32[k % 2], f"qf{k % 2}"
                            qt, qtk = qtmp[k % 2], f"qt{k % 2}"
                            p.tv("vector", "tensor_mul", q[:, 0:W], ai, sc_b[:, 0:W],
                                 reads=(ki, "sc_b"), writes=(qk,))
                            p.tv("vector", "tensor_scalar", q[:, 0:W], q[:, 0:W], RND, None,
                                 ALU.add, reads=(qk,), writes=(qk,))
                            p.tv("vector", "tensor_scalar", q[:, 0:W], q[:, 0:W], RND, None,
                                 ALU.subtract, reads=(qk,), writes=(qk,))
                            p.tv("vector", "tensor_scalar", qt[:, 0:W], q[:, 0:W],
                                 127.0, -128.0, ALU.min, ALU.max, reads=(qk,), writes=(qtk,))
                            if DEBUG_L0FFN and s == 0 and c == 0 and g0 == 0 and k == 0:
                                p.dma("sync", dbg_q2[:, 0:W], qt[:, 0:W],
                                      reads=(qtk,), writes=("dbgq2",))
                            if DEBUG_L0FFN and s == 0 and c == 0 and g0 == 0 and k == 5:
                                p.dma("sync", dbg_q5[:, 0:W], qt[:, 0:W],
                                      reads=(qtk,), writes=("dbgq5",))
                            fns = [(lambda e, ob=ob, kk=kk, qt=qt, W=W,
                                    st_=(k == 0), sp_=(k == NH - 1): e.matmul(
                                ob_ps[ob][:, 0:W], wbuf[:, kk * DM + ob * 128:kk * DM + (ob + 1) * 128],
                                qt[:, 0:W], start=st_, stop=sp_))
                                for ob in range(8)]
                            p.group("tensor", fns, reads=("wbufM", qtk), writes=tuple(ob_keys))
                    for ob in range(8):
                        o, ok = obuf[ob % 2], f"ob{ob % 2}"
                        p.tv("vector", "tensor_mul", o[:, 0:W], ob_ps[ob][:, 0:W],
                             iv_b[:, 0:W], reads=(ob_keys[ob], "iv_b"), writes=(ok,))
                        if DEBUG_L0FFN and s == 0 and c == 0 and g0 == 0 and ob == 0:
                            p.dma("sync", dbg_ob[:, 0:W], o[:, 0:W],
                                  reads=(ok,), writes=("dbgob",))
                            p.tv("scalar", "copy", S1[:, 0:W], ob_ps[0][:, 0:W],
                                 reads=(ob_keys[0],), writes=("sq0", "dbgps_s"))
                            p.dma("sync", dbg_ps[:, 0:W], S1[:, 0:W],
                                  reads=("dbgps_s",), writes=("dbgps",))
                            p.dma("sync", dbg_ivb[:, 0:W], iv_b[:, 0:W],
                                  reads=("iv_b",), writes=("dbgivb",))

                        p.dma("gpsimd", xo_d[s * DM + ob * 128:s * DM + (ob + 1) * 128, g0:g0 + W],
                              o[:, 0:W], reads=(ok, "xo_init"),
                              writes=(f"xo_s{s}c{c}w{g0}o{ob}",), accum_op=ALU.add,
                              allow_slow_non_contiguous=(W == 1))
    return



_cache = {}


def _shard_x(x, core):
    xs = x[2 * core:2 * core + 2]
    return np.ascontiguousarray(xs.transpose(0, 2, 1)).reshape(NSEQ * DM, T)


def _unshard(res_list, key="x_out"):
    outs = []
    for r in res_list:
        xo = r[key].reshape(NSEQ, DM, T)
        outs.append(xo.transpose(0, 2, 1))
    return np.ascontiguousarray(np.concatenate(outs, axis=0))


def run_l0ffn(inputs, x_current, cores=8):
    if "l0ffn" not in _cache:
        _cache["l0ffn"] = build_l0ffn()
    nc = _cache["l0ffn"]
    w = {
        "w1T": np.ascontiguousarray(np.asarray(inputs["L0_ffn_w1"], dtype=np.float32).T),
        "w2T": np.ascontiguousarray(np.asarray(inputs["L0_ffn_w2"], dtype=np.float32).T),
        "convw": np.ascontiguousarray(np.asarray(inputs["L0_ffn_conv_w"], dtype=np.float32).reshape(32, 128, 3).transpose(1, 0, 2).reshape(128, 96)),
        "convb": np.ascontiguousarray(np.asarray(inputs["L0_ffn_conv_b"], dtype=np.float32).reshape(32, 128).T),
        "identf32": np.eye(128, dtype=np.float32),
    }
    in_maps = [dict(w, x_res=_shard_x(x_current, c)) for c in range(cores)]
    res = run_bass_kernel_spmd(nc, in_maps, core_ids=list(range(cores)))
    return _unshard([r for r in res.results])





DEBUG_L1FFN = False


def emit_l1ffn(p, x_d, xo_d, pfx="m_"):
    nc = p.nc
    CH = 512
    NE = 8
    DI = 4096
    NI = DI // 128

    attnT_d = nc.dram_tensor(pfx + "attnT", [DM, DM], F32, kind="ExternalInput").ap()
    gateT_d = nc.dram_tensor(pfx + "gateT", [128, NA * NE], F32, kind="ExternalInput").ap()
    gpT_d = nc.dram_tensor(pfx + "gpT", [NE * DM, DI], F32, kind="ExternalInput").ap()
    upT_d = nc.dram_tensor(pfx + "upT", [NE * DM, DI], F32, kind="ExternalInput").ap()
    dnT_d = nc.dram_tensor(pfx + "dnT", [NE * DI, DM], F32, kind="ExternalInput").ap()
    lnp_d = nc.dram_tensor(pfx + "lnp", [128, 4 * NA], F32, kind="ExternalInput").ap()
    idf_d = nc.dram_tensor(pfx + "identf32", [128, 128], F32, kind="ExternalInput").ap()
    scr2_d = nc.dram_tensor(pfx + "scr2", [NE, CH], F32, kind="Internal").ap()
    if DEBUG_L1FFN:
        dbg_xin = nc.dram_tensor(pfx + "dbg_xin", [128, 8 * 512], F32, kind="ExternalOutput").ap()
        dbg_h = nc.dram_tensor(pfx + "dbg_h", [128, 8 * 512], F32, kind="ExternalOutput").ap()
        dbg_t = nc.dram_tensor(pfx + "dbg_t", [128, 8 * 512], F32, kind="ExternalOutput").ap()
        dbg_lgt = nc.dram_tensor(pfx + "dbg_lgt", [8, 512], F32, kind="ExternalOutput").ap()
        dbg_cmb = nc.dram_tensor(pfx + "dbg_cmb", [8, 512], F32, kind="ExternalOutput").ap()
        dbg_gu = nc.dram_tensor(pfx + "dbg_gu", [128, 512], F32, kind="ExternalOutput").ap()
        dbg_hf = nc.dram_tensor(pfx + "dbg_hf", [128, 8 * 512], F32, kind="ExternalOutput").ap()

    with ExitStack() as st:
        sb = lambda n_, s, d: st.enter_context(nc.sbuf_tensor(pfx + n_, s, d))
        psb = lambda n_, s: st.enter_context(nc.psum_tensor(pfx + n_, s, F32))

        xcur = sb("xcur", [128, NA * CH], F32)
        xin = sb("xin", [128, NA * CH], F32)
        hbuf = sb("hbuf", [128, NA * CH], F32)
        S1 = sb("S1", [128, NA * CH], F32)
        tq = sb("tq", [128, NA * CH], BF16)
        gu = sb("gu", [128, NI * CH], BF16)
        wbufW = sb("wbufW", [128, 32768], BF16)
        gateT = sb("gateT_sb", [128, NA * NE], F32)
        gwg = sb("gwg_sb", [128, NA * NE], F32)
        b2gw = sb("b2gw", [1, NE], F32)
        lnp = sb("lnp_sb", [128, 4 * NA], F32)
        gs = [sb(f"gs{i}", [128, CH], BF16) for i in range(2)]
        otmp = sb("otmp", [128, CH], F32)
        obuf = [sb("obuf0", [128, CH], F32)] * 2
        lgt = sb("lgt", [8, CH], F32)
        lgt_tm = sb("lgt_tm", [128, 4 * 8], F32)
        comb_tm = sb("comb_tm", [128, 4 * 8], F32)
        comb_fm = sb("comb_fm", [8, CH], F32)
        combB = sb("combB", [128, CH], F32)
        crow = sb("crow", [1, CH], F32)
        rows = sb("rows", [1, 4 * 512], F32)
        rows2 = sb("rows2", [1, 4 * 512], F32)
        rA = lambda i, W: rows[0:1, i * 512:i * 512 + W]
        rB = lambda i, W: rows2[0:1, i * 512:i * 512 + W]
        sc_b = sb("sc_b", [128, 512], F32)
        iv_b = sb("iv_b", [128, 512], F32)
        ones1 = sb("ones1", [1, 128], F32)
        onesc = sb("onesc", [128, 1], F32)
        onesrow = sb("onesrow", [1, CH], F32)
        idf = sb("idf", [128, 128], F32)
        e1c = sb("e1c", [128, 1], F32)
        m1c = sb("m1c", [128, 1], F32)
        m2c = sb("m2c", [128, 1], F32)
        w1c = sb("w1c", [128, 1], F32)
        w2c = sb("w2c", [128, 1], F32)
        eqA = sb("eqA", [128, 8], F32)
        eqB = sb("eqB", [128, 8], F32)
        mskd = sb("mskd", [128, 8], F32)

        ps_g = psb("ps_g", [128, 512])
        ps_u = psb("ps_u", [128, 512])
        ps_d = [psb(f"ps_d{i}", [128, 512]) for i in range(4)]
        ps_st = psb("ps_st", [128, 512])
        ps_bc = psb("ps_bc", [128, 512])

        p.tv("vector", "memset", ones1[:], 1.0, writes=("ones1",))
        p.tv("vector", "memset", onesc[:], 1.0, writes=("onesc",))
        p.tv("vector", "memset", onesrow[:], 1.0, writes=("onesrow",))
        p.dma("sync", idf[:], idf_d, writes=("idf",))
        p.dma("sync", lnp[:], lnp_d, writes=("lnp",))
        p.dma("sync", gateT[:], gateT_d, writes=("gateT",))

        g1 = lambda a: lnp[:, 0 * NA + a:0 * NA + a + 1]
        b1 = lambda a: lnp[:, 1 * NA + a:1 * NA + a + 1]
        g2 = lambda a: lnp[:, 2 * NA + a:2 * NA + a + 1]
        b2 = lambda a: lnp[:, 3 * NA + a:3 * NA + a + 1]

        for a in range(NA):
            p.tv("vector", "tensor_scalar", gwg[:, a * NE:(a + 1) * NE],
                 gateT[:, a * NE:(a + 1) * NE], g2(a), None, ALU.mult,
                 reads=("gateT", "lnp"), writes=("gwg",))
        fns = [(lambda e, a=a: e.matmul(ps_st[0:1, 0:NE], b2(a), gateT[:, a * NE:(a + 1) * NE],
                                        start=(a == 0), stop=(a == NA - 1)))
               for a in range(NA)]
        p.group("tensor", fns, reads=("lnp", "gateT"), writes=("ps_st",))
        p.tv("vector", "tensor_copy", b2gw[:], ps_st[0:1, 0:NE], reads=("ps_st",), writes=("b2gw",))

        def ln_to(src, skeys, dst, dkeys, W, eps, gain=None, bias=None):
            stat_sums(p, [(src[:, a * W:(a + 1) * W], skeys[a]) for a in range(NA)],
                      W, ps_st, S1, [f"sq{a}" for a in range(NA)], onesc)
            rstd, mur = ln_rows(p, rA, rB, W, ps_st, DM, eps)
            bcast_row(p, rstd, "rA2", W, ps_bc, sc_b, "sc_b", ones1)
            bcast_row(p, mur, "rA3", W, ps_bc, iv_b, "iv_b", ones1)
            for a in range(NA):
                p.tv("vector", "tensor_mul", S1[:, a * W:(a + 1) * W],
                     src[:, a * W:(a + 1) * W], sc_b[:, 0:W],
                     reads=(skeys[a], "sc_b"), writes=(f"sq{a}",))
                if gain is None:
                    p.tv("vector", "tensor_sub", dst[:, a * W:(a + 1) * W],
                         S1[:, a * W:(a + 1) * W], iv_b[:, 0:W],
                         reads=(f"sq{a}", "iv_b"), writes=(dkeys[a],))
                else:
                    p.tv("vector", "tensor_sub", S1[:, a * W:(a + 1) * W],
                         S1[:, a * W:(a + 1) * W], iv_b[:, 0:W],
                         reads=(f"sq{a}", "iv_b"), writes=(f"sq{a}",))
                    p.tv("vector", "tensor_scalar", dst[:, a * W:(a + 1) * W],
                         S1[:, a * W:(a + 1) * W], gain(a), bias(a), ALU.mult, ALU.add,
                         reads=(f"sq{a}", "lnp"), writes=(dkeys[a],))

        for s in range(NSEQ):
            for c in range(4):
                base = c * CH
                for a in range(NA):
                    p.dma("sync", xcur[:, a * CH:(a + 1) * CH],
                          x_d[s * DM + a * 128:s * DM + (a + 1) * 128, base:base + CH],
                          writes=(f"xcur{a}",))
                for a in range(NA):
                    p.dma("gpsimd", wbufW[:, a * DM:(a + 1) * DM],
                          attnT_d[a * 128:(a + 1) * 128, :], writes=("attnT", "wbufW"))
                ln_to(xcur, [f"xcur{a}" for a in range(NA)], xin,
                      [f"xin{a}" for a in range(NA)], CH, 1e-5)
                ln_to(xin, [f"xin{a}" for a in range(NA)], tq,
                      [f"tq{a}" for a in range(NA)], CH, 1e-5, gain=g1, bias=b1)
                for ob in range(NA):
                    ps, pk = ps_d[ob % 4], f"ps_d{ob % 4}"
                    fns = [(lambda e, a=a, ob=ob, ps=ps: e.matmul(
                        ps[:, 0:CH], wbufW[:, a * DM + ob * 128:a * DM + (ob + 1) * 128],
                        tq[:, a * CH:(a + 1) * CH], start=(a == 0), stop=(a == NA - 1)))
                        for a in range(NA)]
                    p.group("tensor", fns,
                            reads=("attnT", "wbufW") + tuple(f"tq{a}" for a in range(NA)), writes=(pk,))
                    p.tv("vector", "tensor_add", hbuf[:, ob * CH:(ob + 1) * CH],
                         ps[:, 0:CH], xin[:, ob * CH:(ob + 1) * CH],
                         reads=(pk, f"xin{ob}"), writes=(f"h{ob}",))
                if DEBUG_L1FFN and s == 0 and c == 0:
                    p.dma("sync", dbg_xin, xin[:], reads=tuple(f"xin{a}" for a in range(NA)),
                          writes=("dbgxin",))
                    p.dma("sync", dbg_h, hbuf[:], reads=tuple(f"h{a}" for a in range(NA)),
                          writes=("dbgh",))
                ln_to(hbuf, [f"h{a}" for a in range(NA)], tq,
                      [f"tq{a}" for a in range(NA)], CH, 1e-5, gain=g2, bias=b2)
                fns = [(lambda e, a=a: e.matmul(
                    ps_st[0:8, 0:CH], gwg[:, a * NE:(a + 1) * NE],
                    S1[:, a * CH:(a + 1) * CH], start=(a == 0), stop=False))
                    for a in range(NA)]
                fns.append(lambda e: e.matmul(ps_st[0:8, 0:CH], b2gw[:], onesrow[:, 0:CH],
                                              start=False, stop=True))
                p.group("tensor", fns,
                        reads=("gwg", "b2gw", "onesrow") + tuple(f"sq{a}" for a in range(NA)),
                        writes=("ps_st",))
                p.tv("scalar", "copy", lgt[:], ps_st[0:8, 0:CH], reads=("ps_st",), writes=("lgt",))
                if DEBUG_L1FFN and s == 0 and c == 0:
                    p.tv("vector", "tensor_copy", S1[:, 0:NA * CH], tq[:, 0:NA * CH],
                         reads=tuple(f"tq{a}" for a in range(NA)),
                         writes=("dbgt_s",) + tuple(f"sq{a}" for a in range(NA)))
                    p.dma("sync", dbg_t, S1[:], reads=("dbgt_s",), writes=("dbgt",))
                    p.dma("sync", dbg_lgt, lgt[0:8, :], reads=("lgt",), writes=("dbglgt",))
                for blk in range(4):
                    p.tv("tensor", "transpose", ps_st[0:128, 0:8],
                         lgt[0:8, blk * 128:(blk + 1) * 128], idf[0:8, 0:8],
                         reads=("lgt", "idf"), writes=("ps_st",))
                    p.tv("vector", "tensor_copy", lgt_tm[:, blk * 8:(blk + 1) * 8],
                         ps_st[0:128, 0:8], reads=("ps_st",), writes=(f"ltm{blk}",))
                for blk in range(4):
                    lt = lgt_tm[:, blk * 8:(blk + 1) * 8]
                    lk = f"ltm{blk}"
                    p.tv("vector", "reduce_max", m1c[:], lt, AX.X, reads=(lk,), writes=("m1c",))
                    p.tv("vector", "tensor_scalar", eqA[:], lt, m1c[:, 0:1], None, ALU.is_equal,
                         reads=(lk, "m1c"), writes=("eqA",))
                    p.tv("vector", "scalar_tensor_tensor", mskd[:], eqA[:], -1e30, lt,
                         ALU.mult, ALU.add, reads=("eqA", lk), writes=("mskd",))
                    p.tv("vector", "reduce_max", m2c[:], mskd[:], AX.X,
                         reads=("mskd",), writes=("m2c",))
                    p.tv("vector", "tensor_scalar", eqB[:], mskd[:], m2c[:, 0:1], None,
                         ALU.is_equal, reads=("mskd", "m2c"), writes=("eqB",))
                    p.tv("vector", "tensor_sub", e1c[:], m2c[:], m1c[:],
                         reads=("m1c", "m2c"), writes=("e1c",))
                    p.tv("scalar", "activation", e1c[:], e1c[:], AF.Exp,
                         reads=("e1c",), writes=("e1c",))
                    p.tv("vector", "tensor_scalar", w1c[:], e1c[:], 1.0, None, ALU.add,
                         reads=("e1c",), writes=("w1c",))
                    p.tv("vector", "reciprocal", w1c[:], w1c[:], reads=("w1c",), writes=("w1c",))
                    p.tv("vector", "tensor_mul", w2c[:], e1c[:], w1c[:],
                         reads=("e1c", "w1c"), writes=("w2c",))
                    p.tv("vector", "tensor_scalar", eqA[:], eqA[:], w1c[:, 0:1], None, ALU.mult,
                         reads=("eqA", "w1c"), writes=("eqA",))
                    p.tv("vector", "scalar_tensor_tensor", comb_tm[:, blk * 8:(blk + 1) * 8],
                         eqB[:], w2c[:, 0:1], eqA[:], ALU.mult, ALU.add,
                         reads=("eqB", "w2c", "eqA"), writes=(f"ctm{blk}",))
                    p.tv("tensor", "transpose", ps_st[0:8, 0:128],
                         comb_tm[:, blk * 8:(blk + 1) * 8], idf[:],
                         reads=(f"ctm{blk}", "idf"), writes=("ps_st",))
                    p.tv("vector", "tensor_copy", comb_fm[0:8, blk * 128:(blk + 1) * 128],
                         ps_st[0:8, 0:128], reads=("ps_st",), writes=("comb_fm",))
                p.dma("sync", scr2_d, comb_fm[0:8, :], reads=("comb_fm",), writes=("scr2",))
                if DEBUG_L1FFN and s == 0 and c == 0:
                    p.dma("sync", dbg_cmb, comb_fm[0:8, :], reads=("comb_fm",), writes=("dbgcmb",))
                for ex in range(NE):
                    p.dma("sync", crow[0:1, :], scr2_d[ex:ex + 1, :],
                          reads=("scr2",), writes=("crow",))
                    bcast_row(p, crow[0:1, 0:CH], "crow", CH, ps_bc,
                              combB, "combB", ones1)
                    for half in range(2):
                        for a in range(NA):
                            p.dma("gpsimd", wbufW[:, a * 2048:(a + 1) * 2048],
                                  gpT_d[ex * DM + a * 128:ex * DM + (a + 1) * 128,
                                        half * 2048:(half + 1) * 2048],
                                  writes=("wbufG", "wbufW"))
                            p.dma("gpsimd", wbufW[:, 16384 + a * 2048:16384 + (a + 1) * 2048],
                                  upT_d[ex * DM + a * 128:ex * DM + (a + 1) * 128,
                                        half * 2048:(half + 1) * 2048],
                                  writes=("wbufU", "wbufW"))
                        for i in range(16 * half, 16 * half + 16):
                            ii = i - 16 * half
                            fns = [(lambda e, a=a, ii=ii: e.matmul(
                                ps_g[:, 0:CH], wbufW[:, a * 2048 + ii * 128:a * 2048 + (ii + 1) * 128],
                                tq[:, a * CH:(a + 1) * CH], start=(a == 0), stop=(a == NA - 1)))
                                for a in range(NA)]
                            p.group("tensor", fns,
                                    reads=("wbufG", "wbufW") + tuple(f"tq{a}" for a in range(NA)),
                                    writes=("ps_g",))
                            fns = [(lambda e, a=a, ii=ii: e.matmul(
                                ps_u[:, 0:CH],
                                wbufW[:, 16384 + a * 2048 + ii * 128:16384 + a * 2048 + (ii + 1) * 128],
                                tq[:, a * CH:(a + 1) * CH], start=(a == 0), stop=(a == NA - 1)))
                                for a in range(NA)]
                            p.group("tensor", fns,
                                    reads=("wbufU", "wbufW") + tuple(f"tq{a}" for a in range(NA)),
                                    writes=("ps_u",))
                            gb, gk = gs[i % 2], f"gs{i % 2}"
                            p.tv("scalar", "activation", gb[:, 0:CH], ps_g[:, 0:CH], AF.Silu,
                                 reads=("ps_g",), writes=(gk,))
                            p.tv("vector", "tensor_mul", gu[:, i * CH:(i + 1) * CH],
                                 gb[:, 0:CH], ps_u[:, 0:CH],
                                 reads=(gk, "ps_u"), writes=(f"gu{i}",))
                            if DEBUG_L1FFN and s == 0 and c == 0 and ex == 0 and i == 0:
                                p.tv("vector", "tensor_copy", otmp[:, 0:CH],
                                     gu[:, 0:CH], reads=("gu0",), writes=("dbggu_s",))
                                p.dma("sync", dbg_gu, otmp[:, 0:CH],
                                      reads=("dbggu_s",), writes=("dbggu",))
                    for k in range(NI):
                        p.dma("gpsimd", wbufW[:, k * DM:(k + 1) * DM],
                              dnT_d[ex * DI + k * 128:ex * DI + (k + 1) * 128, :],
                              writes=("wbufD", "wbufW"))
                    for sweep in range(2):
                        for ob in range(sweep * 4, sweep * 4 + 4):
                            ps, pk = ps_d[ob % 4], f"ps_d{ob % 4}"
                            fns = [(lambda e, k=k, ob=ob, ps=ps: e.matmul(
                                ps[:, 0:CH], wbufW[:, k * DM + ob * 128:k * DM + (ob + 1) * 128],
                                gu[:, k * CH:(k + 1) * CH], start=(k == 0), stop=(k == NI - 1)))
                                for k in range(NI)]
                            p.group("tensor", fns,
                                    reads=("wbufD", "wbufW") + tuple(f"gu{k}" for k in range(NI)),
                                    writes=(pk,))
                            p.tv("vector", "tensor_mul", otmp[:, 0:CH], ps[:, 0:CH],
                                 combB[:, 0:CH], reads=(pk, "combB"), writes=("otmp",))
                            p.tv("gpsimd", "tensor_tensor", hbuf[:, ob * CH:(ob + 1) * CH],
                                 hbuf[:, ob * CH:(ob + 1) * CH], otmp[:, 0:CH], ALU.add,
                                 reads=(f"h{ob}", "otmp"), writes=(f"h{ob}",))
                if DEBUG_L1FFN and s == 0 and c == 0:
                    p.dma("sync", dbg_hf, hbuf[:], reads=tuple(f"h{a}" for a in range(NA)),
                          writes=("dbghf",))
                for ob in range(NA):
                    o, ok = obuf[0], "ob0"
                    p.tv("vector", "tensor_add", o[:, 0:CH], xcur[:, ob * CH:(ob + 1) * CH],
                         hbuf[:, ob * CH:(ob + 1) * CH],
                         reads=(f"xcur{ob}", f"h{ob}"), writes=(ok,))
                    p.dma("sync", xo_d[s * DM + ob * 128:s * DM + (ob + 1) * 128, base:base + CH],
                          o[:, 0:CH], reads=(ok,), writes=(f"xo_s{s}c{c}o{ob}",))
    return


def run_l1ffn(inputs, x_current, cores=8):
    if "l1ffn" not in _cache:
        _cache["l1ffn"] = build_l1ffn()
    nc = _cache["l1ffn"]
    f32 = lambda v: np.asarray(v, dtype=np.float32)
    w = {
        "attnT": np.ascontiguousarray(f32(inputs["L1_ffn_attn_w"]).T),
        "gateT": np.ascontiguousarray(f32(inputs["L1_ffn_gate_w"]).T.reshape(8, 128, 8)
                                      .transpose(1, 0, 2).reshape(128, 64)),
        "gpT": np.ascontiguousarray(f32(inputs["L1_ffn_gate_proj"]).transpose(0, 2, 1)).reshape(8 * 1024, 4096),
        "upT": np.ascontiguousarray(f32(inputs["L1_ffn_up_proj"]).transpose(0, 2, 1)).reshape(8 * 1024, 4096),
        "dnT": np.ascontiguousarray(f32(inputs["L1_ffn_down_proj"]).transpose(0, 2, 1)).reshape(8 * 4096, 1024),
        "lnp": np.ascontiguousarray(np.stack([
            f32(inputs["L1_ffn_ln1_g"]).reshape(8, 128),
            f32(inputs["L1_ffn_ln1_b"]).reshape(8, 128),
            f32(inputs["L1_ffn_ln2_g"]).reshape(8, 128),
            f32(inputs["L1_ffn_ln2_b"]).reshape(8, 128),
        ], axis=0).reshape(32, 128).T.reshape(128, 32)),
        "identf32": np.eye(128, dtype=np.float32),
    }
    in_maps = [dict(w, x_res=_shard_x(x_current, c)) for c in range(cores)]
    res = run_bass_kernel_spmd(nc, in_maps, core_ids=list(range(cores)))
    return _unshard([r for r in res.results])



DEBUG_L1ATTN = False


def emit_l1attn(p, x_d, xo_d, pfx="r_"):
    nc = p.nc
    CH = 512

    wk_d = nc.dram_tensor(pfx + "wk", [DM, DM], F32, kind="ExternalInput").ap()
    wv_d = nc.dram_tensor(pfx + "wv", [DM, DM], F32, kind="ExternalInput").ap()
    wr_d = nc.dram_tensor(pfx + "wr", [DM, DM], F32, kind="ExternalInput").ap()
    wo_d = nc.dram_tensor(pfx + "wo", [DM, DM], F32, kind="ExternalInput").ap()
    cmk_d = nc.dram_tensor(pfx + "cmk", [DM, 4 * DM], F32, kind="ExternalInput").ap()
    cmr_d = nc.dram_tensor(pfx + "cmr", [DM, DM], F32, kind="ExternalInput").ap()
    cmv_d = nc.dram_tensor(pfx + "cmv", [4 * DM, DM], F32, kind="ExternalInput").ap()
    par_d = nc.dram_tensor(pfx + "par", [128, 11 * NA], F32, kind="ExternalInput").ap()

    with ExitStack() as st:
        sb = lambda n_, s, d: st.enter_context(nc.sbuf_tensor(pfx + n_, s, d))
        psb = lambda n_, s: st.enter_context(nc.psum_tensor(pfx + n_, s, F32))

        xcur = sb("xcur", [128, NA * CH], F32)
        xin = sb("xin", [128, NA * CH], F32)
        hm = sb("hm", [128, NA * CH], BF16)
        tmlead = sb("tmlead", [128, NA], F32)
        cmlead = sb("cmlead", [128, NA], F32)
        mxk = sb("mxk", [128, NA * CH], BF16)
        mxv = sb("mxv", [128, NA * CH], BF16)
        mxr = sb("mxr", [128, NA * CH], BF16)
        rw = sb("rw", [128, NA * CH], BF16)
        kcq = sb("kcq", [128, 32 * CH], BF16)
        wbufW = sb("wbufW", [128, 32768], BF16)
        par = sb("par_sb", [128, 11 * NA], F32)
        sqt = [sb(f"sqt{i}", [128, CH], F32) for i in range(2)]
        ek = sb("ek", [128, CH], F32)
        vv = sb("vv", [128, CH], F32)
        ekv = sb("ekv", [128, CH], F32)
        dtile = sb("dtile", [128, CH], F32)
        scanA = sb("scanA", [128, CH + 1], F32)
        scanB = sb("scanB", [128, CH + 1], F32)
        wkvb = None
        carA = sb("carA", [128, NA], F32)
        carB = sb("carB", [128, NA], F32)
        dcol = sb("dcol", [128, NA], F32)
        eucol = sb("eucol", [128, NA], F32)
        obuf = sb("obuf", [128, CH], F32)
        rows = sb("rows", [1, 4 * 512], F32)
        rows2 = sb("rows2", [1, 2 * 512], F32)
        rA = lambda i, W: rows[0:1, i * 512:i * 512 + W]
        rB = lambda i, W: rows2[0:1, (0 if i == 0 else 1) * 512:(0 if i == 0 else 1) * 512 + W]
        sc_b = sb("sc_b", [128, 512], F32)
        iv_b = sb("iv_b", [128, 512], F32)
        ones1 = sb("ones1", [1, 128], F32)
        onesc = sb("onesc", [128, 1], F32)

        ps_k = psb("ps_k", [128, 512])
        ps_v = psb("ps_v", [128, 512])
        ps_r = psb("ps_r", [128, 512])
        ps_o = psb("ps_o", [128, 512])
        ps_o2 = psb("ps_o2", [128, 512])
        ps_st = psb("ps_st", [128, 512])
        ps_bc = psb("ps_bc", [128, 512])

        p.tv("vector", "memset", ones1[:], 1.0, writes=("ones1",))
        p.tv("vector", "memset", onesc[:], 1.0, writes=("onesc",))
        p.dma("sync", par[:], par_d, writes=("par",))
        p.dma("sync", xo_d, x_d, writes=("xo_init",))

        PG = lambda g, a: par[:, g * NA + a:g * NA + a + 1]
        ln1g, ln1b = (lambda a: PG(0, a)), (lambda a: PG(1, a))
        ln2g, ln2b = (lambda a: PG(2, a)), (lambda a: PG(3, a))
        tmk, tmv, tmr = (lambda a: PG(4, a)), (lambda a: PG(5, a)), (lambda a: PG(6, a))
        cmkx, cmrx = (lambda a: PG(9, a)), (lambda a: PG(10, a))

        p.tv("scalar", "activation", dcol[:], par[:, 7 * NA:8 * NA], AF.Exp,
             reads=("par",), writes=("dcol",))
        p.tv("vector", "tensor_scalar", dcol[:], dcol[:], -1.0, None, ALU.mult,
             reads=("dcol",), writes=("dcol",))
        p.tv("scalar", "activation", dcol[:], dcol[:], AF.Exp, reads=("dcol",), writes=("dcol",))
        p.tv("scalar", "activation", eucol[:], par[:, 8 * NA:9 * NA], AF.Exp,
             reads=("par",), writes=("eucol",))

        def stats_seq(src, skeys, W):
            n = len(skeys)
            for a in range(n):
                sq, sk = sqt[a % 2], f"sqt{a % 2}"
                p.tv("scalar", "activation", sq[:, 0:W], src[:, a * W:(a + 1) * W],
                     AF.Square, reads=(skeys[a],), writes=(sk,))
                p.emit("tensor", lambda e, a=a, W=W, src=src, n=n: e.matmul(
                    ps_st[0:1, 0:W], onesc[:, 0:1], src[:, a * W:(a + 1) * W],
                    start=(a == 0), stop=(a == n - 1)),
                    reads=(skeys[a], "onesc"), writes=("ps_st",))
                p.emit("tensor", lambda e, a=a, W=W, sq=sq, n=n: e.matmul(
                    ps_st[32:33, 0:W], onesc[:, 0:1], sq[:, 0:W],
                    start=(a == 0), stop=(a == n - 1)),
                    reads=(sk, "onesc"), writes=("ps_st",))

        def ln_full(src, skeys, dst, dkeys, W, eps, gain=None, bias=None):
            stats_seq(src, skeys, W)
            rstd, mur = ln_rows(p, rA, rB, W, ps_st, DM, eps)
            bcast_row(p, rstd, "rA2", W, ps_bc, sc_b, "sc_b", ones1)
            bcast_row(p, mur, "rA3", W, ps_bc, iv_b, "iv_b", ones1)
            for a in range(NA):
                off = a * W
                sq, sk = sqt[a % 2], f"sqt{a % 2}"
                p.tv("vector", "tensor_mul", sq[:, 0:W], src[:, a * W:(a + 1) * W],
                     sc_b[:, 0:W], reads=(skeys[a], "sc_b"), writes=(sk,))
                if gain is None:
                    p.tv("vector", "tensor_sub", dst[:, off:off + W], sq[:, 0:W],
                         iv_b[:, 0:W], reads=(sk, "iv_b"), writes=(dkeys[a],))
                else:
                    p.tv("vector", "tensor_sub", sq[:, 0:W], sq[:, 0:W], iv_b[:, 0:W],
                         reads=(sk, "iv_b"), writes=(sk,))
                    p.tv("vector", "tensor_scalar", dst[:, off:off + W], sq[:, 0:W],
                         gain(a), bias(a), ALU.mult, ALU.add,
                         reads=(sk, "par"), writes=(dkeys[a],))

        def mixes(dsts, leadbuf, leadkey):
            for a in range(NA):
                off = a * CH
                cur1 = hm[:, off + 1:off + CH]
                shf1 = hm[:, off:off + CH - 1]
                dif = sqt[0]
                p.tv("vector", "tensor_sub", dif[:, 1:CH], cur1, shf1,
                     reads=(f"hm{a}",), writes=("sqt0",))
                p.tv("vector", "tensor_sub", dif[:, 0:1], hm[:, off:off + 1],
                     leadbuf[:, a:a + 1], reads=(f"hm{a}", leadkey), writes=("sqt0",))
                for (dst, kp, mc) in dsts:
                    p.tv("vector", "scalar_tensor_tensor", dst[:, a * CH + 1:(a + 1) * CH],
                         dif[:, 1:CH], mc(a), shf1, ALU.mult, ALU.add,
                         reads=("sqt0", "par", f"hm{a}"), writes=(f"{kp}{a}",))
                    p.tv("vector", "scalar_tensor_tensor", dst[:, a * CH:a * CH + 1],
                         dif[:, 0:1], mc(a), leadbuf[:, a:a + 1], ALU.mult, ALU.add,
                         reads=("sqt0", "par", leadkey), writes=(f"{kp}{a}",))

        def mm_group(psum, pk, lhsf, rhs, rhs_keys, n, extra_reads=()):
            fns = [(lambda e, a=a: e.matmul(psum[:, 0:CH], lhsf(a),
                                            rhs[:, a * CH:(a + 1) * CH],
                                            start=(a == 0), stop=(a == n - 1)))
                   for a in range(n)]
            p.group("tensor", fns, reads=tuple(rhs_keys) + tuple(extra_reads), writes=(pk,))

        for s in range(NSEQ):
            p.tv("vector", "memset", carA[:], 0.0, writes=("carA",))
            p.tv("vector", "memset", carB[:], 0.0, writes=("carB",))
            p.tv("vector", "memset", tmlead[:], 0.0, writes=("tmlead",))
            p.tv("vector", "memset", cmlead[:], 0.0, writes=("cmlead",))
            for c in range(4):
                base = c * CH
                for a in range(NA):
                    p.dma("sync", xcur[:, a * CH:(a + 1) * CH],
                          x_d[s * DM + a * 128:s * DM + (a + 1) * 128, base:base + CH],
                          writes=(f"xcur{a}",))
                for wi, wd in enumerate((wk_d, wv_d, wr_d, wo_d)):
                    for a in range(NA):
                        p.dma("gpsimd", wbufW[:, wi * 8192 + a * DM:wi * 8192 + (a + 1) * DM],
                              wd[a * 128:(a + 1) * 128, :], writes=("wbufW",))
                xink = [f"xin{a}" for a in range(NA)]
                ln_full(xcur, [f"xcur{a}" for a in range(NA)], xin, xink, CH, 1e-5)
                ln_full(xin, xink, hm, [f"hm{a}" for a in range(NA)], CH, 1e-3,
                        gain=ln1g, bias=ln1b)
                mixes([(mxk, "mxk", tmk), (mxv, "mxv", tmv), (mxr, "mxr", tmr)],
                      tmlead, "tmlead")
                for a in range(NA):
                    p.tv("gpsimd", "tensor_copy", tmlead[:, a:a + 1],
                         hm[:, a * CH + CH - 1:a * CH + CH],
                         reads=(f"hm{a}",), writes=("tmlead",))
                for a in range(NA):
                    mk = tuple(f"mxk{i}" for i in range(NA))
                    mv = tuple(f"mxv{i}" for i in range(NA))
                    mr = tuple(f"mxr{i}" for i in range(NA))
                    mm_group(ps_k, "ps_k",
                             lambda e_, a=a: wbufW[:, 0 * 8192 + e_ * DM + a * 128:
                                                   0 * 8192 + e_ * DM + (a + 1) * 128],
                             mxk, mk, NA, extra_reads=("wbufW",))
                    mm_group(ps_v, "ps_v",
                             lambda e_, a=a: wbufW[:, 1 * 8192 + e_ * DM + a * 128:
                                                   1 * 8192 + e_ * DM + (a + 1) * 128],
                             mxv, mv, NA, extra_reads=("wbufW",))
                    mm_group(ps_r, "ps_r",
                             lambda e_, a=a: wbufW[:, 2 * 8192 + e_ * DM + a * 128:
                                                   2 * 8192 + e_ * DM + (a + 1) * 128],
                             mxr, mr, NA, extra_reads=("wbufW",))
                    p.tv("scalar", "activation", ek[:, 0:CH], ps_k[:, 0:CH], AF.Exp,
                         reads=("ps_k",), writes=("ek",))
                    p.tv("scalar", "copy", vv[:, 0:CH], ps_v[:, 0:CH],
                         reads=("ps_v",), writes=("vv",))
                    p.tv("vector", "tensor_mul", ekv[:, 0:CH], ek[:, 0:CH], vv[:, 0:CH],
                         reads=("ek", "vv"), writes=("ekv",))
                    p.tv("vector", "memset", dtile[:, 0:CH], 1.0, writes=("dtile",))
                    p.tv("vector", "tensor_scalar", dtile[:, 0:CH], dtile[:, 0:CH],
                         dcol[:, a:a + 1], None, ALU.mult,
                         reads=("dtile", "dcol"), writes=("dtile",))
                    p.tv("vector", "tensor_copy", scanA[:, 0:1], carA[:, a:a + 1],
                         reads=("carA",), writes=("scanA",))
                    p.tv("vector", "tensor_tensor_scan", scanA[:, 1:CH + 1], dtile[:, 0:CH],
                         ekv[:, 0:CH], carA[:, a:a + 1], ALU.mult, ALU.add,
                         reads=("dtile", "ekv", "carA"), writes=("scanA",))
                    p.tv("vector", "tensor_copy", scanB[:, 0:1], carB[:, a:a + 1],
                         reads=("carB",), writes=("scanB",))
                    p.tv("vector", "tensor_tensor_scan", scanB[:, 1:CH + 1], dtile[:, 0:CH],
                         ek[:, 0:CH], carB[:, a:a + 1], ALU.mult, ALU.add,
                         reads=("dtile", "ek", "carB"), writes=("scanB",))
                    p.tv("vector", "tensor_copy", carA[:, a:a + 1], scanA[:, CH:CH + 1],
                         reads=("scanA",), writes=("carA",))
                    p.tv("vector", "tensor_copy", carB[:, a:a + 1], scanB[:, CH:CH + 1],
                         reads=("scanB",), writes=("carB",))
                    p.tv("vector", "tensor_scalar", ek[:, 0:CH], ek[:, 0:CH],
                         eucol[:, a:a + 1], None, ALU.mult,
                         reads=("ek", "eucol"), writes=("ek",))
                    p.tv("vector", "tensor_mul", ekv[:, 0:CH], ek[:, 0:CH], vv[:, 0:CH],
                         reads=("ek", "vv"), writes=("ekv",))
                    p.tv("vector", "tensor_add", dtile[:, 0:CH], scanB[:, 0:CH], ek[:, 0:CH],
                         reads=("scanB", "ek", "dtile"), writes=("dtile",))
                    p.tv("vector", "reciprocal", dtile[:, 0:CH], dtile[:, 0:CH],
                         reads=("dtile",), writes=("dtile",))
                    p.tv("vector", "tensor_add", ekv[:, 0:CH], scanA[:, 0:CH], ekv[:, 0:CH],
                         reads=("scanA", "ekv"), writes=("ekv",))
                    p.tv("vector", "tensor_mul", ekv[:, 0:CH], ekv[:, 0:CH], dtile[:, 0:CH],
                         reads=("ekv", "dtile"), writes=("ekv",))
                    p.tv("scalar", "activation", ek[:, 0:CH], ps_r[:, 0:CH], AF.Sigmoid,
                         reads=("ps_r",), writes=("ek",))
                    p.tv("vector", "tensor_mul", rw[:, a * CH:(a + 1) * CH], ek[:, 0:CH],
                         ekv[:, 0:CH], reads=("ek", "ekv"), writes=(f"rw{a}",))
                for ob in range(NA):
                    ps, pk = (ps_o, "ps_o") if ob % 2 == 0 else (ps_o2, "ps_o2")
                    mm_group(ps, pk,
                             lambda e_, ob=ob: wbufW[:, 3 * 8192 + e_ * DM + ob * 128:
                                                     3 * 8192 + e_ * DM + (ob + 1) * 128],
                             rw, tuple(f"rw{i}" for i in range(NA)), NA,
                             extra_reads=("wbufW",))
                    p.tv("vector", "tensor_add", xin[:, ob * CH:(ob + 1) * CH],
                         ps[:, 0:CH], xin[:, ob * CH:(ob + 1) * CH],
                         reads=(pk, f"xin{ob}"), writes=(f"xin{ob}",))
                ln_full(xin, xink, hm, [f"hm{a}" for a in range(NA)], CH, 1e-3,
                        gain=ln2g, bias=ln2b)
                mixes([(mxk, "mxk", cmkx), (mxr, "mxr", cmrx)], cmlead, "cmlead")
                for a in range(NA):
                    p.tv("gpsimd", "tensor_copy", cmlead[:, a:a + 1],
                         hm[:, a * CH + CH - 1:a * CH + CH],
                         reads=(f"hm{a}",), writes=("cmlead",))
                for a in range(NA):
                    p.dma("gpsimd", wbufW[:, a * 4096:(a + 1) * 4096],
                          cmk_d[a * 128:(a + 1) * 128, :], writes=("wbufW",))
                for ob in range(32):
                    ps, pk = (ps_o, "ps_o") if ob % 2 == 0 else (ps_o2, "ps_o2")
                    mm_group(ps, pk,
                             lambda e_, ob=ob: wbufW[:, e_ * 4096 + ob * 128:
                                                     e_ * 4096 + (ob + 1) * 128],
                             mxk, tuple(f"mxk{i}" for i in range(NA)), NA,
                             extra_reads=("wbufW",))
                    sq, sk = sqt[ob % 2], f"sqt{ob % 2}"
                    p.tv("scalar", "activation", sq[:, 0:CH], ps[:, 0:CH], AF.Relu,
                         reads=(pk,), writes=(sk,))
                    p.tv("scalar", "activation", kcq[:, ob * CH:(ob + 1) * CH], sq[:, 0:CH],
                         AF.Square, reads=(sk,), writes=(f"kcq{ob}",))
                for a in range(NA):
                    p.dma("gpsimd", wbufW[:, a * DM:(a + 1) * DM],
                          cmr_d[a * 128:(a + 1) * 128, :], writes=("wbufW",))
                for ob in range(NA):
                    ps, pk = (ps_o, "ps_o") if ob % 2 == 0 else (ps_o2, "ps_o2")
                    mm_group(ps, pk,
                             lambda e_, ob=ob: wbufW[:, e_ * DM + ob * 128:
                                                     e_ * DM + (ob + 1) * 128],
                             mxr, tuple(f"mxr{i}" for i in range(NA)), NA,
                             extra_reads=("wbufW",))
                    p.tv("scalar", "activation", xcur[:, ob * CH:(ob + 1) * CH], ps[:, 0:CH],
                         AF.Sigmoid, reads=(pk,), writes=(f"xcur{ob}",))
                for k in range(32):
                    p.dma("gpsimd", wbufW[:, k * DM:(k + 1) * DM],
                          cmv_d[k * 128:(k + 1) * 128, :], writes=("wbufW",))
                for ob in range(NA):
                    ps, pk = (ps_o, "ps_o") if ob % 2 == 0 else (ps_o2, "ps_o2")
                    fns = [(lambda e, k=k, ob=ob, ps=ps: e.matmul(
                        ps[:, 0:CH], wbufW[:, k * DM + ob * 128:k * DM + (ob + 1) * 128],
                        kcq[:, k * CH:(k + 1) * CH], start=(k == 0), stop=(k == 31)))
                        for k in range(32)]
                    p.group("tensor", fns,
                            reads=("wbufW",) + tuple(f"kcq{k}" for k in range(32)),
                            writes=(pk,))
                    p.tv("vector", "tensor_mul", obuf[:, 0:CH], ps[:, 0:CH],
                         xcur[:, ob * CH:(ob + 1) * CH], reads=(pk, f"xcur{ob}"),
                         writes=("obuf",))
                    p.tv("vector", "tensor_add", obuf[:, 0:CH], obuf[:, 0:CH],
                         xin[:, ob * CH:(ob + 1) * CH], reads=("obuf", f"xin{ob}"),
                         writes=("obuf",))
                    p.dma("gpsimd", xo_d[s * DM + ob * 128:s * DM + (ob + 1) * 128, base:base + CH],
                          obuf[:, 0:CH], reads=("obuf", "xo_init"),
                          writes=(f"xo{s}_{c}_{ob}",), accum_op=ALU.add)
    return


def run_l1attn(inputs, x_current, cores=8):
    if "l1attn" not in _cache:
        _cache["l1attn"] = build_l1attn()
    nc = _cache["l1attn"]
    f32 = lambda v: np.asarray(v, dtype=np.float32)
    ctile = lambda v: np.ascontiguousarray(f32(v).reshape(8, 128).T)
    par = np.concatenate([
        ctile(inputs["L1_attn_ln1_g"]), ctile(inputs["L1_attn_ln1_b"]),
        ctile(inputs["L1_attn_ln2_g"]), ctile(inputs["L1_attn_ln2_b"]),
        ctile(inputs["L1_attn_tm_mix_k"]), ctile(inputs["L1_attn_tm_mix_v"]),
        ctile(inputs["L1_attn_tm_mix_r"]), ctile(inputs["L1_attn_time_decay"]),
        ctile(inputs["L1_attn_time_first"]), ctile(inputs["L1_attn_cm_mix_k"]),
        ctile(inputs["L1_attn_cm_mix_r"])], axis=1)
    w = {
        "wk": np.ascontiguousarray(f32(inputs["L1_attn_tm_Wk"])),
        "wv": np.ascontiguousarray(f32(inputs["L1_attn_tm_Wv"])),
        "wr": np.ascontiguousarray(f32(inputs["L1_attn_tm_Wr"])),
        "wo": np.ascontiguousarray(f32(inputs["L1_attn_Wo"])),
        "cmk": np.ascontiguousarray(f32(inputs["L1_attn_cm_Wk"])),
        "cmr": np.ascontiguousarray(f32(inputs["L1_attn_cm_Wr"])),
        "cmv": np.ascontiguousarray(f32(inputs["L1_attn_cm_Wv"])),
        "par": np.ascontiguousarray(par),
    }
    in_maps = [dict(w, x_res=_shard_x(x_current, c)) for c in range(cores)]
    res = run_bass_kernel_spmd(nc, in_maps, core_ids=list(range(cores)))
    return _unshard([r for r in res.results])



DEBUG_L0ATTN = False


def emit_l0attn(p, x_d, xo_d, pfx="a_"):
    nc = p.nc
    CW = 512
    DI = 2048
    NXT = 16
    NBC = 17
    SC = 64

    ipT_d = nc.dram_tensor(pfx + "ipT", [DM, 4256], F32, kind="ExternalInput").ap()
    opT_d = nc.dram_tensor(pfx + "opT", [DI, DM], F32, kind="ExternalInput").ap()
    convw_d = nc.dram_tensor(pfx + "convw", [128, NBC * 4], F32, kind="ExternalInput").ap()
    convb_d = nc.dram_tensor(pfx + "convb", [128, NBC], F32, kind="ExternalInput").ap()
    lnwb_d = nc.dram_tensor(pfx + "lnwb", [128, 2 * NA], F32, kind="ExternalInput").ap()
    normw_d = nc.dram_tensor(pfx + "normw", [128, NXT], F32, kind="ExternalInput").ap()
    hpar_d = nc.dram_tensor(pfx + "hpar", [32, 3], F32, kind="ExternalInput").ap()
    ecst_d = nc.dram_tensor(pfx + "ecst", [32, DI], F32, kind="ExternalInput").ap()
    triu_d = nc.dram_tensor(pfx + "triu", [64, 64], BF16, kind="ExternalInput").ap()
    ecst2_d = nc.dram_tensor(pfx + "ecst2", [64, 2048], BF16, kind="ExternalInput").ap()
    idb_d = nc.dram_tensor(pfx + "identb16", [128, 128], BF16, kind="ExternalInput").ap()
    scr3_d = nc.dram_tensor(pfx + "scr3", [1, DI], F32, kind="Internal").ap()
    if DEBUG_L0ATTN:
        dbg_cum = nc.dram_tensor(pfx + "dbg_cum", [32, 512], F32, kind="ExternalOutput").ap()
        dbg_dt = nc.dram_tensor(pfx + "dbg_dt", [32, 512], F32, kind="ExternalOutput").ap()
        dbg_xc = nc.dram_tensor(pfx + "dbg_xc", [128, 3 * 512], F32, kind="ExternalOutput").ap()
        dbg_w = nc.dram_tensor(pfx + "dbg_w", [64, 2048], F32, kind="ExternalOutput").ap()
        dbg_y = nc.dram_tensor(pfx + "dbg_y", [128, 2048], F32, kind="ExternalOutput").ap()
        dbg_h = nc.dram_tensor(pfx + "dbg_h", [64, 2048], F32, kind="ExternalOutput").ap()

    with ExitStack() as st:
        sb = lambda n_, s, d: st.enter_context(nc.sbuf_tensor(pfx + n_, s, d))
        xcur = sb("xcur", [128, NA * CW], F32)
        xb = sb("xb", [128, NA * CW], F32)
        xn = sb("xn", [128, NA * CW], BF16)
        zbuf = sb("zbuf", [128, 4 * DI], BF16)
        xbcb = sb("xbcb", [128, NBC * (CW + 3)], BF16)
        xcnv = sb("xcnv", [128, NBC * CW], BF16)
        wbuf = sb("wbuf", [128, 8192], BF16)
        ygf = sb("ygf", [128, NXT * CW], BF16)
        dtb = sb("dtb", [32, CW], F32)
        dt2 = sb("dt2", [32, CW], F32)
        cum = sb("cum", [32, CW], F32)
        e2s = sb("e2s", [32, CW], F32)
        msk = sb("msk", [32, CW], BF16)
        hpar = sb("hpar_sb", [32, 3], F32)
        acol = sb("acol", [32, 1], F32)
        ecst = sb("ecst_sb", [32, DI], F32)
        triu = sb("triu_sb", [64, 64], BF16)
        ecst2 = sb("ecst2_sb", [64, 2048], BF16)
        dt3 = sb("dt3", [32, 64], F32)
        idb = sb("idb_sb", [128, 128], BF16)
        gms = sb("gms", [64, 64], BF16)
        cvb = sb("cvb", [64, 64], BF16)
        xtr = sb("xtr", [64, DI], BF16)
        xtp = sb("xtp", [64, DI], BF16)
        hb16 = sb("hb16", [64, DI], BF16)
        h32 = sb("h32", [64, DI], F32)
        cdx = sb("cdx", [64, DI], F32)
        ypair = sb("ypair", [128, DI], F32)
        ztmp = sb("ztmp", [128, DI], BF16)
        dskb = sb("dskb", [64, DI], BF16)
        rsq = sb("rsq", [128, 1], F32)
        rsq2 = sb("rsq2", [128, 1], F32)
        rsq3 = None
        convw = sb("convw_sb", [128, NBC * 4], F32)
        convb = sb("convb_sb", [128, NBC], F32)
        lnwb = sb("lnwb_sb", [128, 2 * NA], F32)
        normw = sb("normw_sb", [128, NXT], F32)
        sqt = [sb("sqt0", [128, CW], F32)] * 2
        SQK = ["sqt0", "sqt0"]
        rows = sb("rows", [1, 4 * 512], F32)
        rows2 = sb("rows2", [1, 2 * 512], F32)
        rA = lambda i, W: rows[0:1, i * 512:i * 512 + W]
        rB = lambda i, W: rows2[0:1, (0 if i == 0 else 1) * 512:(0 if i == 0 else 1) * 512 + W]
        sc_b = sb("sc_b", [128, 512], F32)
        iv_b = sb("iv_b", [128, 512], F32)
        ones1 = sb("ones1", [1, 128], F32)
        onesc = sb("onesc", [128, 1], F32)

        psb = lambda n_, s, d=F32: st.enter_context(nc.psum_tensor(pfx + n_, s, d))
        ps_y = psb("ps_y", [64, DI])
        ps_st = psb("ps_st", [128, 512])
        ps_bc = psb("ps_bc", [128, 512])
        ps_a = psb("ps_a", [128, 512])
        ps_tr = psb("ps_tr", [128, 128], BF16)

        p.tv("vector", "memset", ones1[:], 1.0, writes=("ones1",))
        p.tv("vector", "memset", onesc[:], 1.0, writes=("onesc",))
        p.dma("sync", hpar[:], hpar_d, writes=("hpar",))
        p.dma("sync", ecst[:], ecst_d, writes=("ecst",))
        p.dma("sync", triu[:], triu_d, writes=("triu",))
        p.dma("sync", ecst2[:], ecst2_d, writes=("ecst2",))
        p.dma("sync", idb[:], idb_d, writes=("idb",))
        p.dma("sync", convw[:], convw_d, writes=("convw",))
        p.dma("sync", convb[:], convb_d, writes=("convb",))
        p.dma("sync", lnwb[:], lnwb_d, writes=("lnwb",))
        p.dma("sync", normw[:], normw_d, writes=("normw",))
        p.dma("sync", xo_d, x_d, writes=("xo_init",))
        p.tv("scalar", "activation", acol[:], hpar[:, 1:2], AF.Exp,
             reads=("hpar",), writes=("acol",))
        p.tv("vector", "tensor_scalar", acol[:], acol[:], -1.0, None, ALU.mult,
             reads=("acol",), writes=("acol",))
        p.tv("vector", "memset", msk[:], 1.0, writes=("msk",))
        for j in range(8):
            p.tv("vector", "memset", msk[:, j * SC:j * SC + 1], 0.0, writes=("msk",))
        for j in range(4):
            p.emit("tensor", lambda e, j=j: e.matmul(
                ps_a[0:1, 0:512], hpar[:, 2:3], ecst[:, j * 512:(j + 1) * 512],
                start=True, stop=True), reads=("hpar", "ecst"), writes=("ps_a",))
            p.tv("vector", "tensor_copy", rA(j, 512), ps_a[0:1, 0:512],
                 reads=("ps_a",), writes=(f"rA{j}",))
        p.dma("sync", scr3_d, rows[0:1, 0:DI], reads=("rA0", "rA1", "rA2", "rA3"),
              writes=("scr3",))
        p.dma("gpsimd", dskb[:], scr3_d.partition_broadcast(64), reads=("scr3",),
              writes=("dskb",))

        lnw = lambda a: lnwb[:, a:a + 1]
        lnb = lambda a: lnwb[:, NA + a:NA + a + 1]

        def stats_seq(src, skeys, W):
            n = len(skeys)
            for a in range(n):
                sq, sk = sqt[a % 2], "sqt0"
                p.tv("scalar", "activation", sq[:, 0:W], src[:, a * W:(a + 1) * W],
                     AF.Square, reads=(skeys[a],), writes=(sk,))
                p.emit("tensor", lambda e, a=a, W=W, src=src, n=n: e.matmul(
                    ps_st[0:1, 0:W], onesc[:, 0:1], src[:, a * W:(a + 1) * W],
                    start=(a == 0), stop=(a == n - 1)),
                    reads=(skeys[a], "onesc"), writes=("ps_st",))
                p.emit("tensor", lambda e, a=a, W=W, sq=sq, n=n: e.matmul(
                    ps_st[32:33, 0:W], onesc[:, 0:1], sq[:, 0:W],
                    start=(a == 0), stop=(a == n - 1)),
                    reads=(sk, "onesc"), writes=("ps_st",))

        def ln_full(src, skeys, dst, dkeys, W, eps, gain=None, bias=None):
            stats_seq(src, skeys, W)
            rstd, mur = ln_rows(p, rA, rB, W, ps_st, DM, eps)
            bcast_row(p, rstd, "rA2", W, ps_bc, sc_b, "sc_b", ones1)
            bcast_row(p, mur, "rA3", W, ps_bc, iv_b, "iv_b", ones1)
            for a in range(NA):
                sq, sk = sqt[a % 2], "sqt0"
                p.tv("vector", "tensor_mul", sq[:, 0:W], src[:, a * W:(a + 1) * W],
                     sc_b[:, 0:W], reads=(skeys[a], "sc_b"), writes=(sk,))
                if gain is None:
                    p.tv("vector", "tensor_sub", dst[:, a * W:(a + 1) * W], sq[:, 0:W],
                         iv_b[:, 0:W], reads=(sk, "iv_b"), writes=(dkeys[a],))
                else:
                    p.tv("vector", "tensor_sub", sq[:, 0:W], sq[:, 0:W], iv_b[:, 0:W],
                         reads=(sk, "iv_b"), writes=(sk,))
                    p.tv("vector", "tensor_scalar", dst[:, a * W:(a + 1) * W], sq[:, 0:W],
                         gain(a), bias(a), ALU.mult, ALU.add,
                         reads=(sk, "lnwb"), writes=(dkeys[a],))

        BV = lambda off: xcnv[:, 16 * CW + off:16 * CW + off + SC][0:64, :]
        CV = lambda off: xcnv[:, 16 * CW + off:16 * CW + off + SC][64:128, :]

        for s in range(NSEQ):
            p.tv("vector", "memset", h32[:], 0.0, writes=("h32",))
            p.tv("vector", "memset", hb16[:], 0.0, writes=("hb16",))
            for a in range(NBC):
                p.tv("vector", "memset", xbcb[:, a * (CW + 3):a * (CW + 3) + 3], 0.0,
                     writes=(f"xl{a}",))
            for c in range(4):
                base = c * CW
                for a in range(NA):
                    p.dma("sync", xcur[:, a * CW:(a + 1) * CW],
                          x_d[s * DM + a * 128:s * DM + (a + 1) * 128, base:base + CW],
                          writes=(f"xcur{a}",))
                ln_full(xcur, [f"xcur{a}" for a in range(NA)], xb,
                        [f"xb{a}" for a in range(NA)], CW, 1e-5)
                ln_full(xb, [f"xb{a}" for a in range(NA)], xn,
                        [f"xn{a}" for a in range(NA)], CW, 1e-5, gain=lnw, bias=lnb)
                xnk = tuple(f"xn{a}" for a in range(NA))
                for half, (o0, nob, wdth) in enumerate(((0, 7, 896), (7, 7, 896), (14, 4, 416))):
                    for a in range(NA):
                        p.dma("gpsimd", wbuf[:, a * wdth:a * wdth + wdth],
                              ipT_d[a * 128:(a + 1) * 128,
                                    2048 + o0 * 128:2048 + o0 * 128 + wdth],
                              writes=("wbuf",))
                    for oi in range(nob):
                        ob = o0 + oi
                        if ob >= NBC:
                            wid = 32
                        else:
                            wid = 128
                        if ob > NBC:
                            continue
                        ps, pk = (ps_a, "ps_a") if oi % 2 == 0 else (ps_bc, "ps_bc")
                        fns = [(lambda e, a=a, oi=oi, ps=ps, wid=wid, wdth=wdth: e.matmul(
                            ps[0:wid, 0:CW], wbuf[:, a * wdth + oi * 128:a * wdth + oi * 128 + wid],
                            xn[:, a * CW:(a + 1) * CW], start=(a == 0), stop=(a == NA - 1)))
                            for a in range(NA)]
                        p.group("tensor", fns, reads=("wbuf",) + xnk, writes=(pk,))
                        if ob < NBC:
                            off = ob * (CW + 3)
                            p.tv("vector", "tensor_copy", xbcb[:, off + 3:off + 3 + CW],
                                 ps[:, 0:CW], reads=(pk,), writes=(f"xc{ob}",))
                        else:
                            p.tv("vector", "tensor_scalar", dtb[:, 0:CW], ps[0:32, 0:CW],
                                 1.0, hpar[0:32, 0:1], ALU.mult, ALU.add,
                                 reads=(pk, "hpar"), writes=("dtb",))
                for half in range(2):
                    for a in range(NA):
                        p.dma("gpsimd", wbuf[:, a * 1024:(a + 1) * 1024],
                              ipT_d[a * 128:(a + 1) * 128, half * 1024:(half + 1) * 1024],
                              writes=("wbuf",))
                    for tb in range(4):
                        for zc in range(2):
                            ps, pk = (ps_a, "ps_a") if zc % 2 == 0 else (ps_bc, "ps_bc")
                            fns = [(lambda e, a=a, tb=tb, zc=zc, ps=ps: e.matmul(
                                ps[:, 0:512],
                                xn[:, a * CW + tb * 128:a * CW + (tb + 1) * 128],
                                wbuf[:, a * 1024 + zc * 512:a * 1024 + (zc + 1) * 512],
                                start=(a == 0), stop=(a == NA - 1)))
                                for a in range(NA)]
                            p.group("tensor", fns, reads=("wbuf",) + xnk, writes=(pk,))
                            p.tv("vector", "tensor_copy",
                                 zbuf[:, tb * DI + half * 1024 + zc * 512:
                                      tb * DI + half * 1024 + (zc + 1) * 512],
                                 ps[:, 0:512], reads=(pk,), writes=("ZB",))
                for a in range(NBC):
                    off = a * (CW + 3)
                    sq, sk = sqt[a % 2], "sqt0"
                    p.tv("vector", "tensor_scalar", sq[:, 0:CW], xbcb[:, off:off + CW],
                         convw[:, 4 * a:4 * a + 1], None, ALU.mult,
                         reads=(f"xc{a}", f"xl{a}", "convw"), writes=(sk,))
                    for j in range(1, 4):
                        p.tv("vector", "scalar_tensor_tensor", sq[:, 0:CW],
                             xbcb[:, off + j:off + j + CW], convw[:, 4 * a + j:4 * a + j + 1],
                             sq[:, 0:CW], ALU.mult, ALU.add,
                             reads=(f"xc{a}", f"xl{a}", "convw", sk), writes=(sk,))
                    p.tv("gpsimd", "tensor_copy", xbcb[:, off:off + 3],
                         xbcb[:, off + CW:off + CW + 3], reads=(f"xc{a}",),
                         writes=(f"xl{a}",))
                    p.emit("scalar", lambda e, a=a, sq=sq: e.activation(
                        xcnv[:, a * CW:(a + 1) * CW], sq[:, 0:CW], AF.Silu,
                        bias=convb[:, a:a + 1]), reads=(sk, "convb"), writes=(f"cv{a}",))
                p.tv("scalar", "activation", dt2[:], dtb[:], AF.Abs, reads=("dtb",), writes=("dt2",))
                p.tv("scalar", "activation", dt2[:], dt2[:], AF.Exp, scale=-1.0,
                     reads=("dt2",), writes=("dt2",))
                p.tv("scalar", "activation", dt2[:], dt2[:], AF.Ln, bias=1.0,
                     reads=("dt2",), writes=("dt2",))
                p.tv("vector", "tensor_scalar", dtb[:], dtb[:], 0.0, None, ALU.max,
                     reads=("dtb",), writes=("dtb",))
                p.tv("vector", "tensor_add", dtb[:], dtb[:], dt2[:],
                     reads=("dtb", "dt2"), writes=("dtb",))
                p.tv("vector", "tensor_scalar", dt2[:], dtb[:], acol[0:32, 0:1], None,
                     ALU.mult, reads=("dtb", "acol"), writes=("dt2",))
                p.tv("vector", "tensor_tensor_scan", cum[:], msk[:], dt2[:], 0.0,
                     ALU.mult, ALU.add, reads=("msk", "dt2"), writes=("cum",))
                if DEBUG_L0ATTN and s == 0 and c == 0:
                    p.dma("sync", dbg_cum, cum[:], reads=("cum",), writes=("dbgcum",))
                    p.dma("sync", dbg_dt, dtb[:], reads=("dtb",), writes=("dbgdt",))
                    p.tv("vector", "tensor_copy", sqt[0][:, 0:CW], xcnv[:, 0:CW],
                         reads=("cv0",), writes=("sqt0",))
                    p.dma("sync", dbg_xc[:, 0:512], sqt[0][:, 0:CW], reads=("sqt0",), writes=("dbgxc",))
                    p.tv("vector", "tensor_copy", sqt[0][:, 0:CW], xcnv[:, CW:2 * CW],
                         reads=("cv1",), writes=("sqt0",))
                    p.dma("sync", dbg_xc[:, 512:1024], sqt[0][:, 0:CW], reads=("sqt0",), writes=("dbgxc",))
                    p.tv("vector", "tensor_copy", sqt[0][:, 0:CW], xcnv[:, 16 * CW:17 * CW],
                         reads=("cv16",), writes=("sqt0",))
                    p.dma("sync", dbg_xc[:, 1024:1536], sqt[0][:, 0:CW], reads=("sqt0",), writes=("dbgxc",))
                p.tv("scalar", "activation", e2s[:], cum[:], AF.Exp,
                     reads=("cum",), writes=("e2s",))
                p.tv("scalar", "activation", dt2[:], dtb[:], AF.Ln,
                     reads=("dtb",), writes=("dt2",))
                p.tv("vector", "tensor_sub", dtb[:], dt2[:], cum[:],
                     reads=("dt2", "cum"), writes=("dtb",))
                for cc in range(8):
                    off = cc * SC
                    for tk in range(NXT):
                        p.tv("tensor", "transpose", ps_tr[0:64, 0:128],
                             xcnv[:, tk * CW + off:tk * CW + off + SC], idb[:],
                             reads=(f"cv{tk}", "idb"), writes=("ps_tr",))
                        p.tv("vector", "tensor_copy", xtr[:, tk * 128:(tk + 1) * 128],
                             ps_tr[0:64, 0:128], reads=("ps_tr",), writes=("xtr",))
                    p.tv("scalar", "activation", dt3[:], dtb[0:32, off:off + SC], AF.Exp,
                         bias=cum[0:32, off + SC - 1:off + SC], reads=("dtb", "cum"),
                         writes=("dt3",))
                    for j in range(4):
                        p.emit("tensor", lambda e, j=j: e.matmul(
                            ps_y[:, j * 512:(j + 1) * 512], dt3[:],
                            ecst[:, j * 512:(j + 1) * 512], start=True, stop=True),
                            reads=("dt3", "ecst"), writes=("ps_y",))
                    p.tv("vector", "tensor_mul", xtp[:], xtr[:], ps_y[0:64, :],
                         reads=("xtr", "ps_y"), writes=("xtp",))
                    p.dma("sync", cvb[:], CV(off), reads=("cv16",), writes=("cvb",))
                    p.emit("tensor", lambda e, off=off: e.matmul(
                        ps_a[0:64, 0:64], cvb[:], BV(off), start=True, stop=True),
                        reads=("cv16", "cvb"), writes=("ps_a",))
                    p.tv("vector", "tensor_tensor", gms[:], ps_a[0:64, 0:64], triu[:],
                         ALU.mult, reads=("ps_a", "triu"), writes=("gms",))
                    for j in range(4):
                        p.emit("tensor", lambda e, j=j, off=off: e.matmul(
                            ps_a[0:64, 0:512], e2s[0:32, off:off + SC],
                            ecst[:, j * 512:(j + 1) * 512], start=True, stop=True),
                            reads=("e2s", "ecst"), writes=("ps_a",))
                        p.tv("vector", "tensor_copy", ztmp[0:64, j * 512:(j + 1) * 512],
                             ps_a[0:64, 0:512], reads=("ps_a",), writes=("ztmp",))
                    for j in range(4):
                        p.emit("tensor", lambda e, j=j: e.matmul(
                            ps_y[:, j * 512:(j + 1) * 512], cvb[:],
                            hb16[:, j * 512:(j + 1) * 512], start=True, stop=True),
                            reads=("cvb", "hb16"), writes=("ps_y",))
                    half = 64 * (cc % 2)
                    p.tv("vector", "tensor_mul", ypair[half:half + 64, :], ps_y[0:64, :],
                         ztmp[0:64, :], reads=("ps_y", "ztmp"), writes=("ypair",))
                    p.dma("sync", scr3_d.rearrange("a (h l) -> (a h) l", l=SC),
                          cum[0:32, off:off + SC], reads=("cum",), writes=("scr3",))
                    p.dma("sync", rows[0:1, 0:DI], scr3_d[0:1, 0:DI],
                          reads=("scr3",), writes=("rA0", "rA1", "rA2", "rA3"))
                    for j in range(4):
                        p.emit("tensor", lambda e, j=j: e.matmul(
                            ps_y[0:64, j * 512:(j + 1) * 512], ones1[0:1, 0:64],
                            rows[0:1, j * 512:(j + 1) * 512], start=True, stop=False),
                            reads=("rA0", "rA1", "rA2", "rA3", "ones1"), writes=("ps_y",))
                        p.emit("tensor", lambda e, j=j, off=off: e.matmul(
                            ps_y[0:64, j * 512:(j + 1) * 512], dtb[0:32, off:off + SC],
                            ecst[:, j * 512:(j + 1) * 512], start=False, stop=True),
                            reads=("dtb", "ecst"), writes=("ps_y",))
                    p.tv("vector", "tensor_scalar", cdx[:], ps_y[0:64, :], 80.0, None,
                         ALU.min, reads=("ps_y",), writes=("cdx",))
                    p.tv("scalar", "activation", ztmp[0:64, :], cdx[:], AF.Exp,
                         reads=("cdx",), writes=("ztmp",))
                    for j in range(4):
                        p.emit("tensor", lambda e, j=j: e.matmul(
                            ps_y[0:64, j * 512:(j + 1) * 512], gms[:],
                            ecst2[:, j * 512:(j + 1) * 512], start=True, stop=True),
                            reads=("gms", "ecst2"), writes=("ps_y",))
                    p.tv("vector", "tensor_mul", ztmp[0:64, :], ztmp[0:64, :],
                         ps_y[0:64, :], reads=("ztmp", "ps_y"), writes=("ztmp",))
                    for h in range(32):
                        p.emit("tensor", lambda e, h=h: e.matmul(
                            ps_y[:, h * SC:(h + 1) * SC], ztmp[0:64, h * SC:(h + 1) * SC],
                            xtr[:, h * SC:(h + 1) * SC], start=True, stop=True),
                            reads=("ztmp", "xtr"), writes=("ps_y",))
                    p.tv("vector", "tensor_add", ypair[half:half + 64, :],
                         ypair[half:half + 64, :], ps_y[0:64, :],
                         reads=("ypair", "ps_y"), writes=("ypair",))
                    p.tv("vector", "tensor_mul", ztmp[half:half + 64, :], xtr[:], dskb[:],
                         reads=("xtr", "dskb"), writes=("ztmp",))
                    p.tv("vector", "tensor_add", ypair[half:half + 64, :],
                         ypair[half:half + 64, :], ztmp[half:half + 64, :],
                         reads=("ypair", "ztmp"), writes=("ypair",))
                    p.tv("tensor", "transpose", ps_tr[0:64, 0:64], BV(off),
                         idb[0:64, 0:64], reads=("cv16", "idb"), writes=("ps_tr",))
                    p.tv("vector", "tensor_copy", gms[:], ps_tr[0:64, 0:64],
                         reads=("ps_tr",), writes=("gms",))
                    for j in range(4):
                        p.emit("tensor", lambda e, j=j: e.matmul(
                            ps_y[:, j * 512:(j + 1) * 512], gms[:],
                            xtp[:, j * 512:(j + 1) * 512], start=True, stop=True),
                            reads=("gms", "xtp"), writes=("ps_y",))
                    for j in range(4):
                        p.emit("tensor", lambda e, j=j, off=off: e.matmul(
                            ps_a[0:1, 0:512], e2s[0:32, off + SC - 1:off + SC],
                            ecst[:, j * 512:(j + 1) * 512], start=True, stop=True),
                            reads=("e2s", "ecst"), writes=("ps_a",))
                        p.tv("vector", "tensor_copy", rA(j, 512), ps_a[0:1, 0:512],
                             reads=("ps_a",), writes=(f"rA{j}",))
                    p.dma("sync", scr3_d, rows[0:1, 0:DI],
                          reads=("rA0", "rA1", "rA2", "rA3"), writes=("scr3",))
                    p.dma("gpsimd", cdx[:], scr3_d.partition_broadcast(64),
                          reads=("scr3",), writes=("cdx",))
                    p.tv("vector", "tensor_mul", h32[:], h32[:], cdx[:],
                         reads=("h32", "cdx"), writes=("h32",))
                    p.tv("vector", "tensor_add", h32[:], h32[:], ps_y[0:64, :],
                         reads=("h32", "ps_y"), writes=("h32",))
                    p.tv("vector", "tensor_copy", hb16[:], h32[:], reads=("h32",),
                         writes=("hb16",))
                    if DEBUG_L0ATTN and s == 0 and c == 0 and cc == 1:
                        p.dma("sync", dbg_y, ypair[:], reads=("ypair",), writes=("dbgy",))
                        p.tv("vector", "tensor_copy", cdx[:], h32[:],
                             reads=("h32",), writes=("cdx",))
                        p.dma("sync", dbg_h, cdx[:], reads=("cdx",), writes=("dbgh",))
                    if cc % 2 == 1:
                        pair = cc // 2
                        zp = zbuf[:, pair * DI:(pair + 1) * DI]
                        p.tv("scalar", "activation", ztmp[:], zp, AF.Silu,
                             reads=("ZB",), writes=("ztmp",))
                        p.tv("vector", "tensor_mul", ypair[:], ypair[:], ztmp[:],
                             reads=("ypair", "ztmp"), writes=("ypair",))
                        p.tv("vector", "scalar_tensor_tensor", ztmp[:], ypair[:], 1.0,
                             ypair[:], ALU.mult, ALU.mult, accum_out=rsq[:],
                             reads=("ypair",), writes=("ztmp", "rsq"))
                        p.tv("vector", "tensor_scalar", rsq2[:], rsq[:], 1.0 / DI, 1e-5,
                             ALU.mult, ALU.add, reads=("rsq",), writes=("rsq2",))
                        p.tv("vector", "reciprocal", rsq[:], rsq2[:], reads=("rsq2",),
                             writes=("rsq",))
                        p.tv("scalar", "activation", rsq[:], rsq[:], AF.Sqrt,
                             reads=("rsq",), writes=("rsq",))
                        nt_ = sc_b[:, 0:1]
                        for _ in range(2):
                            p.tv("vector", "tensor_mul", nt_, rsq[:], rsq[:],
                                 reads=("rsq",), writes=("sc_b",))
                            p.tv("vector", "tensor_mul", nt_, rsq2[:], nt_,
                                 reads=("rsq2", "sc_b"), writes=("sc_b",))
                            p.tv("vector", "tensor_scalar", nt_, nt_, -0.5, 1.5,
                                 ALU.mult, ALU.add, reads=("sc_b",), writes=("sc_b",))
                            p.tv("vector", "tensor_mul", rsq[:], rsq[:], nt_,
                                 reads=("rsq", "sc_b"), writes=("rsq",))
                        p.tv("vector", "tensor_scalar", ztmp[:], ypair[:], rsq[:, 0:1],
                             None, ALU.mult, reads=("ypair", "rsq"), writes=("ztmp",))
                        for tk in range(NXT):
                            p.tv("tensor", "transpose", ps_tr[:, 0:128],
                                 ztmp[:, tk * 128:(tk + 1) * 128], idb[:],
                                 reads=("ztmp", "idb"), writes=("ps_tr",))
                            p.tv("vector", "tensor_copy",
                                 ygf[:, tk * CW + pair * 128:tk * CW + (pair + 1) * 128],
                                 ps_tr[:, 0:128], reads=("ps_tr",), writes=(f"yg{tk}",))
                for tk in range(NXT):
                    p.tv("vector", "tensor_scalar", ygf[:, tk * CW:(tk + 1) * CW],
                         ygf[:, tk * CW:(tk + 1) * CW], normw[:, tk:tk + 1], None,
                         ALU.mult, reads=(f"yg{tk}", "normw"), writes=(f"yg{tk}",))
                for a in range(8):
                    p.dma("gpsimd", wbuf[:, a * 1024:(a + 1) * 1024],
                          opT_d[a * 128:(a + 1) * 128, :], writes=("wbuf",))
                for a in range(8):
                    p.dma("gpsimd", zbuf[:, a * 1024:(a + 1) * 1024],
                          opT_d[(8 + a) * 128:(9 + a) * 128, :], writes=("ZB",))
                for ob in range(NA):
                    ps, pk = (ps_a, "ps_a") if ob % 2 == 0 else (ps_bc, "ps_bc")
                    fns = [(lambda e, k=k, ob=ob, ps=ps: e.matmul(
                        ps[:, 0:CW],
                        (wbuf if k < 8 else zbuf)[:, (k % 8) * 1024 + ob * 128:
                                                  (k % 8) * 1024 + (ob + 1) * 128],
                        ygf[:, k * CW:(k + 1) * CW], start=(k == 0), stop=(k == 15)))
                        for k in range(16)]
                    p.group("tensor", fns,
                            reads=("wbuf", "ZB") + tuple(f"yg{k}" for k in range(16)),
                            writes=(pk,))
                    sq, sk = sqt[ob % 2], "sqt0"
                    p.tv("vector", "tensor_add", sq[:, 0:CW], ps[:, 0:CW],
                         xb[:, ob * CW:(ob + 1) * CW], reads=(pk, f"xb{ob}"), writes=(sk,))
                    p.dma("gpsimd", xo_d[s * DM + ob * 128:s * DM + (ob + 1) * 128,
                                         base:base + CW],
                          sq[:, 0:CW], reads=(sk, "xo_init"),
                          writes=(f"xoF{s}_{c}_{ob}",), accum_op=ALU.add)
    return


def run_l0attn(inputs, x_current, cores=8):
    if "l0attn" not in _cache:
        _cache["l0attn"] = build_l0attn()
    nc = _cache["l0attn"]
    f32 = lambda v: np.asarray(v, dtype=np.float32)
    NBC = 17
    convw = f32(inputs["L0_attn_conv_w"]).reshape(2176, 4).reshape(NBC, 128, 4) \
        .transpose(1, 0, 2).reshape(128, NBC * 4)
    convb = f32(inputs["L0_attn_conv_b"]).reshape(NBC, 128).T
    lnwb = np.concatenate([f32(inputs["L0_attn_ln_w"]).reshape(8, 128).T,
                           f32(inputs["L0_attn_ln_b"]).reshape(8, 128).T], axis=1)
    normw = f32(inputs["L0_attn_norm_w"]).reshape(16, 128).T
    hpar = np.stack([f32(inputs["L0_attn_dt_bias"]), f32(inputs["L0_attn_A_log"]),
                     f32(inputs["L0_attn_D_skip"])], axis=1)
    ecst = np.zeros((32, 2048), dtype=np.float32)
    for h in range(32):
        ecst[h, h * 64:(h + 1) * 64] = 1.0
    triu = np.tril(np.ones((64, 64))).astype(ml_dtypes.bfloat16)
    ecst2 = np.tile(np.eye(64), (1, 32)).astype(ml_dtypes.bfloat16)
    w = {
        "ipT": np.ascontiguousarray(f32(inputs["L0_attn_in_proj_w"]).T),
        "opT": np.ascontiguousarray(f32(inputs["L0_attn_out_proj_w"]).T),
        "convw": np.ascontiguousarray(convw),
        "convb": np.ascontiguousarray(convb),
        "lnwb": np.ascontiguousarray(lnwb),
        "normw": np.ascontiguousarray(normw),
        "hpar": np.ascontiguousarray(hpar),
        "ecst": ecst,
        "triu": triu,
        "ecst2": np.ascontiguousarray(ecst2),
        "identb16": np.eye(128, dtype=ml_dtypes.bfloat16),
    }
    in_maps = [dict(w, x_res=_shard_x(x_current, c)) for c in range(cores)]
    res = run_bass_kernel_spmd(nc, in_maps, core_ids=list(range(cores)))
    return _unshard([r for r in res.results])



def _mk_io(nc):
    x_d = nc.dram_tensor("x_res", [NSEQ * DM, T], F32, kind="ExternalInput").ap()
    xo_d = nc.dram_tensor("x_out", [NSEQ * DM, T], F32, kind="ExternalOutput").ap()
    return x_d, xo_d


def build_l0attn():
    p = Prog("l0attn")
    build_l0attn.last_prog = p
    x_d, xo_d = _mk_io(p.nc)
    emit_l0attn(p, x_d, xo_d, pfx="")
    return p.finalize()


def build_l0ffn():
    p = Prog("l0ffn")
    build_l0ffn.last_prog = p
    x_d, xo_d = _mk_io(p.nc)
    emit_l0ffn(p, x_d, xo_d, pfx="")
    return p.finalize()


def build_l1attn():
    p = Prog("l1attn")
    build_l1attn.last_prog = p
    x_d, xo_d = _mk_io(p.nc)
    emit_l1attn(p, x_d, xo_d, pfx="")
    return p.finalize()


def build_l1ffn():
    p = Prog("l1ffn")
    build_l1ffn.last_prog = p
    x_d, xo_d = _mk_io(p.nc)
    emit_l1ffn(p, x_d, xo_d, pfx="")
    return p.finalize()


def build_fused():
    p = Prog("fused")
    build_fused.last_prog = p
    nc = p.nc
    x_d = nc.dram_tensor("x_res", [NSEQ * DM, T], F32, kind="ExternalInput").ap()
    xo_d = nc.dram_tensor("x_out", [NSEQ * DM, T], F32, kind="ExternalOutput").ap()
    x1_d = nc.dram_tensor("x1", [NSEQ * DM, T], F32, kind="Internal").ap()
    x2_d = nc.dram_tensor("x2", [NSEQ * DM, T], F32, kind="Internal").ap()
    x3_d = nc.dram_tensor("x3", [NSEQ * DM, T], F32, kind="Internal").ap()
    emit_l0attn(p, x_d, x1_d, pfx="a_")
    p.barrier()
    emit_l0ffn(p, x1_d, x2_d, pfx="f_")
    p.barrier()
    emit_l1attn(p, x2_d, x3_d, pfx="r_")
    p.barrier()
    emit_l1ffn(p, x3_d, xo_d, pfx="m_")
    return p.finalize()


def _prep_l0attn(inputs):
    f32 = lambda v: np.asarray(v, dtype=np.float32)
    NBC = 17
    convw = f32(inputs["L0_attn_conv_w"]).reshape(2176, 4).reshape(NBC, 128, 4) \
        .transpose(1, 0, 2).reshape(128, NBC * 4)
    convb = f32(inputs["L0_attn_conv_b"]).reshape(NBC, 128).T
    lnwb = np.concatenate([f32(inputs["L0_attn_ln_w"]).reshape(8, 128).T,
                           f32(inputs["L0_attn_ln_b"]).reshape(8, 128).T], axis=1)
    normw = f32(inputs["L0_attn_norm_w"]).reshape(16, 128).T
    hpar = np.stack([f32(inputs["L0_attn_dt_bias"]), f32(inputs["L0_attn_A_log"]),
                     f32(inputs["L0_attn_D_skip"])], axis=1)
    ecst = np.zeros((32, 2048), dtype=np.float32)
    for h in range(32):
        ecst[h, h * 64:(h + 1) * 64] = 1.0
    return {
        "ipT": np.ascontiguousarray(f32(inputs["L0_attn_in_proj_w"]).T),
        "opT": np.ascontiguousarray(f32(inputs["L0_attn_out_proj_w"]).T),
        "convw": np.ascontiguousarray(convw),
        "convb": np.ascontiguousarray(convb),
        "lnwb": np.ascontiguousarray(lnwb),
        "normw": np.ascontiguousarray(normw),
        "hpar": np.ascontiguousarray(hpar),
        "ecst": ecst,
        "triu": np.tril(np.ones((64, 64))).astype(ml_dtypes.bfloat16),
        "ecst2": np.ascontiguousarray(np.tile(np.eye(64), (1, 32)).astype(ml_dtypes.bfloat16)),
        "identb16": np.eye(128, dtype=ml_dtypes.bfloat16),
    }


def _prep_l0ffn(inputs):
    f32 = lambda v: np.asarray(v, dtype=np.float32)
    return {
        "w1T": np.ascontiguousarray(f32(inputs["L0_ffn_w1"]).T),
        "w2T": np.ascontiguousarray(f32(inputs["L0_ffn_w2"]).T),
        "convw": np.ascontiguousarray(f32(inputs["L0_ffn_conv_w"]).reshape(32, 128, 3)
                                      .transpose(1, 0, 2).reshape(128, 96)),
        "convb": np.ascontiguousarray(f32(inputs["L0_ffn_conv_b"]).reshape(32, 128).T),
        "identf32": np.eye(128, dtype=np.float32),
    }


def _prep_l1attn(inputs):
    f32 = lambda v: np.asarray(v, dtype=np.float32)
    ctile = lambda v: np.ascontiguousarray(f32(v).reshape(8, 128).T)
    par = np.concatenate([
        ctile(inputs["L1_attn_ln1_g"]), ctile(inputs["L1_attn_ln1_b"]),
        ctile(inputs["L1_attn_ln2_g"]), ctile(inputs["L1_attn_ln2_b"]),
        ctile(inputs["L1_attn_tm_mix_k"]), ctile(inputs["L1_attn_tm_mix_v"]),
        ctile(inputs["L1_attn_tm_mix_r"]), ctile(inputs["L1_attn_time_decay"]),
        ctile(inputs["L1_attn_time_first"]), ctile(inputs["L1_attn_cm_mix_k"]),
        ctile(inputs["L1_attn_cm_mix_r"])], axis=1)
    return {
        "wk": np.ascontiguousarray(f32(inputs["L1_attn_tm_Wk"])),
        "wv": np.ascontiguousarray(f32(inputs["L1_attn_tm_Wv"])),
        "wr": np.ascontiguousarray(f32(inputs["L1_attn_tm_Wr"])),
        "wo": np.ascontiguousarray(f32(inputs["L1_attn_Wo"])),
        "cmk": np.ascontiguousarray(f32(inputs["L1_attn_cm_Wk"])),
        "cmr": np.ascontiguousarray(f32(inputs["L1_attn_cm_Wr"])),
        "cmv": np.ascontiguousarray(f32(inputs["L1_attn_cm_Wv"])),
        "par": np.ascontiguousarray(par),
    }


def _prep_l1ffn(inputs):
    f32 = lambda v: np.asarray(v, dtype=np.float32)
    return {
        "attnT": np.ascontiguousarray(f32(inputs["L1_ffn_attn_w"]).T),
        "gateT": np.ascontiguousarray(f32(inputs["L1_ffn_gate_w"]).T.reshape(8, 128, 8)
                                      .transpose(1, 0, 2).reshape(128, 64)),
        "gpT": np.ascontiguousarray(f32(inputs["L1_ffn_gate_proj"]).transpose(0, 2, 1)).reshape(8192, 4096),
        "upT": np.ascontiguousarray(f32(inputs["L1_ffn_up_proj"]).transpose(0, 2, 1)).reshape(8192, 4096),
        "dnT": np.ascontiguousarray(f32(inputs["L1_ffn_down_proj"]).transpose(0, 2, 1)).reshape(32768, 1024),
        "lnp": np.ascontiguousarray(np.stack([
            f32(inputs["L1_ffn_ln1_g"]).reshape(8, 128),
            f32(inputs["L1_ffn_ln1_b"]).reshape(8, 128),
            f32(inputs["L1_ffn_ln2_g"]).reshape(8, 128),
            f32(inputs["L1_ffn_ln2_b"]).reshape(8, 128),
        ], axis=0).reshape(32, 128).T.reshape(128, 32)),
        "identf32": np.eye(128, dtype=np.float32),
    }


def kernel(**inputs):
    x = np.ascontiguousarray(np.asarray(inputs["L0_attn_x"], dtype=np.float32))
    if "fused" not in _cache:
        _cache["fused"] = build_fused()
    nc = _cache["fused"]
    w = {}
    for pfx, prep in (("a_", _prep_l0attn), ("f_", _prep_l0ffn),
                      ("r_", _prep_l1attn), ("m_", _prep_l1ffn)):
        for k, v in prep(inputs).items():
            w[pfx + k] = v
    in_maps = [dict(w, x_res=_shard_x(x, c)) for c in range(8)]
    res = run_bass_kernel_spmd(nc, in_maps, core_ids=list(range(8)))
    return _unshard([r for r in res.results])


# version 04
# speedup vs baseline: 1.5437x; 1.5417x over previous
import numpy as np
import ml_dtypes
from contextlib import ExitStack

import concourse.bass as bass
import concourse.mybir as mybir
from concourse.bass_utils import run_bass_kernel_spmd

F32 = mybir.dt.float32
BF16 = mybir.dt.bfloat16
AF = mybir.ActivationFunctionType
ALU = mybir.AluOpType
AX = mybir.AxisListType
RND = 12582912.0

ENGINES = ("tensor", "vector", "scalar", "gpsimd", "sync")

NSEQ = 2
T = 2048
DM = 1024
NA = DM // 128


class Prog:

    NDMASEM = 12

    def __init__(self, name="prog"):
        self.nc = bass.Bass("TRN2", target_bir_lowering=False)
        self.name = name
        self.ops = {e: [] for e in ENGINES}
        self.dmasems = [f"dma{i}" for i in range(self.NDMASEM)]
        self.allsems = list(ENGINES) + self.dmasems
        self.cnt = {s: 0 for s in self.allsems}
        self.watermark = {e: {s: 0 for s in self.allsems} for e in ENGINES}
        self.wr = {}
        self.rd = {}
        self._dma_rr = 0
        self.log = []

    def _edges(self, eng, reads, writes, mysem):
        need = {}
        for k in reads:
            w = self.wr.get(k)
            if w:
                need[w[0]] = max(need.get(w[0], 0), w[1])
        for k in writes:
            w = self.wr.get(k)
            if w and w[0] != mysem:
                need[w[0]] = max(need.get(w[0], 0), w[1])
            for sm, rv in self.rd.get(k, {}).items():
                if sm != mysem:
                    need[sm] = max(need.get(sm, 0), rv)
        waits = []
        for ts_, val in need.items():
            if ts_ == mysem and val >= self.cnt.get(mysem, 0) + 1:
                continue
            if val > self.watermark[eng][ts_]:
                waits.append((ts_, val))
                self.watermark[eng][ts_] = val
        return waits

    def _post(self, sem, reads, writes):
        v = self.cnt[sem]
        for k in reads:
            self.rd.setdefault(k, {})[sem] = v
        for k in writes:
            self.wr[k] = (sem, v)
            self.rd[k] = {}

    def emit(self, eng, fn, reads=(), writes=(), step=1):
        waits = self._edges(eng, reads, writes, eng)
        self.cnt[eng] += step
        self._post(eng, reads, writes)
        self.log.append((eng, self.cnt[eng], tuple(waits), tuple(reads), tuple(writes)))

        def run(e, sems, _w=waits, _f=fn, _s=step, _eng=eng):
            for ts_, val in _w:
                e.wait_ge(sems[ts_], val)
            _f(e).then_inc(sems[_eng], _s)
        self.ops[eng].append(run)

    def group(self, eng, fns, reads=(), writes=()):
        waits = self._edges(eng, reads, writes, eng)
        self.cnt[eng] += 1
        self._post(eng, reads, writes)
        self.log.append((eng, self.cnt[eng], tuple(waits), tuple(reads), tuple(writes), "grp"))

        def run(e, sems, _w=waits, _fns=fns, _eng=eng):
            for ts_, val in _w:
                e.wait_ge(sems[ts_], val)
            inst = None
            for fn in _fns:
                inst = fn(e)
            inst.then_inc(sems[_eng], 1)
        self.ops[eng].append(run)

    def barrier(self):
        snap = dict(self.cnt)
        for e in ENGINES:
            for ts_ in self.allsems:
                if ts_ != e and snap[ts_] > self.watermark[e][ts_]:
                    self.watermark[e][ts_] = snap[ts_]

            def run(eng, sems, _snap=snap, _e=e):
                for ts_ in list(ENGINES) + [s for s in _snap if s.startswith("dma")]:
                    if ts_ != _e and _snap.get(ts_, 0) > 0:
                        eng.wait_ge(sems[ts_], _snap[ts_])
            self.ops[e].append(run)

    def dma(self, queue, out, in_, reads=(), writes=(), **kw):
        sem = self.dmasems[self._dma_rr]
        self._dma_rr = (self._dma_rr + 1) % self.NDMASEM
        waits = self._edges(queue, reads, writes, sem)
        prior = self.cnt[sem]
        if prior > self.watermark[queue][sem]:
            waits.append((sem, prior))
            self.watermark[queue][sem] = prior
        self.cnt[sem] = prior + 16
        self._post(sem, reads, writes)
        self.log.append((queue, sem, self.cnt[sem], tuple(waits), tuple(reads), tuple(writes)))

        def run(e, sems, _w=waits, _sem=sem):
            for ts_, val in _w:
                e.wait_ge(sems[ts_], val)
            e.dma_start(out=out, in_=in_, **kw).then_inc(sems[_sem], 16)
        self.ops[queue].append(run)

    def tv(self, eng, name, *args, reads=(), writes=(), **kw):
        self.emit(eng, lambda e: getattr(e, name)(*args, **kw),
                  reads=reads, writes=writes)

    def finalize(self):
        nc = self.nc
        with ExitStack() as st:
            sems = {s: st.enter_context(nc.semaphore(f"S_{s}")) for s in self.allsems}
            block = st.enter_context(nc.Block())
            final = dict(self.cnt)
            oplists = self.ops

            def make(engname):
                def body(e):
                    for f in oplists[engname]:
                        f(e, sems)
                    if engname in ("sync", "gpsimd"):
                        for ts_ in self.allsems:
                            if final[ts_] > 0:
                                e.wait_ge(sems[ts_], final[ts_])
                return body
            block.tensor(make("tensor"))
            block.vector(make("vector"))
            block.scalar(make("scalar"))
            block.gpsimd(make("gpsimd"))
            block.sync(make("sync"))
        return nc



def stat_sums(p, src_tiles, W, ps_st, sq, sqkeys, onesc):
    nt = len(src_tiles)
    for i, (ap_, key) in enumerate(src_tiles):
        p.tv("scalar", "activation", sq[:, i * W:(i + 1) * W], ap_, AF.Square,
             reads=(key,), writes=(sqkeys[i],))
    fns = [(lambda e, i=i, ap_=ap_: e.matmul(ps_st[0:1, 0:W], onesc[:, 0:1], ap_,
                                             start=(i == 0), stop=(i == nt - 1)))
           for i, (ap_, key) in enumerate(src_tiles)]
    p.group("tensor", fns, reads=tuple(k for _, k in src_tiles) + ("onesc",),
            writes=("ps_st",))
    fns = [(lambda e, i=i: e.matmul(ps_st[32:33, 0:W], onesc[:, 0:1], sq[:, i * W:(i + 1) * W],
                                    start=(i == 0), stop=(i == nt - 1)))
           for i in range(nt)]
    p.group("tensor", fns, reads=tuple(sqkeys) + ("onesc",), writes=("ps_st",))


def ln_rows(p, rA, rB, W, ps_st, d_model, eps):
    mu, ve, rstd, mur = rA(0, W), rA(1, W), rA(2, W), rA(3, W)
    t0 = rB(0, W)
    p.tv("vector", "tensor_scalar", mu, ps_st[0:1, 0:W], 1.0 / d_model, None, ALU.mult,
         reads=("ps_st",), writes=("rA0",))
    p.tv("vector", "tensor_scalar", ve, ps_st[32:33, 0:W], 1.0 / d_model, eps, ALU.mult, ALU.add,
         reads=("ps_st",), writes=("rA1",))
    p.tv("vector", "tensor_mul", t0, mu, mu, reads=("rA0",), writes=("rB0",))
    p.tv("vector", "tensor_sub", ve, ve, t0, reads=("rA1", "rB0"), writes=("rA1",))
    p.tv("vector", "reciprocal", t0, ve, reads=("rA1",), writes=("rB0",))
    p.tv("scalar", "activation", rstd, t0, AF.Sqrt, reads=("rB0",), writes=("rA2",))
    t3 = rB(0, W)
    for _ in range(2):
        p.tv("vector", "tensor_mul", t3, rstd, rstd, reads=("rA2",), writes=("rB0",))
        p.tv("vector", "tensor_mul", t3, ve, t3, reads=("rA1", "rB0"), writes=("rB0",))
        p.tv("vector", "tensor_scalar", t3, t3, -0.5, 1.5, ALU.mult, ALU.add,
             reads=("rB0",), writes=("rB0",))
        p.tv("vector", "tensor_mul", rstd, rstd, t3, reads=("rA2", "rB0"), writes=("rA2",))
    p.tv("vector", "tensor_mul", mur, mu, rstd, reads=("rA0", "rA2"), writes=("rA3",))
    return rstd, mur


def bcast_row(p, row_ap, row_key, W, ps_bc, dst, dst_key, ones1):
    p.tv("tensor", "matmul", ps_bc[:, 0:W], ones1[0:1, 0:128], row_ap, start=True, stop=True,
         reads=(row_key, "ones1"), writes=("ps_bc",))
    p.tv("scalar", "copy", dst[:, 0:W], ps_bc[:, 0:W], reads=("ps_bc",), writes=(dst_key,))


def amax_rows(p, cmb, cmb_key, W, ps_st, amax_tm, idf, scr_d, srow, bk):
    if W == 1:
        p.tv("tensor", "transpose", ps_st[0:1, 0:128], cmb[:, 0:1], idf[:],
             reads=(cmb_key, "idf"), writes=("ps_st",))
        p.tv("vector", "reduce_max", srow[0:1, 0:1], ps_st[0:1, 0:128], AX.X,
             reads=("ps_st",), writes=(bk + "_row",))
        return
    nblk = (W + 127) // 128
    for b in range(nblk):
        bw = min(128, W - b * 128)
        p.tv("tensor", "transpose", ps_st[0:bw, b * 128:b * 128 + 128],
             cmb[:, b * 128:b * 128 + bw], idf[:],
             reads=(cmb_key, "idf"), writes=("ps_st",))
        p.tv("vector", "reduce_max", amax_tm[0:bw, b:b + 1], ps_st[0:bw, b * 128:b * 128 + 128],
             AX.X, reads=("ps_st",), writes=(bk + f"_tm{b}",))
        p.dma("sync", scr_d.rearrange("a b -> b a")[b * 128:b * 128 + bw, :],
              amax_tm[0:bw, b:b + 1], reads=(bk + f"_tm{b}",), writes=(bk + f"_scr{b}",))
    p.dma("sync", srow[0:1, 0:W], scr_d[0:1, 0:W],
          reads=tuple(bk + f"_scr{b}" for b in range(nblk)), writes=(bk + "_row",))


def quant_rows(p, srow, srow_key, rA, rB, m_ap, m_key, W):
    p.tv("vector", "tensor_scalar", rB(1, W), srow[0:1, 0:W], 1e-5, None,
         ALU.max, reads=(srow_key,), writes=("rB1",))
    p.tv("vector", "reciprocal", rB(2, W), rB(1, W), reads=("rB1",), writes=("rB2",))
    p.tv("vector", "tensor_scalar", rA(0, W), rB(2, W), 127.0, None,
         ALU.mult, reads=("rB2",), writes=("rA0",))
    p.tv("vector", "tensor_scalar", rA(1, W), rB(1, W), 1.0 / 127.0, None,
         ALU.mult, reads=("rB1",), writes=("rA1",))
    p.tv("vector", "tensor_scalar", rA(1, W), rA(1, W), m_ap, None,
         ALU.mult, reads=("rA1", m_key), writes=("rA1",))



DEBUG_L0FFN = False


def emit_l0ffn(p, x_d, xo_d, pfx="f_"):
    nc = p.nc
    DH = 4096
    CH = 512
    NH = DH // 128
    AR_W = 516

    w1T_d = nc.dram_tensor(pfx + "w1T", [DM, DH], F32, kind="ExternalInput").ap()
    w2T_d = nc.dram_tensor(pfx + "w2T", [DH, DM], F32, kind="ExternalInput").ap()
    convw_d = nc.dram_tensor(pfx + "convw", [128, 3 * DH // 128], F32, kind="ExternalInput").ap()
    convb_d = nc.dram_tensor(pfx + "convb", [128, DH // 128], F32, kind="ExternalInput").ap()
    idf_d = nc.dram_tensor(pfx + "identf32", [128, 128], F32, kind="ExternalInput").ap()
    if DEBUG_L0FFN:
        dbg_xn = nc.dram_tensor(pfx + "dbg_xn", [128, 8 * 512], F32, kind="ExternalOutput").ap()
        dbg_xq = nc.dram_tensor(pfx + "dbg_xq", [128, 8 * 512], F32, kind="ExternalOutput").ap()
        dbg_h = nc.dram_tensor(pfx + "dbg_h", [128, 4 * 516], F32, kind="ExternalOutput").ap()
        dbg_hs = nc.dram_tensor(pfx + "dbg_hs", [128, 4 * 516], F32, kind="ExternalOutput").ap()
        dbg_ra = nc.dram_tensor(pfx + "dbg_ra", [1, 2048], F32, kind="ExternalOutput").ap()
        dbg_rb = nc.dram_tensor(pfx + "dbg_rb", [1, 2048], F32, kind="ExternalOutput").ap()
        dbg_rows = nc.dram_tensor(pfx + "dbg_rows", [1, 4 * 640], F32, kind="ExternalOutput").ap()
        dbg_sq = nc.dram_tensor(pfx + "dbg_sq", [128, 4096], F32, kind="ExternalOutput").ap()
        dbg_x0 = nc.dram_tensor(pfx + "dbg_x0", [128, 512], F32, kind="ExternalOutput").ap()
        dbg_sl1 = nc.dram_tensor(pfx + "dbg_sl1", [128, 4 * 516], F32, kind="ExternalOutput").ap()
        dbg_wq = nc.dram_tensor(pfx + "dbg_wq", [128, DH], BF16, kind="ExternalOutput").ap()
        dbg_m = nc.dram_tensor(pfx + "dbg_m", [1, 4], F32, kind="ExternalOutput").ap()
        dbg_wraw = nc.dram_tensor(pfx + "dbg_wraw", [128, DH], F32, kind="ExternalOutput").ap()
        dbg_wrnd = nc.dram_tensor(pfx + "dbg_wrnd", [128, DH], F32, kind="ExternalOutput").ap()
        dbg_swc = nc.dram_tensor(pfx + "dbg_swc", [128, 1], F32, kind="ExternalOutput").ap()
        dbg_m2 = nc.dram_tensor(pfx + "dbg_m2", [1, 4], F32, kind="ExternalOutput").ap()
        dbg_pbc = nc.dram_tensor(pfx + "dbg_pbc", [128, 1], F32, kind="ExternalOutput").ap()
        dbg_mini = nc.dram_tensor(pfx + "dbg_mini", [1, 16], F32, kind="ExternalOutput").ap()
        dbg_mhs = nc.dram_tensor(pfx + "dbg_mhs", [128, 32], F32, kind="ExternalOutput").ap()
        dbg_wq2 = nc.dram_tensor(pfx + "dbg_wq2", [128, DM], BF16, kind="ExternalOutput").ap()
        dbg_q2 = nc.dram_tensor(pfx + "dbg_q2", [128, 640], BF16, kind="ExternalOutput").ap()
        dbg_q5 = nc.dram_tensor(pfx + "dbg_q5", [128, 640], BF16, kind="ExternalOutput").ap()
        dbg_ob = nc.dram_tensor(pfx + "dbg_ob", [128, 640], F32, kind="ExternalOutput").ap()
        dbg_ps = nc.dram_tensor(pfx + "dbg_ps", [128, 640], F32, kind="ExternalOutput").ap()
        dbg_ivb = nc.dram_tensor(pfx + "dbg_ivb", [128, 640], F32, kind="ExternalOutput").ap()
    wq1_d = nc.dram_tensor(pfx + "wq1", [DM, DH], BF16, kind="Internal").ap()
    wq2_d = nc.dram_tensor(pfx + "wq2", [DH, DM], BF16, kind="Internal").ap()
    scr_d = nc.dram_tensor(pfx + "scr", [1, 2048], F32, kind="Internal").ap()

    with ExitStack() as st:
        sb = lambda n_, s, d: st.enter_context(nc.sbuf_tensor(pfx + n_, s, d))
        psb = lambda n_, s: st.enter_context(nc.psum_tensor(pfx + n_, s, F32))

        xcur = sb("xcur", [128, NA * CH], F32)
        S1 = sb("S1", [128, NA * CH], F32)
        xq = sb("xq", [128, NA * CH], BF16)
        wbuf = sb("wbuf", [128, 16384], BF16)
        aren = sb("aren", [128, 33 * AR_W], F32)
        cmbA = sb("cmbA", [128, 640], F32)
        ctmpA = sb("ctmpA", [128, 640], F32)
        ctmpB = sb("ctmpB", [128, 640], F32)
        rows = sb("rows", [1, 4 * 640], F32)
        rows2 = sb("rows2", [1, 4 * 640], F32)
        rA = lambda i, W: rows[0:1, i * 640:i * 640 + W]
        rB = lambda i, W: rows2[0:1, i * 640:i * 640 + W]
        amax_tm = sb("amax_tm", [128, 8], F32)
        sc_b = sb("sc_b", [128, 640], F32)
        iv_b = sb("iv_b", [128, 640], F32)
        qtmp = [sb(f"qtmp{i}", [128, 640], BF16) for i in range(2)]
        qf32 = [sb("qf32_0", [128, 640], F32)] * 2
        obuf = [sb(f"obuf{i}", [128, 640], F32) for i in range(2)]
        ones1 = sb("ones1", [1, 128], F32)
        onesc = sb("onesc", [128, 1], F32)
        idf = sb("idf", [128, 128], F32)
        convw = sb("convw_sb", [128, 3 * NH], F32)
        convb = sb("convb_sb", [128, NH], F32)
        srow = sb("srow", [1, 640], F32)
        asum = sb("asum", [128, 32], F32)
        mrow = sb("mrow", [1, 4], F32)
        swcol = sb("swcol", [128, 1], F32)

        ps_mm = [psb(f"psA{i}", [128, 512]) for i in range(2)]
        ps_w2 = [psb(f"psW{i}", [128, 512]) for i in range(4)]
        ps_st = psb("ps_st", [128, 512])
        ps_bc = psb("ps_bc", [128, 512])
        ob_ps = ps_w2 + ps_mm + [ps_st, ps_bc]
        ob_keys = ["psW0", "psW1", "psW2", "psW3", "psA0", "psA1", "ps_st", "ps_bc"]

        p.tv("vector", "memset", ones1[:], 1.0, writes=("ones1",))
        p.tv("vector", "memset", onesc[:], 1.0, writes=("onesc",))
        p.dma("sync", idf[:], idf_d, writes=("idf",))
        p.dma("sync", convw[:], convw_d, writes=("convw",))
        p.dma("sync", convb[:], convb_d, writes=("convb",))
        p.tv("vector", "memset", cmbA[:], 0.0, writes=("cmbA",))
        p.dma("sync", xo_d, x_d, writes=("xo_init",))

        def wprep(wT_d, wq_dram, KD, OD, mcol_slice, mkey):
            nkt = KD // 128
            stag = [(S1, "wstA"), (xcur, "wstB")]
            for k in range(nkt):
                buf, bk = stag[k % 2]
                oth, ok_ = stag[(k + 1) % 2]
                p.dma("sync", buf[:, 0:OD], wT_d[k * 128:(k + 1) * 128, :], writes=(bk,))
                p.emit("scalar", lambda e, buf=buf, oth=oth, k=k, OD=OD: e.activation(
                    oth[:, 0:OD], buf[:, 0:OD], AF.Abs, accum_out=asum[:, k:k + 1]),
                    reads=(bk,), writes=(ok_, f"as{k}"))
            p.tv("vector", "reduce_sum", asum[:, 0:1], asum[:, 0:nkt], AX.X,
                 reads=tuple(f"as{k}" for k in range(nkt)), writes=("asum0",))
            p.tv("tensor", "matmul", ps_st[0:1, 0:1], onesc[:, 0:1], asum[:, 0:1],
                 start=True, stop=True, reads=("asum0", "onesc"), writes=("ps_st",))
            p.tv("vector", "tensor_scalar", mcol_slice, ps_st[0:1, 0:1],
                 1.0 / (KD * OD), 1e-5, ALU.mult, ALU.max, reads=("ps_st",), writes=(mkey,))
            p.tv("vector", "reciprocal", mrow[0:1, 2:3], mcol_slice, reads=(mkey,), writes=("minv",))
            if DEBUG_L0FFN and mkey == "m1":
                p.dma("sync", dbg_m2, mrow[0:1, :], reads=("minv", mkey), writes=("dbgm2",))
            p.tv("tensor", "matmul", ps_bc[:, 0:1], ones1[0:1, :], mrow[0:1, 2:3],
                 start=True, stop=True, reads=("minv", "ones1"), writes=("ps_bc",))
            p.tv("scalar", "copy", swcol[:], ps_bc[:, 0:1], reads=("ps_bc",), writes=("swcol",))
            if DEBUG_L0FFN and mkey == "m1":
                p.dma("sync", dbg_pbc, swcol[:], reads=("swcol",), writes=("dbgpbc",))
            for k in range(nkt):
                buf, bk = stag[k % 2]
                p.dma("sync", buf[:, 0:OD], wT_d[k * 128:(k + 1) * 128, :], writes=(bk,))
                if DEBUG_L0FFN and mkey == "m1" and k == 0:
                    p.dma("sync", dbg_wraw, buf[:, 0:OD], reads=(bk,), writes=("dbgwraw",))
                    p.dma("sync", dbg_swc, swcol[:], reads=("swcol",), writes=("dbgswc",))
                p.tv("vector", "tensor_scalar", buf[:, 0:OD], buf[:, 0:OD], swcol[:, 0:1], RND,
                     ALU.mult, ALU.add, reads=(bk, "swcol"), writes=(bk,))
                p.tv("vector", "tensor_scalar", buf[:, 0:OD], buf[:, 0:OD], RND, None,
                     ALU.subtract, reads=(bk,), writes=(bk,))
                if DEBUG_L0FFN and mkey == "m1" and k == 0:
                    p.dma("sync", dbg_wrnd, buf[:, 0:OD], reads=(bk,), writes=("dbgwrnd",))
                p.tv("vector", "tensor_scalar", wbuf[:, 0:OD], buf[:, 0:OD], 1.0, -1.0,
                     ALU.min, ALU.max, reads=(bk,), writes=("wbufM",))
                p.dma("sync", wq_dram[k * 128:(k + 1) * 128, :], wbuf[:, 0:OD],
                      reads=("wbufM",), writes=(mkey + f"_q{k}",))

        wprep(w1T_d, wq1_d, DM, DH, mrow[0:1, 0:1], "m1")
        wprep(w2T_d, wq2_d, DH, DM, mrow[0:1, 1:2], "m2")
        m1_ap, m2_ap = mrow[0:1, 0:1], mrow[0:1, 1:2]
        if DEBUG_L0FFN:
            p.dma("sync", dbg_wq, wq1_d[0:128, :], reads=("m1_q0",), writes=("dbgwq",))
            p.dma("sync", dbg_wq2, wq2_d[0:128, :], reads=("m2_q0",), writes=("dbgwq2",))
            p.dma("sync", dbg_m, mrow[0:1, :], reads=("m1", "m2", "m1_m", "m2_m"), writes=("dbgm",))
        p.barrier()

        windows = [(0, 511, 0), (511, 512, 1), (1023, 512, 2), (1535, 512, 3), (2047, 1, 3)]

        for s in range(NSEQ):
            for j in range(1, 33):
                p.tv("vector", "memset", aren[:, j * AR_W:j * AR_W + 2], 0.0,
                     writes=(f"slot{j}_lead",))
            for c in range(4):
                base = c * CH
                for a in range(NA):
                    p.dma("sync", xcur[:, a * CH:(a + 1) * CH],
                          x_d[s * DM + a * 128: s * DM + (a + 1) * 128, base:base + CH],
                          writes=(f"xcur{a}",))
                stat_sums(p, [(xcur[:, a * CH:(a + 1) * CH], f"xcur{a}") for a in range(NA)],
                          CH, ps_st, S1, [f"sq{a}" for a in range(NA)], onesc)
                if DEBUG_L0FFN and s == 0 and c == 0:
                    p.dma("sync", dbg_sq, S1[:],
                          reads=tuple(f"sq{a}" for a in range(NA)), writes=("dbgsq",))
                rstd, mur = ln_rows(p, rA, rB, CH, ps_st, DM, 1e-5)
                if DEBUG_L0FFN and s == 0 and c == 0:
                    p.dma("sync", dbg_rows, rows[0:1, :],
                          reads=("rA0", "rA1", "rA2", "rA3"), writes=("dbgrows",))
                    p.dma("sync", dbg_x0, xcur[:, 0:CH], reads=("xcur0",), writes=("dbgx0",))
                bcast_row(p, rstd, "rA2", CH, ps_bc, sc_b, "sc_b", ones1)
                bcast_row(p, mur, "rA3", CH, ps_bc, iv_b, "iv_b", ones1)
                for a in range(NA):
                    p.tv("vector", "tensor_mul", S1[:, a * CH:(a + 1) * CH],
                         xcur[:, a * CH:(a + 1) * CH], sc_b[:, 0:CH],
                         reads=(f"xcur{a}", "sc_b"), writes=(f"sq{a}",))
                    p.tv("vector", "tensor_sub", xcur[:, a * CH:(a + 1) * CH],
                         S1[:, a * CH:(a + 1) * CH], iv_b[:, 0:CH],
                         reads=(f"sq{a}", "iv_b"), writes=(f"xcur{a}",))
                for a in range(NA):
                    ct, ck = (ctmpA, "ctmpA") if a % 2 == 0 else (ctmpB, "ctmpB")
                    p.tv("scalar", "activation", ct[:, 0:CH], xcur[:, a * CH:(a + 1) * CH],
                         AF.Abs, reads=(f"xcur{a}",), writes=(ck,))
                    if a == 0:
                        p.tv("vector", "tensor_copy", cmbA[:, 0:CH], ct[:, 0:CH],
                             reads=(ck,), writes=("cmbA",))
                    else:
                        p.tv("vector", "tensor_max", cmbA[:, 0:CH], cmbA[:, 0:CH], ct[:, 0:CH],
                             reads=("cmbA", ck), writes=("cmbA",))
                amax_rows(p, cmbA, "cmbA", CH, ps_st, amax_tm, idf, scr_d, srow, "qa")
                quant_rows(p, srow, "qa_row", rA, rB, m1_ap, "m1", CH)
                bcast_row(p, rA(0, CH), "rA0", CH, ps_bc, sc_b, "sc_b", ones1)
                bcast_row(p, rA(1, CH), "rA1", CH, ps_bc, iv_b, "iv_b", ones1)
                for a in range(NA):
                    q = qf32[a % 2]
                    qk = f"qf{a % 2}"
                    p.tv("vector", "tensor_mul", q[:, 0:CH], xcur[:, a * CH:(a + 1) * CH],
                         sc_b[:, 0:CH], reads=(f"xcur{a}", "sc_b"), writes=(qk,))
                    p.tv("vector", "tensor_scalar", q[:, 0:CH], q[:, 0:CH], RND, None,
                         ALU.add, reads=(qk,), writes=(qk,))
                    p.tv("vector", "tensor_scalar", q[:, 0:CH], q[:, 0:CH], RND, None,
                         ALU.subtract, reads=(qk,), writes=(qk,))
                    p.tv("vector", "tensor_scalar", xq[:, a * CH:(a + 1) * CH], q[:, 0:CH],
                         127.0, -128.0, ALU.min, ALU.max, reads=(qk,), writes=(f"xq{a}",))
                for half in range(2):
                    for a in range(NA):
                        p.dma("sync", wbuf[:, a * 2048:(a + 1) * 2048],
                              wq1_d[a * 128:(a + 1) * 128, half * 2048:(half + 1) * 2048],
                              writes=("wbufM",))
                    for i in range(16 * half, 16 * half + 16):
                        ii = i - 16 * half
                        ps, pkey = ps_mm[i % 2], f"psA{i % 2}"
                        fns = [(lambda e, a=a, ii=ii, ps=ps: e.matmul(
                            ps[:, 0:CH], wbuf[:, a * 2048 + ii * 128:a * 2048 + (ii + 1) * 128],
                            xq[:, a * CH:(a + 1) * CH], start=(a == 0), stop=(a == NA - 1)))
                            for a in range(NA)]
                        p.group("tensor", fns,
                                reads=("wbufM",) + tuple(f"xq{a}" for a in range(NA)),
                                writes=(pkey,))
                        j = i + 1
                        p.tv("vector", "tensor_mul", aren[:, j * AR_W + 2:j * AR_W + 514],
                             ps[:, 0:CH], iv_b[:, 0:CH],
                             reads=(pkey, "iv_b"), writes=(f"slot{j}_body",))
                if c == 3:
                    for i in range(NH):
                        j = i + 1
                        p.tv("vector", "memset", aren[:, j * AR_W + 514:j * AR_W + 515], 0.0,
                             writes=(f"slot{j}_pad",))
                if DEBUG_L0FFN and s == 0 and c == 0:
                    p.dma("sync", dbg_xn, xcur[:],
                          reads=tuple(f"xcur{a}" for a in range(NA)), writes=("dbgxn",))
                    p.tv("vector", "tensor_copy", S1[:, 0:NA * CH], xq[:, 0:NA * CH],
                         reads=tuple(f"xq{a}" for a in range(NA)), writes=("dbgxq_s",))
                    p.dma("sync", dbg_xq, S1[:], reads=("dbgxq_s",), writes=("dbgxq",))
                    p.dma("sync", dbg_h, aren[:, 1 * AR_W:5 * AR_W],
                          reads=tuple(f"slot{j}_body" for j in range(1, 6)), writes=("dbgh",))
                    p.dma("sync", dbg_ra[0:1, 0:640], srow[0:1, 0:640],
                          reads=("qa_row",), writes=("dbgra",))
                for (g0, W, wc) in windows:
                    if wc != c:
                        continue
                    if DEBUG_L0FFN and s == 0 and c == 1:
                        p.dma("sync", dbg_sl1, aren[:, 1 * AR_W:5 * AR_W],
                              reads=tuple(f"slot{j}_body" for j in range(1, 6)) +
                                    tuple(f"slot{j}_lead" for j in range(1, 6)),
                              writes=("dbgsl1",))
                    off = g0 - base + 2
                    for i in range(NH):
                        j = i + 1
                        sl = aren[:, j * AR_W:(j + 1) * AR_W]
                        rd = (f"slot{j}_body", f"slot{j}_lead") + \
                             ((f"slot{j}_pad",) if W == 1 else ())
                        if W == 1:
                            vl, vc, vr = sl[:, 0:1], sl[:, 1:2], sl[:, 514:515]
                        else:
                            vl = sl[:, off - 1:off - 1 + W]
                            vc = sl[:, off:off + W]
                            vr = sl[:, off + 1:off + 1 + W]
                        p.tv("vector", "tensor_scalar", ctmpA[:, 0:W], vl,
                             convw[:, 3 * i:3 * i + 1], None, ALU.mult,
                             reads=rd + ("convw",), writes=("ctmpA",))
                        p.tv("vector", "scalar_tensor_tensor", ctmpB[:, 0:W], vc,
                             convw[:, 3 * i + 1:3 * i + 2], ctmpA[:, 0:W], ALU.mult, ALU.add,
                             reads=rd + ("convw", "ctmpA"), writes=("ctmpB",))
                        p.tv("vector", "scalar_tensor_tensor", ctmpA[:, 0:W], vr,
                             convw[:, 3 * i + 2:3 * i + 3], ctmpB[:, 0:W], ALU.mult, ALU.add,
                             reads=rd + ("convw", "ctmpB"), writes=("ctmpA",))
                        if W > 1:
                            p.tv("gpsimd", "tensor_copy", sl[:, 0:2], sl[:, 512:514],
                                 reads=(f"slot{j}_body",), writes=(f"slot{j}_lead",))
                            dst = aren[:, i * AR_W + 2:i * AR_W + 2 + W]
                            dk = f"slot{i}_body"
                        else:
                            dst = aren[:, j * AR_W + 515:j * AR_W + 516]
                            dk = f"slot{j}_mini"
                        p.emit("scalar", lambda e, dst=dst, W=W, i=i: e.activation(
                            dst, ctmpA[:, 0:W], AF.Silu, bias=convb[:, i:i + 1]),
                            reads=("ctmpA", "convb"), writes=(dk,))

                    def hs_ap(i):
                        if W > 1:
                            return aren[:, i * AR_W + 2:i * AR_W + 2 + W], f"slot{i}_body"
                        return (aren[:, (i + 1) * AR_W + 515:(i + 1) * AR_W + 516],
                                f"slot{i + 1}_mini")
                    for i in range(NH):
                        ai, ki = hs_ap(i)
                        ct, ck = (ctmpA, "ctmpA") if i % 2 == 0 else (ctmpB, "ctmpB")
                        p.tv("scalar", "activation", ct[:, 0:W], ai, AF.Abs,
                             reads=(ki,), writes=(ck,))
                        if i == 0:
                            p.tv("vector", "tensor_copy", cmbA[:, 0:W], ct[:, 0:W],
                                 reads=(ck,), writes=("cmbA",))
                        else:
                            p.tv("vector", "tensor_max", cmbA[:, 0:W], cmbA[:, 0:W], ct[:, 0:W],
                                 reads=("cmbA", ck), writes=("cmbA",))
                    amax_rows(p, cmbA, "cmbA", W, ps_st, amax_tm, idf, scr_d, srow, "qb")
                    if DEBUG_L0FFN and s == 0 and W == 1:
                        p.dma("sync", dbg_mini[0:1, 0:1], srow[0:1, 0:1],
                              reads=("qb_row",), writes=("dbgmini0",))
                        for _i in range(32):
                            _a, _k = hs_ap(_i)
                            p.dma("sync", dbg_mhs[:, _i:_i + 1], _a,
                                  reads=(_k,), writes=(f"dbgmhs{_i}",),
                                  allow_slow_non_contiguous=True)
                    if DEBUG_L0FFN and s == 0 and c == 0 and g0 == 0:
                        p.dma("sync", dbg_hs, aren[:, 0 * AR_W:4 * AR_W],
                              reads=tuple(f"slot{i}_body" for i in range(4)), writes=("dbghs",))
                        p.dma("sync", dbg_rb[0:1, 0:640], srow[0:1, 0:640],
                              reads=("qb_row",), writes=("dbgrb",))
                    quant_rows(p, srow, "qb_row", rA, rB, m2_ap, "m2", W)
                    bcast_row(p, rA(0, W), "rA0", W, ps_bc, sc_b, "sc_b", ones1)
                    bcast_row(p, rA(1, W), "rA1", W, ps_bc, iv_b, "iv_b", ones1)
                    for half in range(2):
                        for k in range(16 * half, 16 * half + 16):
                            p.dma("sync", wbuf[:, (k - 16 * half) * DM:(k - 16 * half + 1) * DM],
                                  wq2_d[k * 128:(k + 1) * 128, :], writes=("wbufM",))
                        for k in range(16 * half, 16 * half + 16):
                            kk = k - 16 * half
                            ai, ki = hs_ap(k)
                            q, qk = qf32[k % 2], f"qf{k % 2}"
                            qt, qtk = qtmp[k % 2], f"qt{k % 2}"
                            p.tv("vector", "tensor_mul", q[:, 0:W], ai, sc_b[:, 0:W],
                                 reads=(ki, "sc_b"), writes=(qk,))
                            p.tv("vector", "tensor_scalar", q[:, 0:W], q[:, 0:W], RND, None,
                                 ALU.add, reads=(qk,), writes=(qk,))
                            p.tv("vector", "tensor_scalar", q[:, 0:W], q[:, 0:W], RND, None,
                                 ALU.subtract, reads=(qk,), writes=(qk,))
                            p.tv("vector", "tensor_scalar", qt[:, 0:W], q[:, 0:W],
                                 127.0, -128.0, ALU.min, ALU.max, reads=(qk,), writes=(qtk,))
                            if DEBUG_L0FFN and s == 0 and c == 0 and g0 == 0 and k == 0:
                                p.dma("sync", dbg_q2[:, 0:W], qt[:, 0:W],
                                      reads=(qtk,), writes=("dbgq2",))
                            if DEBUG_L0FFN and s == 0 and c == 0 and g0 == 0 and k == 5:
                                p.dma("sync", dbg_q5[:, 0:W], qt[:, 0:W],
                                      reads=(qtk,), writes=("dbgq5",))
                            fns = [(lambda e, ob=ob, kk=kk, qt=qt, W=W,
                                    st_=(k == 0), sp_=(k == NH - 1): e.matmul(
                                ob_ps[ob][:, 0:W], wbuf[:, kk * DM + ob * 128:kk * DM + (ob + 1) * 128],
                                qt[:, 0:W], start=st_, stop=sp_))
                                for ob in range(8)]
                            p.group("tensor", fns, reads=("wbufM", qtk), writes=tuple(ob_keys))
                    for ob in range(8):
                        o, ok = obuf[ob % 2], f"ob{ob % 2}"
                        p.tv("vector", "tensor_mul", o[:, 0:W], ob_ps[ob][:, 0:W],
                             iv_b[:, 0:W], reads=(ob_keys[ob], "iv_b"), writes=(ok,))
                        if DEBUG_L0FFN and s == 0 and c == 0 and g0 == 0 and ob == 0:
                            p.dma("sync", dbg_ob[:, 0:W], o[:, 0:W],
                                  reads=(ok,), writes=("dbgob",))
                            p.tv("scalar", "copy", S1[:, 0:W], ob_ps[0][:, 0:W],
                                 reads=(ob_keys[0],), writes=("sq0", "dbgps_s"))
                            p.dma("sync", dbg_ps[:, 0:W], S1[:, 0:W],
                                  reads=("dbgps_s",), writes=("dbgps",))
                            p.dma("sync", dbg_ivb[:, 0:W], iv_b[:, 0:W],
                                  reads=("iv_b",), writes=("dbgivb",))

                        p.dma("gpsimd", xo_d[s * DM + ob * 128:s * DM + (ob + 1) * 128, g0:g0 + W],
                              o[:, 0:W], reads=(ok, "xo_init"),
                              writes=(f"xo_s{s}c{c}w{g0}o{ob}",), accum_op=ALU.add,
                              allow_slow_non_contiguous=(W == 1))
    return



_cache = {}


def _shard_x(x, core):
    xs = x[2 * core:2 * core + 2]
    return np.ascontiguousarray(xs.transpose(0, 2, 1)).reshape(NSEQ * DM, T)


def _unshard(res_list, key="x_out"):
    outs = []
    for r in res_list:
        xo = r[key].reshape(NSEQ, DM, T)
        outs.append(xo.transpose(0, 2, 1))
    return np.ascontiguousarray(np.concatenate(outs, axis=0))


def run_l0ffn(inputs, x_current, cores=8):
    if "l0ffn" not in _cache:
        _cache["l0ffn"] = build_l0ffn()
    nc = _cache["l0ffn"]
    w = {
        "w1T": np.ascontiguousarray(np.asarray(inputs["L0_ffn_w1"], dtype=np.float32).T),
        "w2T": np.ascontiguousarray(np.asarray(inputs["L0_ffn_w2"], dtype=np.float32).T),
        "convw": np.ascontiguousarray(np.asarray(inputs["L0_ffn_conv_w"], dtype=np.float32).reshape(32, 128, 3).transpose(1, 0, 2).reshape(128, 96)),
        "convb": np.ascontiguousarray(np.asarray(inputs["L0_ffn_conv_b"], dtype=np.float32).reshape(32, 128).T),
        "identf32": np.eye(128, dtype=np.float32),
    }
    in_maps = [dict(w, x_res=_shard_x(x_current, c)) for c in range(cores)]
    res = run_bass_kernel_spmd(nc, in_maps, core_ids=list(range(cores)))
    return _unshard([r for r in res.results])





DEBUG_L1FFN = False


def emit_l1ffn(p, x_d, xo_d, pfx="m_"):
    nc = p.nc
    CH = 512
    NE = 8
    DI = 4096
    NPAIR = NSEQ * 2
    NST = NPAIR * NE * 4

    attnT_d = nc.dram_tensor(pfx + "attnT", [DM, DM], F32, kind="ExternalInput").ap()
    gateT_d = nc.dram_tensor(pfx + "gateT", [128, NA * NE], F32, kind="ExternalInput").ap()
    gpT_d = nc.dram_tensor(pfx + "gpT", [NE * DM, DI], F32, kind="ExternalInput").ap()
    upT_d = nc.dram_tensor(pfx + "upT", [NE * DM, DI], F32, kind="ExternalInput").ap()
    dnT_d = nc.dram_tensor(pfx + "dnT", [NE * DI, DM], F32, kind="ExternalInput").ap()
    lnp_d = nc.dram_tensor(pfx + "lnp", [128, 4 * NA], F32, kind="ExternalInput").ap()
    idf_d = nc.dram_tensor(pfx + "identf32", [128, 128], F32, kind="ExternalInput").ap()
    scr2_d = nc.dram_tensor(pfx + "scr2", [NE, 2 * CH], F32, kind="Internal").ap()
    gpb_d = nc.dram_tensor(pfx + "gpb", [NE * DM, DI], BF16, kind="Internal").ap()
    upb_d = nc.dram_tensor(pfx + "upb", [NE * DM, DI], BF16, kind="Internal").ap()
    dnb_d = nc.dram_tensor(pfx + "dnb", [NE * DI, DM], BF16, kind="Internal").ap()

    with ExitStack() as st:
        sb = lambda n_, s, d: st.enter_context(nc.sbuf_tensor(pfx + n_, s, d))
        hbuf = sb("hbuf", [128, 2 * NA * CH], F32)
        tq = sb("tq", [128, 2 * NA * CH], BF16)
        S1 = sb("S1", [128, NA * CH], F32)
        gu = sb("gu", [128, 2 * 8 * CH], BF16)
        wG = [sb(f"wG{i}", [128, 8192], BF16) for i in range(2)]
        wU = [sb(f"wU{i}", [128, 8192], BF16) for i in range(2)]
        wD = sb("wD", [128, 8192], BF16)
        atw = sb("atw", [128, NA * 1024], BF16)
        combB = sb("combB", [128, 2 * CH], F32)
        otmp = sb("otmp", [128, 2 * CH], F32)
        gs = [sb(f"gs{i}", [128, CH], BF16) for i in range(2)]
        sc_b = sb("sc_b", [128, CH], F32)
        iv_b = sb("iv_b", [128, CH], F32)
        rows = sb("rows", [1, 4 * CH], F32)
        rows2 = sb("rows2", [1, CH], F32)
        rA = lambda i, W: rows[0:1, i * CH:i * CH + W]
        rB = lambda i, W: rows2[0:1, 0:W]
        lgt = sb("lgt", [8, 2 * CH], F32)
        lgt_tm = sb("lgt_tm", [128, 4 * 8], F32)
        comb_tm = sb("comb_tm", [128, 4 * 8], F32)
        gateT = sb("gateT_sb", [128, NA * NE], F32)
        gwg = sb("gwg_sb", [128, NA * NE], F32)
        b2gw = sb("b2gw", [1, NE], F32)
        lnp = sb("lnp_sb", [128, 4 * NA], F32)
        ones1 = sb("ones1", [1, CH], F32)
        onesc = sb("onesc", [128, 1], F32)
        idf = sb("idf", [128, 128], F32)
        e1c = sb("e1c", [128, 1], F32)
        m1c = sb("m1c", [128, 1], F32)
        m2c = sb("m2c", [128, 1], F32)
        w1c = sb("w1c", [128, 1], F32)
        w2c = sb("w2c", [128, 1], F32)
        eqA = sb("eqA", [128, 8], F32)
        eqB = sb("eqB", [128, 8], F32)
        mskd = sb("mskd", [128, 8], F32)

        ps = [st.enter_context(nc.psum_tensor(pfx + f"ps{i}", [128, CH], F32))
              for i in range(8)]
        PSK = ["psb0", "psb1", "psb2", "psb3", "ps_bc", "ps_st", "psb6", "psb7"]
        ps_st, ps_bc = ps[5], ps[4]
        gbank, gbkey = [ps[6], ps[4]], ["psb6", "ps_bc"]
        ubank, ubkey = [ps[7], ps[5]], ["psb7", "ps_st"]

        p.tv("vector", "memset", ones1[:], 1.0, writes=("ones1",))
        p.tv("vector", "memset", onesc[:], 1.0, writes=("onesc",))
        p.dma("sync", idf[:], idf_d, writes=("idf",))
        p.dma("sync", lnp[:], lnp_d, writes=("lnp",))
        p.dma("sync", gateT[:], gateT_d, writes=("gateT",))
        for a in range(NA):
            p.dma("gpsimd", atw[:, a * 1024:(a + 1) * 1024],
                  attnT_d[a * 128:(a + 1) * 128, :], writes=(f"atw{a}",))

        g1 = lambda a: lnp[:, 0 * NA + a:0 * NA + a + 1]
        b1 = lambda a: lnp[:, 1 * NA + a:1 * NA + a + 1]
        g2 = lambda a: lnp[:, 2 * NA + a:2 * NA + a + 1]
        b2 = lambda a: lnp[:, 3 * NA + a:3 * NA + a + 1]

        for a in range(NA):
            p.tv("vector", "tensor_scalar", gwg[:, a * NE:(a + 1) * NE],
                 gateT[:, a * NE:(a + 1) * NE], g2(a), None, ALU.mult,
                 reads=("gateT", "lnp"), writes=("gwg",))
        fns = [(lambda e, a=a: e.matmul(ps_st[0:1, 0:NE], b2(a), gateT[:, a * NE:(a + 1) * NE],
                                        start=(a == 0), stop=(a == NA - 1)))
               for a in range(NA)]
        p.group("tensor", fns, reads=("lnp", "gateT"), writes=("ps_st",))
        p.tv("vector", "tensor_copy", b2gw[:], ps_st[0:1, 0:NE], reads=("ps_st",), writes=("b2gw",))

        def stat_sums2(items):
            n = len(items)
            for (ap_, key, sap, skey) in items:
                p.tv("scalar", "activation", sap, ap_, AF.Square, reads=(key,), writes=(skey,))
            fns = [(lambda e, i=i, ap_=it[0]: e.matmul(ps_st[0:1, 0:CH], onesc[:, 0:1], ap_,
                                                       start=(i == 0), stop=(i == n - 1)))
                   for i, it in enumerate(items)]
            p.group("tensor", fns, reads=tuple(it[1] for it in items) + ("onesc",),
                    writes=("ps_st",))
            fns = [(lambda e, i=i, sap=it[2]: e.matmul(ps_st[32:33, 0:CH], onesc[:, 0:1], sap,
                                                       start=(i == 0), stop=(i == n - 1)))
                   for i, it in enumerate(items)]
            p.group("tensor", fns, reads=tuple(it[3] for it in items) + ("onesc",),
                    writes=("ps_st",))

        def ln_apply(j, src, skeys, dst, dkeys, gain=None, bias=None):
            ln_rows(p, rA, rB, CH, ps_st, DM, 1e-5)
            bcast_row(p, rA(2, CH), "rA2", CH, ps_bc, sc_b, "sc_b", ones1)
            bcast_row(p, rA(3, CH), "rA3", CH, ps_bc, iv_b, "iv_b", ones1)
            for a in range(NA):
                if gain is None:
                    p.tv("vector", "tensor_mul", dst(a), src(a), sc_b[:],
                         reads=(skeys[a], "sc_b"), writes=(dkeys[a],))
                    p.tv("vector", "tensor_sub", dst(a), dst(a), iv_b[:],
                         reads=(dkeys[a], "iv_b"), writes=(dkeys[a],))
                else:
                    p.tv("vector", "tensor_mul", S1[:, a * CH:(a + 1) * CH], src(a), sc_b[:],
                         reads=(skeys[a], "sc_b"), writes=(f"S1_{a}",))
                    p.tv("vector", "tensor_sub", S1[:, a * CH:(a + 1) * CH],
                         S1[:, a * CH:(a + 1) * CH], iv_b[:],
                         reads=(f"S1_{a}", "iv_b"), writes=(f"S1_{a}",))
                    p.tv("vector", "tensor_scalar", dst(a), S1[:, a * CH:(a + 1) * CH],
                         gain(a), bias(a), ALU.mult, ALU.add,
                         reads=(f"S1_{a}", "lnp"), writes=(dkeys[a],))

        def load_gu(g):
            pi, r2 = divmod(g, NE * 4)
            ex, q = divmod(r2, 4)
            par = g % 2
            first = (pi == 0)
            qeng = "gpsimd" if first else "sync"
            for a in range(NA):
                rs = ex * DM + a * 128
                p.dma(qeng, wG[par][:, a * 1024:(a + 1) * 1024],
                      (gpT_d if first else gpb_d)[rs:rs + 128, q * 1024:(q + 1) * 1024],
                      reads=() if first else (f"pcG{ex}_{q}_{a}",),
                      writes=(f"sG{par}_{a}",))
                p.dma(qeng, wU[par][:, a * 1024:(a + 1) * 1024],
                      (upT_d if first else upb_d)[rs:rs + 128, q * 1024:(q + 1) * 1024],
                      reads=() if first else (f"pcU{ex}_{q}_{a}",),
                      writes=(f"sU{par}_{a}",))
            if first:
                for a in range(NA):
                    rs = ex * DM + a * 128
                    p.dma("sync", gpb_d[rs:rs + 128, q * 1024:(q + 1) * 1024],
                          wG[par][:, a * 1024:(a + 1) * 1024],
                          reads=(f"sG{par}_{a}",), writes=(f"pcG{ex}_{q}_{a}",))
                    p.dma("sync", upb_d[rs:rs + 128, q * 1024:(q + 1) * 1024],
                          wU[par][:, a * 1024:(a + 1) * 1024],
                          reads=(f"sU{par}_{a}",), writes=(f"pcU{ex}_{q}_{a}",))

        def load_d(g):
            pi, r2 = divmod(g, NE * 4)
            ex, q = divmod(r2, 4)
            first = (pi == 0)
            qeng = "gpsimd" if first else "sync"
            for kk in range(NA):
                rsd = ex * DI + q * 1024 + kk * 128
                p.dma(qeng, wD[:, kk * 1024:(kk + 1) * 1024],
                      (dnT_d if first else dnb_d)[rsd:rsd + 128, 0:DM],
                      reads=() if first else (f"pcD{ex}_{q}_{kk}",),
                      writes=(f"sD{kk}",))
            if first:
                for kk in range(NA):
                    rsd = ex * DI + q * 1024 + kk * 128
                    p.dma("sync", dnb_d[rsd:rsd + 128, 0:DM], wD[:, kk * 1024:(kk + 1) * 1024],
                          reads=(f"sD{kk}",), writes=(f"pcD{ex}_{q}_{kk}",))

        def emit_gu(g):
            par = g % 2
            for j in range(2):
                for i in range(8):
                    gb, gk = gbank[i % 2], gbkey[i % 2]
                    ub, uk = ubank[i % 2], ubkey[i % 2]
                    fns = [(lambda e, a=a, i=i, gb=gb, par=par, j=j: e.matmul(
                        gb[:, 0:CH], wG[par][:, a * 1024 + i * 128:a * 1024 + (i + 1) * 128],
                        tq[:, (j * NA + a) * CH:(j * NA + a + 1) * CH],
                        start=(a == 0), stop=(a == NA - 1))) for a in range(NA)]
                    p.group("tensor", fns,
                            reads=tuple(f"sG{par}_{a}" for a in range(NA))
                            + tuple(f"tq{j}{a}" for a in range(NA)), writes=(gk,))
                    fns = [(lambda e, a=a, i=i, ub=ub, par=par, j=j: e.matmul(
                        ub[:, 0:CH], wU[par][:, a * 1024 + i * 128:a * 1024 + (i + 1) * 128],
                        tq[:, (j * NA + a) * CH:(j * NA + a + 1) * CH],
                        start=(a == 0), stop=(a == NA - 1))) for a in range(NA)]
                    p.group("tensor", fns,
                            reads=tuple(f"sU{par}_{a}" for a in range(NA))
                            + tuple(f"tq{j}{a}" for a in range(NA)), writes=(uk,))
                    p.tv("scalar", "activation", gs[i % 2][:, 0:CH], gb[:, 0:CH], AF.Silu,
                         reads=(gk,), writes=(f"gsl{i % 2}",))
                    p.tv("vector", "tensor_mul", gu[:, (j * 8 + i) * CH:(j * 8 + i + 1) * CH],
                         gs[i % 2][:, 0:CH], ub[:, 0:CH],
                         reads=(f"gsl{i % 2}", uk), writes=(f"gu{j}{i}",))

        def emit_down(g, ex):
            for j in range(2):
                for obh in range(2):
                    for ob4 in range(4):
                        ob = obh * 4 + ob4
                        bank, bkey = ps[ob4], PSK[ob4]
                        fns = [(lambda e, kk=kk, ob=ob, bank=bank, j=j: e.matmul(
                            bank[:, 0:CH], wD[:, kk * 1024 + ob * 128:kk * 1024 + (ob + 1) * 128],
                            gu[:, (j * 8 + kk) * CH:(j * 8 + kk + 1) * CH],
                            start=(kk == 0), stop=(kk == 7))) for kk in range(8)]
                        p.group("tensor", fns,
                                reads=tuple(f"sD{kk}" for kk in range(8))
                                + tuple(f"gu{j}{kk}" for kk in range(8)), writes=(bkey,))
                        p.tv("vector", "tensor_mul", otmp[:, (ob4 % 2) * CH:(ob4 % 2 + 1) * CH],
                             bank[:, 0:CH], combB[:, j * CH:(j + 1) * CH],
                             reads=(bkey, "cmb0"), writes=(f"ot{ob4 % 2}",))
                        p.tv("gpsimd", "tensor_tensor",
                             hbuf[:, (j * NA + ob) * CH:(j * NA + ob + 1) * CH],
                             hbuf[:, (j * NA + ob) * CH:(j * NA + ob + 1) * CH],
                             otmp[:, (ob4 % 2) * CH:(ob4 % 2 + 1) * CH], ALU.add,
                             reads=(f"h{j}{ob}", f"ot{ob4 % 2}"), writes=(f"h{j}{ob}",))

        def routing_chunk(s, half, j):
            base = half * 1024 + j * CH
            H = lambda a: hbuf[:, (j * NA + a) * CH:(j * NA + a + 1) * CH]
            SL = lambda a: S1[:, a * CH:(a + 1) * CH]
            GQ = lambda a: gu[:, (j * 8 + a) * CH:(j * 8 + a + 1) * CH]
            TQ = lambda a: tq[:, (j * NA + a) * CH:(j * NA + a + 1) * CH]
            hk = [f"h{j}{a}" for a in range(NA)]
            sk = [f"S1_{a}" for a in range(NA)]
            for a in range(NA):
                p.dma("gpsimd", SL(a),
                      x_d[s * DM + a * 128:s * DM + (a + 1) * 128, base:base + CH],
                      writes=(sk[a],))
            stat_sums2([(SL(a), sk[a], H(a), hk[a]) for a in range(NA)])
            ln_apply(j, SL, sk, H, hk)
            stat_sums2([(H(a), hk[a], SL(a), sk[a]) for a in range(NA)])
            ln_apply(j, H, hk, GQ, [f"gu{j}{a}" for a in range(NA)], gain=g1, bias=b1)
            for ob in range(NA):
                bank, bkey = ps[ob % 4], PSK[ob % 4]
                fns = [(lambda e, a=a, ob=ob, bank=bank, ga=None, j=j: e.matmul(
                    bank[:, 0:CH], atw[:, a * 1024 + ob * 128:a * 1024 + (ob + 1) * 128],
                    gu[:, (j * 8 + a) * CH:(j * 8 + a + 1) * CH],
                    start=(a == 0), stop=(a == NA - 1))) for a in range(NA)]
                p.group("tensor", fns,
                        reads=tuple(f"atw{a}" for a in range(NA))
                        + tuple(f"gu{j}{a}" for a in range(NA)), writes=(bkey,))
                p.tv("vector", "tensor_add", H(ob), bank[:, 0:CH], H(ob),
                     reads=(bkey, hk[ob]), writes=(hk[ob],))
            stat_sums2([(H(a), hk[a], SL(a), sk[a]) for a in range(NA)])
            ln_apply(j, H, hk, TQ, [f"tq{j}{a}" for a in range(NA)], gain=g2, bias=b2)
            fns = [(lambda e, a=a: e.matmul(
                ps_st[0:8, 0:CH], gwg[:, a * NE:(a + 1) * NE],
                S1[:, a * CH:(a + 1) * CH], start=(a == 0), stop=False))
                for a in range(NA)]
            fns.append(lambda e: e.matmul(ps_st[0:8, 0:CH], b2gw[:], ones1[0:1, 0:CH],
                                          start=False, stop=True))
            p.group("tensor", fns,
                    reads=("gwg", "b2gw", "ones1") + tuple(sk), writes=("ps_st",))
            p.tv("scalar", "copy", lgt[0:8, j * CH:(j + 1) * CH], ps_st[0:8, 0:CH],
                 reads=("ps_st",), writes=(f"lgt{j}",))
            for blk in range(4):
                lk = f"lgt{j}"
                lcol = j * CH + blk * 128
                p.tv("tensor", "transpose", ps_st[0:128, 0:8],
                     lgt[0:8, lcol:lcol + 128], idf[0:8, 0:8],
                     reads=(lk, "idf"), writes=("ps_st",))
                p.tv("vector", "tensor_copy", lgt_tm[:, blk * 8:(blk + 1) * 8],
                     ps_st[0:128, 0:8], reads=("ps_st",), writes=(f"ltm{blk}",))
                lt = lgt_tm[:, blk * 8:(blk + 1) * 8]
                ltk = f"ltm{blk}"
                p.tv("vector", "reduce_max", m1c[:], lt, AX.X, reads=(ltk,), writes=("m1c",))
                p.tv("vector", "tensor_scalar", eqA[:], lt, m1c[:, 0:1], None, ALU.is_equal,
                     reads=(ltk, "m1c"), writes=("eqA",))
                p.tv("vector", "scalar_tensor_tensor", mskd[:], eqA[:], -1e30, lt,
                     ALU.mult, ALU.add, reads=("eqA", ltk), writes=("mskd",))
                p.tv("vector", "reduce_max", m2c[:], mskd[:], AX.X,
                     reads=("mskd",), writes=("m2c",))
                p.tv("vector", "tensor_scalar", eqB[:], mskd[:], m2c[:, 0:1], None,
                     ALU.is_equal, reads=("mskd", "m2c"), writes=("eqB",))
                p.tv("vector", "tensor_sub", e1c[:], m2c[:], m1c[:],
                     reads=("m1c", "m2c"), writes=("e1c",))
                p.tv("scalar", "activation", e1c[:], e1c[:], AF.Exp,
                     reads=("e1c",), writes=("e1c",))
                p.tv("vector", "tensor_scalar", w1c[:], e1c[:], 1.0, None, ALU.add,
                     reads=("e1c",), writes=("w1c",))
                p.tv("vector", "reciprocal", w1c[:], w1c[:], reads=("w1c",), writes=("w1c",))
                p.tv("vector", "tensor_mul", w2c[:], e1c[:], w1c[:],
                     reads=("e1c", "w1c"), writes=("w2c",))
                p.tv("vector", "tensor_scalar", eqA[:], eqA[:], w1c[:, 0:1], None, ALU.mult,
                     reads=("eqA", "w1c"), writes=("eqA",))
                p.tv("vector", "scalar_tensor_tensor", comb_tm[:, blk * 8:(blk + 1) * 8],
                     eqB[:], w2c[:, 0:1], eqA[:], ALU.mult, ALU.add,
                     reads=("eqB", "w2c", "eqA"), writes=(f"ctm{blk}",))
                p.tv("tensor", "transpose", ps_st[0:8, 0:128],
                     comb_tm[:, blk * 8:(blk + 1) * 8], idf[:],
                     reads=(f"ctm{blk}", "idf"), writes=("ps_st",))
                p.tv("vector", "tensor_copy", lgt[0:8, lcol:lcol + 128],
                     ps_st[0:8, 0:128], reads=("ps_st",), writes=(lk,))

        load_gu(0)
        load_d(0)
        for pair in range(NPAIR):
            s, half = divmod(pair, 2)
            for j in range(2):
                routing_chunk(s, half, j)
            p.dma("gpsimd", scr2_d, lgt[0:8, :], reads=("lgt0", "lgt1"), writes=("scr2",))
            for ex in range(NE):
                p.dma("gpsimd", combB[:, 0:2 * CH],
                      scr2_d[ex:ex + 1, :].partition_broadcast(128),
                      reads=("scr2",), writes=("cmb0",))
                for q in range(4):
                    g = pair * 32 + ex * 4 + q
                    if g + 1 < NST:
                        load_gu(g + 1)
                    emit_gu(g)
                    emit_down(g, ex)
                    if g + 1 < NST:
                        load_d(g + 1)
            for j in range(2):
                base = half * 1024 + j * CH
                for a in range(NA):
                    p.dma("gpsimd", S1[:, a * CH:(a + 1) * CH],
                          x_d[s * DM + a * 128:s * DM + (a + 1) * 128, base:base + CH],
                          writes=(f"S1_{a}",))
                for ob in range(NA):
                    p.tv("vector", "tensor_add", otmp[:, (ob % 2) * CH:(ob % 2 + 1) * CH],
                         S1[:, ob * CH:(ob + 1) * CH],
                         hbuf[:, (j * NA + ob) * CH:(j * NA + ob + 1) * CH],
                         reads=(f"S1_{ob}", f"h{j}{ob}"), writes=(f"ot{ob % 2}",))
                    p.dma("gpsimd",
                          xo_d[s * DM + ob * 128:s * DM + (ob + 1) * 128, base:base + CH],
                          otmp[:, (ob % 2) * CH:(ob % 2 + 1) * CH],
                          reads=(f"ot{ob % 2}",), writes=(f"xo{pair}{j}{ob}",))
    return


def run_l1ffn(inputs, x_current, cores=8):
    if "l1ffn" not in _cache:
        _cache["l1ffn"] = build_l1ffn()
    nc = _cache["l1ffn"]
    f32 = lambda v: np.asarray(v, dtype=np.float32)
    w = {
        "attnT": np.ascontiguousarray(f32(inputs["L1_ffn_attn_w"]).T),
        "gateT": np.ascontiguousarray(f32(inputs["L1_ffn_gate_w"]).T.reshape(8, 128, 8)
                                      .transpose(1, 0, 2).reshape(128, 64)),
        "gpT": np.ascontiguousarray(f32(inputs["L1_ffn_gate_proj"]).transpose(0, 2, 1)).reshape(8 * 1024, 4096),
        "upT": np.ascontiguousarray(f32(inputs["L1_ffn_up_proj"]).transpose(0, 2, 1)).reshape(8 * 1024, 4096),
        "dnT": np.ascontiguousarray(f32(inputs["L1_ffn_down_proj"]).transpose(0, 2, 1)).reshape(8 * 4096, 1024),
        "lnp": np.ascontiguousarray(np.stack([
            f32(inputs["L1_ffn_ln1_g"]).reshape(8, 128),
            f32(inputs["L1_ffn_ln1_b"]).reshape(8, 128),
            f32(inputs["L1_ffn_ln2_g"]).reshape(8, 128),
            f32(inputs["L1_ffn_ln2_b"]).reshape(8, 128),
        ], axis=0).reshape(32, 128).T.reshape(128, 32)),
        "identf32": np.eye(128, dtype=np.float32),
    }
    in_maps = [dict(w, x_res=_shard_x(x_current, c)) for c in range(cores)]
    res = run_bass_kernel_spmd(nc, in_maps, core_ids=list(range(cores)))
    return _unshard([r for r in res.results])



DEBUG_L1ATTN = False


def emit_l1attn(p, x_d, xo_d, pfx="r_"):
    nc = p.nc
    CH = 512

    wk_d = nc.dram_tensor(pfx + "wk", [DM, DM], F32, kind="ExternalInput").ap()
    wv_d = nc.dram_tensor(pfx + "wv", [DM, DM], F32, kind="ExternalInput").ap()
    wr_d = nc.dram_tensor(pfx + "wr", [DM, DM], F32, kind="ExternalInput").ap()
    wo_d = nc.dram_tensor(pfx + "wo", [DM, DM], F32, kind="ExternalInput").ap()
    cmk_d = nc.dram_tensor(pfx + "cmk", [DM, 4 * DM], F32, kind="ExternalInput").ap()
    cmr_d = nc.dram_tensor(pfx + "cmr", [DM, DM], F32, kind="ExternalInput").ap()
    cmv_d = nc.dram_tensor(pfx + "cmv", [4 * DM, DM], F32, kind="ExternalInput").ap()
    par_d = nc.dram_tensor(pfx + "par", [128, 11 * NA], F32, kind="ExternalInput").ap()

    with ExitStack() as st:
        sb = lambda n_, s, d: st.enter_context(nc.sbuf_tensor(pfx + n_, s, d))
        psb = lambda n_, s: st.enter_context(nc.psum_tensor(pfx + n_, s, F32))

        xcur = sb("xcur", [128, NA * CH], F32)
        xin = sb("xin", [128, NA * CH], F32)
        hm = sb("hm", [128, NA * CH], BF16)
        tmlead = sb("tmlead", [128, NA], F32)
        cmlead = sb("cmlead", [128, NA], F32)
        mxk = sb("mxk", [128, NA * CH], BF16)
        mxv = sb("mxv", [128, NA * CH], BF16)
        mxr = sb("mxr", [128, NA * CH], BF16)
        rw = sb("rw", [128, NA * CH], BF16)
        kcq = sb("kcq", [128, 32 * CH], BF16)
        wbufW = sb("wbufW", [128, 32768], BF16)
        par = sb("par_sb", [128, 11 * NA], F32)
        sqt = [sb(f"sqt{i}", [128, CH], F32) for i in range(2)]
        ek = sb("ek", [128, CH], F32)
        vv = sb("vv", [128, CH], F32)
        ekv = sb("ekv", [128, CH], F32)
        dtile = sb("dtile", [128, CH], F32)
        scanA = sb("scanA", [128, CH + 1], F32)
        scanB = sb("scanB", [128, CH + 1], F32)
        wkvb = None
        carA = sb("carA", [128, NA], F32)
        carB = sb("carB", [128, NA], F32)
        dcol = sb("dcol", [128, NA], F32)
        eucol = sb("eucol", [128, NA], F32)
        obuf = sb("obuf", [128, CH], F32)
        rows = sb("rows", [1, 4 * 512], F32)
        rows2 = sb("rows2", [1, 2 * 512], F32)
        rA = lambda i, W: rows[0:1, i * 512:i * 512 + W]
        rB = lambda i, W: rows2[0:1, (0 if i == 0 else 1) * 512:(0 if i == 0 else 1) * 512 + W]
        sc_b = sb("sc_b", [128, 512], F32)
        iv_b = sb("iv_b", [128, 512], F32)
        ones1 = sb("ones1", [1, 128], F32)
        onesc = sb("onesc", [128, 1], F32)

        ps_k = psb("ps_k", [128, 512])
        ps_v = psb("ps_v", [128, 512])
        ps_r = psb("ps_r", [128, 512])
        ps_o = psb("ps_o", [128, 512])
        ps_o2 = psb("ps_o2", [128, 512])
        ps_st = psb("ps_st", [128, 512])
        ps_bc = psb("ps_bc", [128, 512])

        p.tv("vector", "memset", ones1[:], 1.0, writes=("ones1",))
        p.tv("vector", "memset", onesc[:], 1.0, writes=("onesc",))
        p.dma("sync", par[:], par_d, writes=("par",))
        p.dma("sync", xo_d, x_d, writes=("xo_init",))

        PG = lambda g, a: par[:, g * NA + a:g * NA + a + 1]
        ln1g, ln1b = (lambda a: PG(0, a)), (lambda a: PG(1, a))
        ln2g, ln2b = (lambda a: PG(2, a)), (lambda a: PG(3, a))
        tmk, tmv, tmr = (lambda a: PG(4, a)), (lambda a: PG(5, a)), (lambda a: PG(6, a))
        cmkx, cmrx = (lambda a: PG(9, a)), (lambda a: PG(10, a))

        p.tv("scalar", "activation", dcol[:], par[:, 7 * NA:8 * NA], AF.Exp,
             reads=("par",), writes=("dcol",))
        p.tv("vector", "tensor_scalar", dcol[:], dcol[:], -1.0, None, ALU.mult,
             reads=("dcol",), writes=("dcol",))
        p.tv("scalar", "activation", dcol[:], dcol[:], AF.Exp, reads=("dcol",), writes=("dcol",))
        p.tv("scalar", "activation", eucol[:], par[:, 8 * NA:9 * NA], AF.Exp,
             reads=("par",), writes=("eucol",))

        def stats_seq(src, skeys, W):
            n = len(skeys)
            for a in range(n):
                sq, sk = sqt[a % 2], f"sqt{a % 2}"
                p.tv("scalar", "activation", sq[:, 0:W], src[:, a * W:(a + 1) * W],
                     AF.Square, reads=(skeys[a],), writes=(sk,))
                p.emit("tensor", lambda e, a=a, W=W, src=src, n=n: e.matmul(
                    ps_st[0:1, 0:W], onesc[:, 0:1], src[:, a * W:(a + 1) * W],
                    start=(a == 0), stop=(a == n - 1)),
                    reads=(skeys[a], "onesc"), writes=("ps_st",))
                p.emit("tensor", lambda e, a=a, W=W, sq=sq, n=n: e.matmul(
                    ps_st[32:33, 0:W], onesc[:, 0:1], sq[:, 0:W],
                    start=(a == 0), stop=(a == n - 1)),
                    reads=(sk, "onesc"), writes=("ps_st",))

        def ln_full(src, skeys, dst, dkeys, W, eps, gain=None, bias=None):
            stats_seq(src, skeys, W)
            rstd, mur = ln_rows(p, rA, rB, W, ps_st, DM, eps)
            bcast_row(p, rstd, "rA2", W, ps_bc, sc_b, "sc_b", ones1)
            bcast_row(p, mur, "rA3", W, ps_bc, iv_b, "iv_b", ones1)
            for a in range(NA):
                off = a * W
                sq, sk = sqt[a % 2], f"sqt{a % 2}"
                p.tv("vector", "tensor_mul", sq[:, 0:W], src[:, a * W:(a + 1) * W],
                     sc_b[:, 0:W], reads=(skeys[a], "sc_b"), writes=(sk,))
                if gain is None:
                    p.tv("vector", "tensor_sub", dst[:, off:off + W], sq[:, 0:W],
                         iv_b[:, 0:W], reads=(sk, "iv_b"), writes=(dkeys[a],))
                else:
                    p.tv("vector", "tensor_sub", sq[:, 0:W], sq[:, 0:W], iv_b[:, 0:W],
                         reads=(sk, "iv_b"), writes=(sk,))
                    p.tv("vector", "tensor_scalar", dst[:, off:off + W], sq[:, 0:W],
                         gain(a), bias(a), ALU.mult, ALU.add,
                         reads=(sk, "par"), writes=(dkeys[a],))

        def mixes(dsts, leadbuf, leadkey):
            for a in range(NA):
                off = a * CH
                cur1 = hm[:, off + 1:off + CH]
                shf1 = hm[:, off:off + CH - 1]
                dif = sqt[0]
                p.tv("vector", "tensor_sub", dif[:, 1:CH], cur1, shf1,
                     reads=(f"hm{a}",), writes=("sqt0",))
                p.tv("vector", "tensor_sub", dif[:, 0:1], hm[:, off:off + 1],
                     leadbuf[:, a:a + 1], reads=(f"hm{a}", leadkey), writes=("sqt0",))
                for (dst, kp, mc) in dsts:
                    p.tv("vector", "scalar_tensor_tensor", dst[:, a * CH + 1:(a + 1) * CH],
                         dif[:, 1:CH], mc(a), shf1, ALU.mult, ALU.add,
                         reads=("sqt0", "par", f"hm{a}"), writes=(f"{kp}{a}",))
                    p.tv("vector", "scalar_tensor_tensor", dst[:, a * CH:a * CH + 1],
                         dif[:, 0:1], mc(a), leadbuf[:, a:a + 1], ALU.mult, ALU.add,
                         reads=("sqt0", "par", leadkey), writes=(f"{kp}{a}",))

        def mm_group(psum, pk, lhsf, rhs, rhs_keys, n, extra_reads=()):
            fns = [(lambda e, a=a: e.matmul(psum[:, 0:CH], lhsf(a),
                                            rhs[:, a * CH:(a + 1) * CH],
                                            start=(a == 0), stop=(a == n - 1)))
                   for a in range(n)]
            p.group("tensor", fns, reads=tuple(rhs_keys) + tuple(extra_reads), writes=(pk,))

        for s in range(NSEQ):
            p.tv("vector", "memset", carA[:], 0.0, writes=("carA",))
            p.tv("vector", "memset", carB[:], 0.0, writes=("carB",))
            p.tv("vector", "memset", tmlead[:], 0.0, writes=("tmlead",))
            p.tv("vector", "memset", cmlead[:], 0.0, writes=("cmlead",))
            for c in range(4):
                base = c * CH
                for a in range(NA):
                    p.dma("sync", xcur[:, a * CH:(a + 1) * CH],
                          x_d[s * DM + a * 128:s * DM + (a + 1) * 128, base:base + CH],
                          writes=(f"xcur{a}",))
                for wi, wd in enumerate((wk_d, wv_d, wr_d, wo_d)):
                    for a in range(NA):
                        p.dma("gpsimd", wbufW[:, wi * 8192 + a * DM:wi * 8192 + (a + 1) * DM],
                              wd[a * 128:(a + 1) * 128, :], writes=("wbufW",))
                xink = [f"xin{a}" for a in range(NA)]
                ln_full(xcur, [f"xcur{a}" for a in range(NA)], xin, xink, CH, 1e-5)
                ln_full(xin, xink, hm, [f"hm{a}" for a in range(NA)], CH, 1e-3,
                        gain=ln1g, bias=ln1b)
                mixes([(mxk, "mxk", tmk), (mxv, "mxv", tmv), (mxr, "mxr", tmr)],
                      tmlead, "tmlead")
                for a in range(NA):
                    p.tv("gpsimd", "tensor_copy", tmlead[:, a:a + 1],
                         hm[:, a * CH + CH - 1:a * CH + CH],
                         reads=(f"hm{a}",), writes=("tmlead",))
                for a in range(NA):
                    mk = tuple(f"mxk{i}" for i in range(NA))
                    mv = tuple(f"mxv{i}" for i in range(NA))
                    mr = tuple(f"mxr{i}" for i in range(NA))
                    mm_group(ps_k, "ps_k",
                             lambda e_, a=a: wbufW[:, 0 * 8192 + e_ * DM + a * 128:
                                                   0 * 8192 + e_ * DM + (a + 1) * 128],
                             mxk, mk, NA, extra_reads=("wbufW",))
                    mm_group(ps_v, "ps_v",
                             lambda e_, a=a: wbufW[:, 1 * 8192 + e_ * DM + a * 128:
                                                   1 * 8192 + e_ * DM + (a + 1) * 128],
                             mxv, mv, NA, extra_reads=("wbufW",))
                    mm_group(ps_r, "ps_r",
                             lambda e_, a=a: wbufW[:, 2 * 8192 + e_ * DM + a * 128:
                                                   2 * 8192 + e_ * DM + (a + 1) * 128],
                             mxr, mr, NA, extra_reads=("wbufW",))
                    p.tv("scalar", "activation", ek[:, 0:CH], ps_k[:, 0:CH], AF.Exp,
                         reads=("ps_k",), writes=("ek",))
                    p.tv("scalar", "copy", vv[:, 0:CH], ps_v[:, 0:CH],
                         reads=("ps_v",), writes=("vv",))
                    p.tv("vector", "tensor_mul", ekv[:, 0:CH], ek[:, 0:CH], vv[:, 0:CH],
                         reads=("ek", "vv"), writes=("ekv",))
                    p.tv("vector", "memset", dtile[:, 0:CH], 1.0, writes=("dtile",))
                    p.tv("vector", "tensor_scalar", dtile[:, 0:CH], dtile[:, 0:CH],
                         dcol[:, a:a + 1], None, ALU.mult,
                         reads=("dtile", "dcol"), writes=("dtile",))
                    p.tv("vector", "tensor_copy", scanA[:, 0:1], carA[:, a:a + 1],
                         reads=("carA",), writes=("scanA",))
                    p.tv("vector", "tensor_tensor_scan", scanA[:, 1:CH + 1], dtile[:, 0:CH],
                         ekv[:, 0:CH], carA[:, a:a + 1], ALU.mult, ALU.add,
                         reads=("dtile", "ekv", "carA"), writes=("scanA",))
                    p.tv("vector", "tensor_copy", scanB[:, 0:1], carB[:, a:a + 1],
                         reads=("carB",), writes=("scanB",))
                    p.tv("vector", "tensor_tensor_scan", scanB[:, 1:CH + 1], dtile[:, 0:CH],
                         ek[:, 0:CH], carB[:, a:a + 1], ALU.mult, ALU.add,
                         reads=("dtile", "ek", "carB"), writes=("scanB",))
                    p.tv("vector", "tensor_copy", carA[:, a:a + 1], scanA[:, CH:CH + 1],
                         reads=("scanA",), writes=("carA",))
                    p.tv("vector", "tensor_copy", carB[:, a:a + 1], scanB[:, CH:CH + 1],
                         reads=("scanB",), writes=("carB",))
                    p.tv("vector", "tensor_scalar", ek[:, 0:CH], ek[:, 0:CH],
                         eucol[:, a:a + 1], None, ALU.mult,
                         reads=("ek", "eucol"), writes=("ek",))
                    p.tv("vector", "tensor_mul", ekv[:, 0:CH], ek[:, 0:CH], vv[:, 0:CH],
                         reads=("ek", "vv"), writes=("ekv",))
                    p.tv("vector", "tensor_add", dtile[:, 0:CH], scanB[:, 0:CH], ek[:, 0:CH],
                         reads=("scanB", "ek", "dtile"), writes=("dtile",))
                    p.tv("vector", "reciprocal", dtile[:, 0:CH], dtile[:, 0:CH],
                         reads=("dtile",), writes=("dtile",))
                    p.tv("vector", "tensor_add", ekv[:, 0:CH], scanA[:, 0:CH], ekv[:, 0:CH],
                         reads=("scanA", "ekv"), writes=("ekv",))
                    p.tv("vector", "tensor_mul", ekv[:, 0:CH], ekv[:, 0:CH], dtile[:, 0:CH],
                         reads=("ekv", "dtile"), writes=("ekv",))
                    p.tv("scalar", "activation", ek[:, 0:CH], ps_r[:, 0:CH], AF.Sigmoid,
                         reads=("ps_r",), writes=("ek",))
                    p.tv("vector", "tensor_mul", rw[:, a * CH:(a + 1) * CH], ek[:, 0:CH],
                         ekv[:, 0:CH], reads=("ek", "ekv"), writes=(f"rw{a}",))
                for ob in range(NA):
                    ps, pk = (ps_o, "ps_o") if ob % 2 == 0 else (ps_o2, "ps_o2")
                    mm_group(ps, pk,
                             lambda e_, ob=ob: wbufW[:, 3 * 8192 + e_ * DM + ob * 128:
                                                     3 * 8192 + e_ * DM + (ob + 1) * 128],
                             rw, tuple(f"rw{i}" for i in range(NA)), NA,
                             extra_reads=("wbufW",))
                    p.tv("vector", "tensor_add", xin[:, ob * CH:(ob + 1) * CH],
                         ps[:, 0:CH], xin[:, ob * CH:(ob + 1) * CH],
                         reads=(pk, f"xin{ob}"), writes=(f"xin{ob}",))
                ln_full(xin, xink, hm, [f"hm{a}" for a in range(NA)], CH, 1e-3,
                        gain=ln2g, bias=ln2b)
                mixes([(mxk, "mxk", cmkx), (mxr, "mxr", cmrx)], cmlead, "cmlead")
                for a in range(NA):
                    p.tv("gpsimd", "tensor_copy", cmlead[:, a:a + 1],
                         hm[:, a * CH + CH - 1:a * CH + CH],
                         reads=(f"hm{a}",), writes=("cmlead",))
                for a in range(NA):
                    p.dma("gpsimd", wbufW[:, a * 4096:(a + 1) * 4096],
                          cmk_d[a * 128:(a + 1) * 128, :], writes=("wbufW",))
                for ob in range(32):
                    ps, pk = (ps_o, "ps_o") if ob % 2 == 0 else (ps_o2, "ps_o2")
                    mm_group(ps, pk,
                             lambda e_, ob=ob: wbufW[:, e_ * 4096 + ob * 128:
                                                     e_ * 4096 + (ob + 1) * 128],
                             mxk, tuple(f"mxk{i}" for i in range(NA)), NA,
                             extra_reads=("wbufW",))
                    sq, sk = sqt[ob % 2], f"sqt{ob % 2}"
                    p.tv("scalar", "activation", sq[:, 0:CH], ps[:, 0:CH], AF.Relu,
                         reads=(pk,), writes=(sk,))
                    p.tv("scalar", "activation", kcq[:, ob * CH:(ob + 1) * CH], sq[:, 0:CH],
                         AF.Square, reads=(sk,), writes=(f"kcq{ob}",))
                for a in range(NA):
                    p.dma("gpsimd", wbufW[:, a * DM:(a + 1) * DM],
                          cmr_d[a * 128:(a + 1) * 128, :], writes=("wbufW",))
                for ob in range(NA):
                    ps, pk = (ps_o, "ps_o") if ob % 2 == 0 else (ps_o2, "ps_o2")
                    mm_group(ps, pk,
                             lambda e_, ob=ob: wbufW[:, e_ * DM + ob * 128:
                                                     e_ * DM + (ob + 1) * 128],
                             mxr, tuple(f"mxr{i}" for i in range(NA)), NA,
                             extra_reads=("wbufW",))
                    p.tv("scalar", "activation", xcur[:, ob * CH:(ob + 1) * CH], ps[:, 0:CH],
                         AF.Sigmoid, reads=(pk,), writes=(f"xcur{ob}",))
                for k in range(32):
                    p.dma("gpsimd", wbufW[:, k * DM:(k + 1) * DM],
                          cmv_d[k * 128:(k + 1) * 128, :], writes=("wbufW",))
                for ob in range(NA):
                    ps, pk = (ps_o, "ps_o") if ob % 2 == 0 else (ps_o2, "ps_o2")
                    fns = [(lambda e, k=k, ob=ob, ps=ps: e.matmul(
                        ps[:, 0:CH], wbufW[:, k * DM + ob * 128:k * DM + (ob + 1) * 128],
                        kcq[:, k * CH:(k + 1) * CH], start=(k == 0), stop=(k == 31)))
                        for k in range(32)]
                    p.group("tensor", fns,
                            reads=("wbufW",) + tuple(f"kcq{k}" for k in range(32)),
                            writes=(pk,))
                    p.tv("vector", "tensor_mul", obuf[:, 0:CH], ps[:, 0:CH],
                         xcur[:, ob * CH:(ob + 1) * CH], reads=(pk, f"xcur{ob}"),
                         writes=("obuf",))
                    p.tv("vector", "tensor_add", obuf[:, 0:CH], obuf[:, 0:CH],
                         xin[:, ob * CH:(ob + 1) * CH], reads=("obuf", f"xin{ob}"),
                         writes=("obuf",))
                    p.dma("gpsimd", xo_d[s * DM + ob * 128:s * DM + (ob + 1) * 128, base:base + CH],
                          obuf[:, 0:CH], reads=("obuf", "xo_init"),
                          writes=(f"xo{s}_{c}_{ob}",), accum_op=ALU.add)
    return


def run_l1attn(inputs, x_current, cores=8):
    if "l1attn" not in _cache:
        _cache["l1attn"] = build_l1attn()
    nc = _cache["l1attn"]
    f32 = lambda v: np.asarray(v, dtype=np.float32)
    ctile = lambda v: np.ascontiguousarray(f32(v).reshape(8, 128).T)
    par = np.concatenate([
        ctile(inputs["L1_attn_ln1_g"]), ctile(inputs["L1_attn_ln1_b"]),
        ctile(inputs["L1_attn_ln2_g"]), ctile(inputs["L1_attn_ln2_b"]),
        ctile(inputs["L1_attn_tm_mix_k"]), ctile(inputs["L1_attn_tm_mix_v"]),
        ctile(inputs["L1_attn_tm_mix_r"]), ctile(inputs["L1_attn_time_decay"]),
        ctile(inputs["L1_attn_time_first"]), ctile(inputs["L1_attn_cm_mix_k"]),
        ctile(inputs["L1_attn_cm_mix_r"])], axis=1)
    w = {
        "wk": np.ascontiguousarray(f32(inputs["L1_attn_tm_Wk"])),
        "wv": np.ascontiguousarray(f32(inputs["L1_attn_tm_Wv"])),
        "wr": np.ascontiguousarray(f32(inputs["L1_attn_tm_Wr"])),
        "wo": np.ascontiguousarray(f32(inputs["L1_attn_Wo"])),
        "cmk": np.ascontiguousarray(f32(inputs["L1_attn_cm_Wk"])),
        "cmr": np.ascontiguousarray(f32(inputs["L1_attn_cm_Wr"])),
        "cmv": np.ascontiguousarray(f32(inputs["L1_attn_cm_Wv"])),
        "par": np.ascontiguousarray(par),
    }
    in_maps = [dict(w, x_res=_shard_x(x_current, c)) for c in range(cores)]
    res = run_bass_kernel_spmd(nc, in_maps, core_ids=list(range(cores)))
    return _unshard([r for r in res.results])



DEBUG_L0ATTN = False


def emit_l0attn(p, x_d, xo_d, pfx="a_"):
    nc = p.nc
    CW = 512
    DI = 2048
    NXT = 16
    NBC = 17
    SC = 64

    ipT_d = nc.dram_tensor(pfx + "ipT", [DM, 4256], F32, kind="ExternalInput").ap()
    opT_d = nc.dram_tensor(pfx + "opT", [DI, DM], F32, kind="ExternalInput").ap()
    convw_d = nc.dram_tensor(pfx + "convw", [128, NBC * 4], F32, kind="ExternalInput").ap()
    convb_d = nc.dram_tensor(pfx + "convb", [128, NBC], F32, kind="ExternalInput").ap()
    lnwb_d = nc.dram_tensor(pfx + "lnwb", [128, 2 * NA], F32, kind="ExternalInput").ap()
    normw_d = nc.dram_tensor(pfx + "normw", [128, NXT], F32, kind="ExternalInput").ap()
    hpar_d = nc.dram_tensor(pfx + "hpar", [32, 3], F32, kind="ExternalInput").ap()
    ecst_d = nc.dram_tensor(pfx + "ecst", [32, DI], F32, kind="ExternalInput").ap()
    triu_d = nc.dram_tensor(pfx + "triu", [64, 64], BF16, kind="ExternalInput").ap()
    ecst2_d = nc.dram_tensor(pfx + "ecst2", [64, 2048], BF16, kind="ExternalInput").ap()
    idb_d = nc.dram_tensor(pfx + "identb16", [128, 128], BF16, kind="ExternalInput").ap()
    scr3_d = nc.dram_tensor(pfx + "scr3", [1, DI], F32, kind="Internal").ap()
    if DEBUG_L0ATTN:
        dbg_cum = nc.dram_tensor(pfx + "dbg_cum", [32, 512], F32, kind="ExternalOutput").ap()
        dbg_dt = nc.dram_tensor(pfx + "dbg_dt", [32, 512], F32, kind="ExternalOutput").ap()
        dbg_xc = nc.dram_tensor(pfx + "dbg_xc", [128, 3 * 512], F32, kind="ExternalOutput").ap()
        dbg_w = nc.dram_tensor(pfx + "dbg_w", [64, 2048], F32, kind="ExternalOutput").ap()
        dbg_y = nc.dram_tensor(pfx + "dbg_y", [128, 2048], F32, kind="ExternalOutput").ap()
        dbg_h = nc.dram_tensor(pfx + "dbg_h", [64, 2048], F32, kind="ExternalOutput").ap()

    with ExitStack() as st:
        sb = lambda n_, s, d: st.enter_context(nc.sbuf_tensor(pfx + n_, s, d))
        xcur = sb("xcur", [128, NA * CW], F32)
        xb = sb("xb", [128, NA * CW], F32)
        xn = sb("xn", [128, NA * CW], BF16)
        zbuf = sb("zbuf", [128, 4 * DI], BF16)
        xbcb = sb("xbcb", [128, NBC * (CW + 3)], BF16)
        xcnv = sb("xcnv", [128, NBC * CW], BF16)
        wbuf = sb("wbuf", [128, 8192], BF16)
        ygf = sb("ygf", [128, NXT * CW], BF16)
        dtb = sb("dtb", [32, CW], F32)
        dt2 = sb("dt2", [32, CW], F32)
        cum = sb("cum", [32, CW], F32)
        e2s = sb("e2s", [32, CW], F32)
        msk = sb("msk", [32, CW], BF16)
        hpar = sb("hpar_sb", [32, 3], F32)
        acol = sb("acol", [32, 1], F32)
        ecst = sb("ecst_sb", [32, DI], F32)
        triu = sb("triu_sb", [64, 64], BF16)
        ecst2 = sb("ecst2_sb", [64, 2048], BF16)
        dt3 = sb("dt3", [32, 64], F32)
        idb = sb("idb_sb", [128, 128], BF16)
        gms = sb("gms", [64, 64], BF16)
        cvb = sb("cvb", [64, 64], BF16)
        xtr = sb("xtr", [64, DI], BF16)
        xtp = sb("xtp", [64, DI], BF16)
        hb16 = sb("hb16", [64, DI], BF16)
        h32 = sb("h32", [64, DI], F32)
        cdx = sb("cdx", [64, DI], F32)
        ypair = sb("ypair", [128, DI], F32)
        ztmp = sb("ztmp", [128, DI], BF16)
        dskb = sb("dskb", [64, DI], BF16)
        rsq = sb("rsq", [128, 1], F32)
        rsq2 = sb("rsq2", [128, 1], F32)
        rsq3 = None
        convw = sb("convw_sb", [128, NBC * 4], F32)
        convb = sb("convb_sb", [128, NBC], F32)
        lnwb = sb("lnwb_sb", [128, 2 * NA], F32)
        normw = sb("normw_sb", [128, NXT], F32)
        sqt = [sb("sqt0", [128, CW], F32)] * 2
        SQK = ["sqt0", "sqt0"]
        rows = sb("rows", [1, 4 * 512], F32)
        rows2 = sb("rows2", [1, 2 * 512], F32)
        rA = lambda i, W: rows[0:1, i * 512:i * 512 + W]
        rB = lambda i, W: rows2[0:1, (0 if i == 0 else 1) * 512:(0 if i == 0 else 1) * 512 + W]
        sc_b = sb("sc_b", [128, 512], F32)
        iv_b = sb("iv_b", [128, 512], F32)
        ones1 = sb("ones1", [1, 128], F32)
        onesc = sb("onesc", [128, 1], F32)

        psb = lambda n_, s, d=F32: st.enter_context(nc.psum_tensor(pfx + n_, s, d))
        ps_y = psb("ps_y", [64, DI])
        ps_st = psb("ps_st", [128, 512])
        ps_bc = psb("ps_bc", [128, 512])
        ps_a = psb("ps_a", [128, 512])
        ps_tr = psb("ps_tr", [128, 128], BF16)

        p.tv("vector", "memset", ones1[:], 1.0, writes=("ones1",))
        p.tv("vector", "memset", onesc[:], 1.0, writes=("onesc",))
        p.dma("sync", hpar[:], hpar_d, writes=("hpar",))
        p.dma("sync", ecst[:], ecst_d, writes=("ecst",))
        p.dma("sync", triu[:], triu_d, writes=("triu",))
        p.dma("sync", ecst2[:], ecst2_d, writes=("ecst2",))
        p.dma("sync", idb[:], idb_d, writes=("idb",))
        p.dma("sync", convw[:], convw_d, writes=("convw",))
        p.dma("sync", convb[:], convb_d, writes=("convb",))
        p.dma("sync", lnwb[:], lnwb_d, writes=("lnwb",))
        p.dma("sync", normw[:], normw_d, writes=("normw",))
        p.dma("sync", xo_d, x_d, writes=("xo_init",))
        p.tv("scalar", "activation", acol[:], hpar[:, 1:2], AF.Exp,
             reads=("hpar",), writes=("acol",))
        p.tv("vector", "tensor_scalar", acol[:], acol[:], -1.0, None, ALU.mult,
             reads=("acol",), writes=("acol",))
        p.tv("vector", "memset", msk[:], 1.0, writes=("msk",))
        for j in range(8):
            p.tv("vector", "memset", msk[:, j * SC:j * SC + 1], 0.0, writes=("msk",))
        for j in range(4):
            p.emit("tensor", lambda e, j=j: e.matmul(
                ps_a[0:1, 0:512], hpar[:, 2:3], ecst[:, j * 512:(j + 1) * 512],
                start=True, stop=True), reads=("hpar", "ecst"), writes=("ps_a",))
            p.tv("vector", "tensor_copy", rA(j, 512), ps_a[0:1, 0:512],
                 reads=("ps_a",), writes=(f"rA{j}",))
        p.dma("sync", scr3_d, rows[0:1, 0:DI], reads=("rA0", "rA1", "rA2", "rA3"),
              writes=("scr3",))
        p.dma("gpsimd", dskb[:], scr3_d.partition_broadcast(64), reads=("scr3",),
              writes=("dskb",))

        lnw = lambda a: lnwb[:, a:a + 1]
        lnb = lambda a: lnwb[:, NA + a:NA + a + 1]

        def stats_seq(src, skeys, W):
            n = len(skeys)
            for a in range(n):
                sq, sk = sqt[a % 2], "sqt0"
                p.tv("scalar", "activation", sq[:, 0:W], src[:, a * W:(a + 1) * W],
                     AF.Square, reads=(skeys[a],), writes=(sk,))
                p.emit("tensor", lambda e, a=a, W=W, src=src, n=n: e.matmul(
                    ps_st[0:1, 0:W], onesc[:, 0:1], src[:, a * W:(a + 1) * W],
                    start=(a == 0), stop=(a == n - 1)),
                    reads=(skeys[a], "onesc"), writes=("ps_st",))
                p.emit("tensor", lambda e, a=a, W=W, sq=sq, n=n: e.matmul(
                    ps_st[32:33, 0:W], onesc[:, 0:1], sq[:, 0:W],
                    start=(a == 0), stop=(a == n - 1)),
                    reads=(sk, "onesc"), writes=("ps_st",))

        def ln_full(src, skeys, dst, dkeys, W, eps, gain=None, bias=None):
            stats_seq(src, skeys, W)
            rstd, mur = ln_rows(p, rA, rB, W, ps_st, DM, eps)
            bcast_row(p, rstd, "rA2", W, ps_bc, sc_b, "sc_b", ones1)
            bcast_row(p, mur, "rA3", W, ps_bc, iv_b, "iv_b", ones1)
            for a in range(NA):
                sq, sk = sqt[a % 2], "sqt0"
                p.tv("vector", "tensor_mul", sq[:, 0:W], src[:, a * W:(a + 1) * W],
                     sc_b[:, 0:W], reads=(skeys[a], "sc_b"), writes=(sk,))
                if gain is None:
                    p.tv("vector", "tensor_sub", dst[:, a * W:(a + 1) * W], sq[:, 0:W],
                         iv_b[:, 0:W], reads=(sk, "iv_b"), writes=(dkeys[a],))
                else:
                    p.tv("vector", "tensor_sub", sq[:, 0:W], sq[:, 0:W], iv_b[:, 0:W],
                         reads=(sk, "iv_b"), writes=(sk,))
                    p.tv("vector", "tensor_scalar", dst[:, a * W:(a + 1) * W], sq[:, 0:W],
                         gain(a), bias(a), ALU.mult, ALU.add,
                         reads=(sk, "lnwb"), writes=(dkeys[a],))

        BV = lambda off: xcnv[:, 16 * CW + off:16 * CW + off + SC][0:64, :]
        CV = lambda off: xcnv[:, 16 * CW + off:16 * CW + off + SC][64:128, :]

        for s in range(NSEQ):
            p.tv("vector", "memset", h32[:], 0.0, writes=("h32",))
            p.tv("vector", "memset", hb16[:], 0.0, writes=("hb16",))
            for a in range(NBC):
                p.tv("vector", "memset", xbcb[:, a * (CW + 3):a * (CW + 3) + 3], 0.0,
                     writes=(f"xl{a}",))
            for c in range(4):
                base = c * CW
                for a in range(NA):
                    p.dma("sync", xcur[:, a * CW:(a + 1) * CW],
                          x_d[s * DM + a * 128:s * DM + (a + 1) * 128, base:base + CW],
                          writes=(f"xcur{a}",))
                ln_full(xcur, [f"xcur{a}" for a in range(NA)], xb,
                        [f"xb{a}" for a in range(NA)], CW, 1e-5)
                ln_full(xb, [f"xb{a}" for a in range(NA)], xn,
                        [f"xn{a}" for a in range(NA)], CW, 1e-5, gain=lnw, bias=lnb)
                xnk = tuple(f"xn{a}" for a in range(NA))
                for half, (o0, nob, wdth) in enumerate(((0, 7, 896), (7, 7, 896), (14, 4, 416))):
                    for a in range(NA):
                        p.dma("gpsimd", wbuf[:, a * wdth:a * wdth + wdth],
                              ipT_d[a * 128:(a + 1) * 128,
                                    2048 + o0 * 128:2048 + o0 * 128 + wdth],
                              writes=("wbuf",))
                    for oi in range(nob):
                        ob = o0 + oi
                        if ob >= NBC:
                            wid = 32
                        else:
                            wid = 128
                        if ob > NBC:
                            continue
                        ps, pk = (ps_a, "ps_a") if oi % 2 == 0 else (ps_bc, "ps_bc")
                        fns = [(lambda e, a=a, oi=oi, ps=ps, wid=wid, wdth=wdth: e.matmul(
                            ps[0:wid, 0:CW], wbuf[:, a * wdth + oi * 128:a * wdth + oi * 128 + wid],
                            xn[:, a * CW:(a + 1) * CW], start=(a == 0), stop=(a == NA - 1)))
                            for a in range(NA)]
                        p.group("tensor", fns, reads=("wbuf",) + xnk, writes=(pk,))
                        if ob < NBC:
                            off = ob * (CW + 3)
                            p.tv("vector", "tensor_copy", xbcb[:, off + 3:off + 3 + CW],
                                 ps[:, 0:CW], reads=(pk,), writes=(f"xc{ob}",))
                        else:
                            p.tv("vector", "tensor_scalar", dtb[:, 0:CW], ps[0:32, 0:CW],
                                 1.0, hpar[0:32, 0:1], ALU.mult, ALU.add,
                                 reads=(pk, "hpar"), writes=("dtb",))
                for half in range(2):
                    for a in range(NA):
                        p.dma("gpsimd", wbuf[:, a * 1024:(a + 1) * 1024],
                              ipT_d[a * 128:(a + 1) * 128, half * 1024:(half + 1) * 1024],
                              writes=("wbuf",))
                    for tb in range(4):
                        for zc in range(2):
                            ps, pk = (ps_a, "ps_a") if zc % 2 == 0 else (ps_bc, "ps_bc")
                            fns = [(lambda e, a=a, tb=tb, zc=zc, ps=ps: e.matmul(
                                ps[:, 0:512],
                                xn[:, a * CW + tb * 128:a * CW + (tb + 1) * 128],
                                wbuf[:, a * 1024 + zc * 512:a * 1024 + (zc + 1) * 512],
                                start=(a == 0), stop=(a == NA - 1)))
                                for a in range(NA)]
                            p.group("tensor", fns, reads=("wbuf",) + xnk, writes=(pk,))
                            p.tv("vector", "tensor_copy",
                                 zbuf[:, tb * DI + half * 1024 + zc * 512:
                                      tb * DI + half * 1024 + (zc + 1) * 512],
                                 ps[:, 0:512], reads=(pk,), writes=("ZB",))
                for a in range(NBC):
                    off = a * (CW + 3)
                    sq, sk = sqt[a % 2], "sqt0"
                    p.tv("vector", "tensor_scalar", sq[:, 0:CW], xbcb[:, off:off + CW],
                         convw[:, 4 * a:4 * a + 1], None, ALU.mult,
                         reads=(f"xc{a}", f"xl{a}", "convw"), writes=(sk,))
                    for j in range(1, 4):
                        p.tv("vector", "scalar_tensor_tensor", sq[:, 0:CW],
                             xbcb[:, off + j:off + j + CW], convw[:, 4 * a + j:4 * a + j + 1],
                             sq[:, 0:CW], ALU.mult, ALU.add,
                             reads=(f"xc{a}", f"xl{a}", "convw", sk), writes=(sk,))
                    p.tv("gpsimd", "tensor_copy", xbcb[:, off:off + 3],
                         xbcb[:, off + CW:off + CW + 3], reads=(f"xc{a}",),
                         writes=(f"xl{a}",))
                    p.emit("scalar", lambda e, a=a, sq=sq: e.activation(
                        xcnv[:, a * CW:(a + 1) * CW], sq[:, 0:CW], AF.Silu,
                        bias=convb[:, a:a + 1]), reads=(sk, "convb"), writes=(f"cv{a}",))
                p.tv("scalar", "activation", dt2[:], dtb[:], AF.Abs, reads=("dtb",), writes=("dt2",))
                p.tv("scalar", "activation", dt2[:], dt2[:], AF.Exp, scale=-1.0,
                     reads=("dt2",), writes=("dt2",))
                p.tv("scalar", "activation", dt2[:], dt2[:], AF.Ln, bias=1.0,
                     reads=("dt2",), writes=("dt2",))
                p.tv("vector", "tensor_scalar", dtb[:], dtb[:], 0.0, None, ALU.max,
                     reads=("dtb",), writes=("dtb",))
                p.tv("vector", "tensor_add", dtb[:], dtb[:], dt2[:],
                     reads=("dtb", "dt2"), writes=("dtb",))
                p.tv("vector", "tensor_scalar", dt2[:], dtb[:], acol[0:32, 0:1], None,
                     ALU.mult, reads=("dtb", "acol"), writes=("dt2",))
                p.tv("vector", "tensor_tensor_scan", cum[:], msk[:], dt2[:], 0.0,
                     ALU.mult, ALU.add, reads=("msk", "dt2"), writes=("cum",))
                if DEBUG_L0ATTN and s == 0 and c == 0:
                    p.dma("sync", dbg_cum, cum[:], reads=("cum",), writes=("dbgcum",))
                    p.dma("sync", dbg_dt, dtb[:], reads=("dtb",), writes=("dbgdt",))
                    p.tv("vector", "tensor_copy", sqt[0][:, 0:CW], xcnv[:, 0:CW],
                         reads=("cv0",), writes=("sqt0",))
                    p.dma("sync", dbg_xc[:, 0:512], sqt[0][:, 0:CW], reads=("sqt0",), writes=("dbgxc",))
                    p.tv("vector", "tensor_copy", sqt[0][:, 0:CW], xcnv[:, CW:2 * CW],
                         reads=("cv1",), writes=("sqt0",))
                    p.dma("sync", dbg_xc[:, 512:1024], sqt[0][:, 0:CW], reads=("sqt0",), writes=("dbgxc",))
                    p.tv("vector", "tensor_copy", sqt[0][:, 0:CW], xcnv[:, 16 * CW:17 * CW],
                         reads=("cv16",), writes=("sqt0",))
                    p.dma("sync", dbg_xc[:, 1024:1536], sqt[0][:, 0:CW], reads=("sqt0",), writes=("dbgxc",))
                p.tv("scalar", "activation", e2s[:], cum[:], AF.Exp,
                     reads=("cum",), writes=("e2s",))
                p.tv("scalar", "activation", dt2[:], dtb[:], AF.Ln,
                     reads=("dtb",), writes=("dt2",))
                p.tv("vector", "tensor_sub", dtb[:], dt2[:], cum[:],
                     reads=("dt2", "cum"), writes=("dtb",))
                for cc in range(8):
                    off = cc * SC
                    for tk in range(NXT):
                        p.tv("tensor", "transpose", ps_tr[0:64, 0:128],
                             xcnv[:, tk * CW + off:tk * CW + off + SC], idb[:],
                             reads=(f"cv{tk}", "idb"), writes=("ps_tr",))
                        p.tv("vector", "tensor_copy", xtr[:, tk * 128:(tk + 1) * 128],
                             ps_tr[0:64, 0:128], reads=("ps_tr",), writes=("xtr",))
                    p.tv("scalar", "activation", dt3[:], dtb[0:32, off:off + SC], AF.Exp,
                         bias=cum[0:32, off + SC - 1:off + SC], reads=("dtb", "cum"),
                         writes=("dt3",))
                    for j in range(4):
                        p.emit("tensor", lambda e, j=j: e.matmul(
                            ps_y[:, j * 512:(j + 1) * 512], dt3[:],
                            ecst[:, j * 512:(j + 1) * 512], start=True, stop=True),
                            reads=("dt3", "ecst"), writes=("ps_y",))
                    p.tv("vector", "tensor_mul", xtp[:], xtr[:], ps_y[0:64, :],
                         reads=("xtr", "ps_y"), writes=("xtp",))
                    p.dma("sync", cvb[:], CV(off), reads=("cv16",), writes=("cvb",))
                    p.emit("tensor", lambda e, off=off: e.matmul(
                        ps_a[0:64, 0:64], cvb[:], BV(off), start=True, stop=True),
                        reads=("cv16", "cvb"), writes=("ps_a",))
                    p.tv("vector", "tensor_tensor", gms[:], ps_a[0:64, 0:64], triu[:],
                         ALU.mult, reads=("ps_a", "triu"), writes=("gms",))
                    for j in range(4):
                        p.emit("tensor", lambda e, j=j, off=off: e.matmul(
                            ps_a[0:64, 0:512], e2s[0:32, off:off + SC],
                            ecst[:, j * 512:(j + 1) * 512], start=True, stop=True),
                            reads=("e2s", "ecst"), writes=("ps_a",))
                        p.tv("vector", "tensor_copy", ztmp[0:64, j * 512:(j + 1) * 512],
                             ps_a[0:64, 0:512], reads=("ps_a",), writes=("ztmp",))
                    for j in range(4):
                        p.emit("tensor", lambda e, j=j: e.matmul(
                            ps_y[:, j * 512:(j + 1) * 512], cvb[:],
                            hb16[:, j * 512:(j + 1) * 512], start=True, stop=True),
                            reads=("cvb", "hb16"), writes=("ps_y",))
                    half = 64 * (cc % 2)
                    p.tv("vector", "tensor_mul", ypair[half:half + 64, :], ps_y[0:64, :],
                         ztmp[0:64, :], reads=("ps_y", "ztmp"), writes=("ypair",))
                    p.dma("sync", scr3_d.rearrange("a (h l) -> (a h) l", l=SC),
                          cum[0:32, off:off + SC], reads=("cum",), writes=("scr3",))
                    p.dma("sync", rows[0:1, 0:DI], scr3_d[0:1, 0:DI],
                          reads=("scr3",), writes=("rA0", "rA1", "rA2", "rA3"))
                    for j in range(4):
                        p.emit("tensor", lambda e, j=j: e.matmul(
                            ps_y[0:64, j * 512:(j + 1) * 512], ones1[0:1, 0:64],
                            rows[0:1, j * 512:(j + 1) * 512], start=True, stop=False),
                            reads=("rA0", "rA1", "rA2", "rA3", "ones1"), writes=("ps_y",))
                        p.emit("tensor", lambda e, j=j, off=off: e.matmul(
                            ps_y[0:64, j * 512:(j + 1) * 512], dtb[0:32, off:off + SC],
                            ecst[:, j * 512:(j + 1) * 512], start=False, stop=True),
                            reads=("dtb", "ecst"), writes=("ps_y",))
                    p.tv("vector", "tensor_scalar", cdx[:], ps_y[0:64, :], 80.0, None,
                         ALU.min, reads=("ps_y",), writes=("cdx",))
                    p.tv("scalar", "activation", ztmp[0:64, :], cdx[:], AF.Exp,
                         reads=("cdx",), writes=("ztmp",))
                    for j in range(4):
                        p.emit("tensor", lambda e, j=j: e.matmul(
                            ps_y[0:64, j * 512:(j + 1) * 512], gms[:],
                            ecst2[:, j * 512:(j + 1) * 512], start=True, stop=True),
                            reads=("gms", "ecst2"), writes=("ps_y",))
                    p.tv("vector", "tensor_mul", ztmp[0:64, :], ztmp[0:64, :],
                         ps_y[0:64, :], reads=("ztmp", "ps_y"), writes=("ztmp",))
                    for h in range(32):
                        p.emit("tensor", lambda e, h=h: e.matmul(
                            ps_y[:, h * SC:(h + 1) * SC], ztmp[0:64, h * SC:(h + 1) * SC],
                            xtr[:, h * SC:(h + 1) * SC], start=True, stop=True),
                            reads=("ztmp", "xtr"), writes=("ps_y",))
                    p.tv("vector", "tensor_add", ypair[half:half + 64, :],
                         ypair[half:half + 64, :], ps_y[0:64, :],
                         reads=("ypair", "ps_y"), writes=("ypair",))
                    p.tv("vector", "tensor_mul", ztmp[half:half + 64, :], xtr[:], dskb[:],
                         reads=("xtr", "dskb"), writes=("ztmp",))
                    p.tv("vector", "tensor_add", ypair[half:half + 64, :],
                         ypair[half:half + 64, :], ztmp[half:half + 64, :],
                         reads=("ypair", "ztmp"), writes=("ypair",))
                    p.tv("tensor", "transpose", ps_tr[0:64, 0:64], BV(off),
                         idb[0:64, 0:64], reads=("cv16", "idb"), writes=("ps_tr",))
                    p.tv("vector", "tensor_copy", gms[:], ps_tr[0:64, 0:64],
                         reads=("ps_tr",), writes=("gms",))
                    for j in range(4):
                        p.emit("tensor", lambda e, j=j: e.matmul(
                            ps_y[:, j * 512:(j + 1) * 512], gms[:],
                            xtp[:, j * 512:(j + 1) * 512], start=True, stop=True),
                            reads=("gms", "xtp"), writes=("ps_y",))
                    for j in range(4):
                        p.emit("tensor", lambda e, j=j, off=off: e.matmul(
                            ps_a[0:1, 0:512], e2s[0:32, off + SC - 1:off + SC],
                            ecst[:, j * 512:(j + 1) * 512], start=True, stop=True),
                            reads=("e2s", "ecst"), writes=("ps_a",))
                        p.tv("vector", "tensor_copy", rA(j, 512), ps_a[0:1, 0:512],
                             reads=("ps_a",), writes=(f"rA{j}",))
                    p.dma("sync", scr3_d, rows[0:1, 0:DI],
                          reads=("rA0", "rA1", "rA2", "rA3"), writes=("scr3",))
                    p.dma("gpsimd", cdx[:], scr3_d.partition_broadcast(64),
                          reads=("scr3",), writes=("cdx",))
                    p.tv("vector", "tensor_mul", h32[:], h32[:], cdx[:],
                         reads=("h32", "cdx"), writes=("h32",))
                    p.tv("vector", "tensor_add", h32[:], h32[:], ps_y[0:64, :],
                         reads=("h32", "ps_y"), writes=("h32",))
                    p.tv("vector", "tensor_copy", hb16[:], h32[:], reads=("h32",),
                         writes=("hb16",))
                    if DEBUG_L0ATTN and s == 0 and c == 0 and cc == 1:
                        p.dma("sync", dbg_y, ypair[:], reads=("ypair",), writes=("dbgy",))
                        p.tv("vector", "tensor_copy", cdx[:], h32[:],
                             reads=("h32",), writes=("cdx",))
                        p.dma("sync", dbg_h, cdx[:], reads=("cdx",), writes=("dbgh",))
                    if cc % 2 == 1:
                        pair = cc // 2
                        zp = zbuf[:, pair * DI:(pair + 1) * DI]
                        p.tv("scalar", "activation", ztmp[:], zp, AF.Silu,
                             reads=("ZB",), writes=("ztmp",))
                        p.tv("vector", "tensor_mul", ypair[:], ypair[:], ztmp[:],
                             reads=("ypair", "ztmp"), writes=("ypair",))
                        p.tv("vector", "scalar_tensor_tensor", ztmp[:], ypair[:], 1.0,
                             ypair[:], ALU.mult, ALU.mult, accum_out=rsq[:],
                             reads=("ypair",), writes=("ztmp", "rsq"))
                        p.tv("vector", "tensor_scalar", rsq2[:], rsq[:], 1.0 / DI, 1e-5,
                             ALU.mult, ALU.add, reads=("rsq",), writes=("rsq2",))
                        p.tv("vector", "reciprocal", rsq[:], rsq2[:], reads=("rsq2",),
                             writes=("rsq",))
                        p.tv("scalar", "activation", rsq[:], rsq[:], AF.Sqrt,
                             reads=("rsq",), writes=("rsq",))
                        nt_ = sc_b[:, 0:1]
                        for _ in range(2):
                            p.tv("vector", "tensor_mul", nt_, rsq[:], rsq[:],
                                 reads=("rsq",), writes=("sc_b",))
                            p.tv("vector", "tensor_mul", nt_, rsq2[:], nt_,
                                 reads=("rsq2", "sc_b"), writes=("sc_b",))
                            p.tv("vector", "tensor_scalar", nt_, nt_, -0.5, 1.5,
                                 ALU.mult, ALU.add, reads=("sc_b",), writes=("sc_b",))
                            p.tv("vector", "tensor_mul", rsq[:], rsq[:], nt_,
                                 reads=("rsq", "sc_b"), writes=("rsq",))
                        p.tv("vector", "tensor_scalar", ztmp[:], ypair[:], rsq[:, 0:1],
                             None, ALU.mult, reads=("ypair", "rsq"), writes=("ztmp",))
                        for tk in range(NXT):
                            p.tv("tensor", "transpose", ps_tr[:, 0:128],
                                 ztmp[:, tk * 128:(tk + 1) * 128], idb[:],
                                 reads=("ztmp", "idb"), writes=("ps_tr",))
                            p.tv("vector", "tensor_copy",
                                 ygf[:, tk * CW + pair * 128:tk * CW + (pair + 1) * 128],
                                 ps_tr[:, 0:128], reads=("ps_tr",), writes=(f"yg{tk}",))
                for tk in range(NXT):
                    p.tv("vector", "tensor_scalar", ygf[:, tk * CW:(tk + 1) * CW],
                         ygf[:, tk * CW:(tk + 1) * CW], normw[:, tk:tk + 1], None,
                         ALU.mult, reads=(f"yg{tk}", "normw"), writes=(f"yg{tk}",))
                for a in range(8):
                    p.dma("gpsimd", wbuf[:, a * 1024:(a + 1) * 1024],
                          opT_d[a * 128:(a + 1) * 128, :], writes=("wbuf",))
                for a in range(8):
                    p.dma("gpsimd", zbuf[:, a * 1024:(a + 1) * 1024],
                          opT_d[(8 + a) * 128:(9 + a) * 128, :], writes=("ZB",))
                for ob in range(NA):
                    ps, pk = (ps_a, "ps_a") if ob % 2 == 0 else (ps_bc, "ps_bc")
                    fns = [(lambda e, k=k, ob=ob, ps=ps: e.matmul(
                        ps[:, 0:CW],
                        (wbuf if k < 8 else zbuf)[:, (k % 8) * 1024 + ob * 128:
                                                  (k % 8) * 1024 + (ob + 1) * 128],
                        ygf[:, k * CW:(k + 1) * CW], start=(k == 0), stop=(k == 15)))
                        for k in range(16)]
                    p.group("tensor", fns,
                            reads=("wbuf", "ZB") + tuple(f"yg{k}" for k in range(16)),
                            writes=(pk,))
                    sq, sk = sqt[ob % 2], "sqt0"
                    p.tv("vector", "tensor_add", sq[:, 0:CW], ps[:, 0:CW],
                         xb[:, ob * CW:(ob + 1) * CW], reads=(pk, f"xb{ob}"), writes=(sk,))
                    p.dma("gpsimd", xo_d[s * DM + ob * 128:s * DM + (ob + 1) * 128,
                                         base:base + CW],
                          sq[:, 0:CW], reads=(sk, "xo_init"),
                          writes=(f"xoF{s}_{c}_{ob}",), accum_op=ALU.add)
    return


def run_l0attn(inputs, x_current, cores=8):
    if "l0attn" not in _cache:
        _cache["l0attn"] = build_l0attn()
    nc = _cache["l0attn"]
    f32 = lambda v: np.asarray(v, dtype=np.float32)
    NBC = 17
    convw = f32(inputs["L0_attn_conv_w"]).reshape(2176, 4).reshape(NBC, 128, 4) \
        .transpose(1, 0, 2).reshape(128, NBC * 4)
    convb = f32(inputs["L0_attn_conv_b"]).reshape(NBC, 128).T
    lnwb = np.concatenate([f32(inputs["L0_attn_ln_w"]).reshape(8, 128).T,
                           f32(inputs["L0_attn_ln_b"]).reshape(8, 128).T], axis=1)
    normw = f32(inputs["L0_attn_norm_w"]).reshape(16, 128).T
    hpar = np.stack([f32(inputs["L0_attn_dt_bias"]), f32(inputs["L0_attn_A_log"]),
                     f32(inputs["L0_attn_D_skip"])], axis=1)
    ecst = np.zeros((32, 2048), dtype=np.float32)
    for h in range(32):
        ecst[h, h * 64:(h + 1) * 64] = 1.0
    triu = np.tril(np.ones((64, 64))).astype(ml_dtypes.bfloat16)
    ecst2 = np.tile(np.eye(64), (1, 32)).astype(ml_dtypes.bfloat16)
    w = {
        "ipT": np.ascontiguousarray(f32(inputs["L0_attn_in_proj_w"]).T),
        "opT": np.ascontiguousarray(f32(inputs["L0_attn_out_proj_w"]).T),
        "convw": np.ascontiguousarray(convw),
        "convb": np.ascontiguousarray(convb),
        "lnwb": np.ascontiguousarray(lnwb),
        "normw": np.ascontiguousarray(normw),
        "hpar": np.ascontiguousarray(hpar),
        "ecst": ecst,
        "triu": triu,
        "ecst2": np.ascontiguousarray(ecst2),
        "identb16": np.eye(128, dtype=ml_dtypes.bfloat16),
    }
    in_maps = [dict(w, x_res=_shard_x(x_current, c)) for c in range(cores)]
    res = run_bass_kernel_spmd(nc, in_maps, core_ids=list(range(cores)))
    return _unshard([r for r in res.results])



def _mk_io(nc):
    x_d = nc.dram_tensor("x_res", [NSEQ * DM, T], F32, kind="ExternalInput").ap()
    xo_d = nc.dram_tensor("x_out", [NSEQ * DM, T], F32, kind="ExternalOutput").ap()
    return x_d, xo_d


def build_l0attn():
    p = Prog("l0attn")
    build_l0attn.last_prog = p
    x_d, xo_d = _mk_io(p.nc)
    emit_l0attn(p, x_d, xo_d, pfx="")
    return p.finalize()


def build_l0ffn():
    p = Prog("l0ffn")
    build_l0ffn.last_prog = p
    x_d, xo_d = _mk_io(p.nc)
    emit_l0ffn(p, x_d, xo_d, pfx="")
    return p.finalize()


def build_l1attn():
    p = Prog("l1attn")
    build_l1attn.last_prog = p
    x_d, xo_d = _mk_io(p.nc)
    emit_l1attn(p, x_d, xo_d, pfx="")
    return p.finalize()


def build_l1ffn():
    p = Prog("l1ffn")
    build_l1ffn.last_prog = p
    x_d, xo_d = _mk_io(p.nc)
    emit_l1ffn(p, x_d, xo_d, pfx="")
    return p.finalize()


def build_fused():
    p = Prog("fused")
    build_fused.last_prog = p
    nc = p.nc
    x_d = nc.dram_tensor("x_res", [NSEQ * DM, T], F32, kind="ExternalInput").ap()
    xo_d = nc.dram_tensor("x_out", [NSEQ * DM, T], F32, kind="ExternalOutput").ap()
    x1_d = nc.dram_tensor("x1", [NSEQ * DM, T], F32, kind="Internal").ap()
    x2_d = nc.dram_tensor("x2", [NSEQ * DM, T], F32, kind="Internal").ap()
    x3_d = nc.dram_tensor("x3", [NSEQ * DM, T], F32, kind="Internal").ap()
    emit_l0attn(p, x_d, x1_d, pfx="a_")
    p.barrier()
    emit_l0ffn(p, x1_d, x2_d, pfx="f_")
    p.barrier()
    emit_l1attn(p, x2_d, x3_d, pfx="r_")
    p.barrier()
    emit_l1ffn(p, x3_d, xo_d, pfx="m_")
    return p.finalize()


def _prep_l0attn(inputs):
    f32 = lambda v: np.asarray(v, dtype=np.float32)
    NBC = 17
    convw = f32(inputs["L0_attn_conv_w"]).reshape(2176, 4).reshape(NBC, 128, 4) \
        .transpose(1, 0, 2).reshape(128, NBC * 4)
    convb = f32(inputs["L0_attn_conv_b"]).reshape(NBC, 128).T
    lnwb = np.concatenate([f32(inputs["L0_attn_ln_w"]).reshape(8, 128).T,
                           f32(inputs["L0_attn_ln_b"]).reshape(8, 128).T], axis=1)
    normw = f32(inputs["L0_attn_norm_w"]).reshape(16, 128).T
    hpar = np.stack([f32(inputs["L0_attn_dt_bias"]), f32(inputs["L0_attn_A_log"]),
                     f32(inputs["L0_attn_D_skip"])], axis=1)
    ecst = np.zeros((32, 2048), dtype=np.float32)
    for h in range(32):
        ecst[h, h * 64:(h + 1) * 64] = 1.0
    return {
        "ipT": np.ascontiguousarray(f32(inputs["L0_attn_in_proj_w"]).T),
        "opT": np.ascontiguousarray(f32(inputs["L0_attn_out_proj_w"]).T),
        "convw": np.ascontiguousarray(convw),
        "convb": np.ascontiguousarray(convb),
        "lnwb": np.ascontiguousarray(lnwb),
        "normw": np.ascontiguousarray(normw),
        "hpar": np.ascontiguousarray(hpar),
        "ecst": ecst,
        "triu": np.tril(np.ones((64, 64))).astype(ml_dtypes.bfloat16),
        "ecst2": np.ascontiguousarray(np.tile(np.eye(64), (1, 32)).astype(ml_dtypes.bfloat16)),
        "identb16": np.eye(128, dtype=ml_dtypes.bfloat16),
    }


def _prep_l0ffn(inputs):
    f32 = lambda v: np.asarray(v, dtype=np.float32)
    return {
        "w1T": np.ascontiguousarray(f32(inputs["L0_ffn_w1"]).T),
        "w2T": np.ascontiguousarray(f32(inputs["L0_ffn_w2"]).T),
        "convw": np.ascontiguousarray(f32(inputs["L0_ffn_conv_w"]).reshape(32, 128, 3)
                                      .transpose(1, 0, 2).reshape(128, 96)),
        "convb": np.ascontiguousarray(f32(inputs["L0_ffn_conv_b"]).reshape(32, 128).T),
        "identf32": np.eye(128, dtype=np.float32),
    }


def _prep_l1attn(inputs):
    f32 = lambda v: np.asarray(v, dtype=np.float32)
    ctile = lambda v: np.ascontiguousarray(f32(v).reshape(8, 128).T)
    par = np.concatenate([
        ctile(inputs["L1_attn_ln1_g"]), ctile(inputs["L1_attn_ln1_b"]),
        ctile(inputs["L1_attn_ln2_g"]), ctile(inputs["L1_attn_ln2_b"]),
        ctile(inputs["L1_attn_tm_mix_k"]), ctile(inputs["L1_attn_tm_mix_v"]),
        ctile(inputs["L1_attn_tm_mix_r"]), ctile(inputs["L1_attn_time_decay"]),
        ctile(inputs["L1_attn_time_first"]), ctile(inputs["L1_attn_cm_mix_k"]),
        ctile(inputs["L1_attn_cm_mix_r"])], axis=1)
    return {
        "wk": np.ascontiguousarray(f32(inputs["L1_attn_tm_Wk"])),
        "wv": np.ascontiguousarray(f32(inputs["L1_attn_tm_Wv"])),
        "wr": np.ascontiguousarray(f32(inputs["L1_attn_tm_Wr"])),
        "wo": np.ascontiguousarray(f32(inputs["L1_attn_Wo"])),
        "cmk": np.ascontiguousarray(f32(inputs["L1_attn_cm_Wk"])),
        "cmr": np.ascontiguousarray(f32(inputs["L1_attn_cm_Wr"])),
        "cmv": np.ascontiguousarray(f32(inputs["L1_attn_cm_Wv"])),
        "par": np.ascontiguousarray(par),
    }


def _prep_l1ffn(inputs):
    f32 = lambda v: np.asarray(v, dtype=np.float32)
    return {
        "attnT": np.ascontiguousarray(f32(inputs["L1_ffn_attn_w"]).T),
        "gateT": np.ascontiguousarray(f32(inputs["L1_ffn_gate_w"]).T.reshape(8, 128, 8)
                                      .transpose(1, 0, 2).reshape(128, 64)),
        "gpT": np.ascontiguousarray(f32(inputs["L1_ffn_gate_proj"]).transpose(0, 2, 1)).reshape(8192, 4096),
        "upT": np.ascontiguousarray(f32(inputs["L1_ffn_up_proj"]).transpose(0, 2, 1)).reshape(8192, 4096),
        "dnT": np.ascontiguousarray(f32(inputs["L1_ffn_down_proj"]).transpose(0, 2, 1)).reshape(32768, 1024),
        "lnp": np.ascontiguousarray(np.stack([
            f32(inputs["L1_ffn_ln1_g"]).reshape(8, 128),
            f32(inputs["L1_ffn_ln1_b"]).reshape(8, 128),
            f32(inputs["L1_ffn_ln2_g"]).reshape(8, 128),
            f32(inputs["L1_ffn_ln2_b"]).reshape(8, 128),
        ], axis=0).reshape(32, 128).T.reshape(128, 32)),
        "identf32": np.eye(128, dtype=np.float32),
    }


def kernel(**inputs):
    x = np.ascontiguousarray(np.asarray(inputs["L0_attn_x"], dtype=np.float32))
    if "fused" not in _cache:
        _cache["fused"] = build_fused()
    nc = _cache["fused"]
    w = {}
    for pfx, prep in (("a_", _prep_l0attn), ("f_", _prep_l0ffn),
                      ("r_", _prep_l1attn), ("m_", _prep_l1ffn)):
        for k, v in prep(inputs).items():
            w[pfx + k] = v
    in_maps = [dict(w, x_res=_shard_x(x, c)) for c in range(8)]
    res = run_bass_kernel_spmd(nc, in_maps, core_ids=list(range(8)))
    return _unshard([r for r in res.results])
